# version 05
# baseline (speedup 1.0000x reference)
import numpy as np
import ml_dtypes
import concourse.bass as bass
import concourse.mybir as mybir
from concourse.bass_utils import run_bass_kernel_spmd
from concourse.alu_op_type import AluOpType
from contextlib import ExitStack

F32 = mybir.dt.float32
BF16 = mybir.dt.bfloat16
AF = mybir.ActivationFunctionType
AX = mybir.AxisListType
OP = AluOpType

B = 8
S = 2048
D = 4096
H = 32
KVH = 8
G = 4
DH = 128
NQT = S // 128
NIC = D // 128
SCALE = DH ** -0.5
BIGM = float(2.0 ** 100)
EPS = 1e-5

ENGINES = ("sync", "gpsimd", "vector", "scalar", "tensor")
DBG_NO = ()


def pmaj(ap):
    return ap.transpose([1, 0, 2])


def sb3(ap, n):
    return ap.rearrange("p (a b) -> p a b", a=n)


def host_constants():
    bf = ml_dtypes.bfloat16
    c = {}
    c["IDB"] = np.eye(128, dtype=bf)
    perm = np.zeros((128, 128), dtype=bf)
    for j in range(64):
        perm[2 * j, 2 * j + 1] = 1
        perm[2 * j + 1, 2 * j] = 1
    c["PERM"] = perm
    j = np.arange(64, dtype=np.float64)
    inv = 10000.0 ** (-2.0 * j / 128.0)
    t = np.arange(S, dtype=np.float64)
    ang = t[None, :] * inv[:, None]
    c2 = np.zeros((128, S), dtype=np.float64)
    s2 = np.zeros((128, S), dtype=np.float64)
    c2[0::2] = np.cos(ang)
    c2[1::2] = np.cos(ang)
    s2[0::2] = -np.sin(ang)
    s2[1::2] = np.sin(ang)
    c["RC"] = c2.astype(bf)
    c["RS"] = s2.astype(bf)
    E = np.zeros((128, S), dtype=bf)
    for cc in range(128):
        E[cc, cc * 16:(cc + 1) * 16] = BIGM
    c["EBIG"] = E
    od = np.zeros((128, 128), dtype=bf)
    for p in range(128):
        od[p, (p // 16) * 16: p + 1] = BIGM
    c["ODC"] = od
    cm = np.zeros((128, NQT, 132), dtype=bf)
    for qt in range(NQT):
        for p in range(128):
            i = qt * 128 + p
            for cc in range(128):
                if i < 16 * (cc + 1):
                    cm[p, qt, cc] = -BIGM
            cm[p, qt, 129:132] = -BIGM
    c["CMASK"] = cm.reshape(128, NQT * 132)
    band = np.zeros((128, 256), dtype=bf)
    for p in range(128):
        for jc in range(256):
            if not (0 <= 128 + p - jc <= 64):
                band[p, jc] = -BIGM
    c["BAND"] = band
    tri = np.zeros((128, 128), dtype=bf)
    for p in range(128):
        tri[p, p + 1:] = -BIGM
    c["TRI"] = tri
    c["ONES"] = np.ones((1, 512), dtype=bf)
    c["ONECOL"] = np.ones((128, 1), dtype=bf)
    return c


NLANES = 8


class Prog:

    def __init__(self, nc, sems):
        self.nc = nc
        self.sems = sems
        self.count = {k: 0 for k in sems}
        self.cur = None
        self.eng = None
        self.wait_cache = {}
        self.engine_sem = {"sync": "sq", "gpsimd": "gq", "vector": "dve",
                           "scalar": "act", "tensor": "pe"}
        self.lane_idx = {"sync": 0, "gpsimd": 0}

    def reset_for(self, engname, engobj):
        self.cur = engname
        self.eng = engobj
        self.count = {k: 0 for k in self.sems}
        self.wait_cache = {}
        self.lane_idx = {"sync": 0, "gpsimd": 0}

    def op(self, engname, fn, deps=(), inc=0):
        semname = self.engine_sem[engname]
        if self.cur == engname:
            for d in deps:
                if d is None:
                    continue
                s, v = d
                if v <= 0:
                    continue
                key = s
                if self.wait_cache.get(key, 0) < v:
                    self.eng.wait_ge(self.sems[s], v)
                    self.wait_cache[key] = v
            inst = fn(self.eng)
            if inc:
                inst.then_inc(self.sems[semname], inc)
        if inc:
            self.count[semname] += inc
            return (semname, self.count[semname])
        return None

    def dma(self, q, out, in_, deps=(), **kw):
        lane = self.lane_idx[q]
        self.lane_idx[q] = (lane + 1) % NLANES
        semname = self.engine_sem[q] + str(lane)
        prev = self.count[semname]
        deps = list(deps) + ([(semname, prev)] if prev else [])
        if self.cur == q:
            for d in deps:
                if d is None:
                    continue
                s, v = d
                if v <= 0:
                    continue
                if self.wait_cache.get(s, 0) < v:
                    self.eng.wait_ge(self.sems[s], v)
                    self.wait_cache[s] = v
            inst = self.eng.dma_start(out=out, in_=in_, **kw)
            inst.then_inc(self.sems[semname], 16)
        self.count[semname] += 16
        return (semname, self.count[semname])

    def mm(self, out, lhsT, rhs, start, stop, deps=(), inc=0):
        return self.op("tensor",
                       lambda e: e.matmul(out=out, lhsT=lhsT, rhs=rhs,
                                          start=start, stop=stop),
                       deps=deps, inc=inc)

    def tr(self, out, in_, iden, deps=(), inc=0):
        k = in_.shape[0]
        b = in_.base_partition()
        if iden.shape[0] != k or b != 0:
            iden = iden[b:b + k, b:b + k]
        return self.op("tensor",
                       lambda e: e.transpose(out=out, in_=in_, identity=iden),
                       deps=deps, inc=inc)

    def act(self, out, in_, func, bias=None, scale=None, accum_out=None,
            deps=(), inc=0, _dep_bf=None):
        deps = list(deps) + ([_dep_bf] if _dep_bf is not None else [])
        kw = {}
        if bias is not None:
            kw["bias"] = bias
        if scale is not None:
            kw["scale"] = scale
        if accum_out is not None:
            kw["accum_out"] = accum_out
        return self.op("scalar",
                       lambda e: e.activation(out=out, in_=in_, func=func, **kw),
                       deps=deps, inc=inc)

    def v(self, fn, deps=(), inc=1):
        return self.op("vector", fn, deps=deps, inc=inc)

    def bar(self):
        return [(s, c) for s, c in self.count.items()]


def build_program(upto=13, taps=(), sbuf_taps=(), debug_no=()):
    global DBG_NO
    DBG_NO = debug_no
    nc = bass.Bass("TRN2", target_bir_lowering=False)
    CONST = host_constants()

    def inp(name, shape, dt=F32):
        return nc.dram_tensor(name, list(shape), dt, kind="ExternalInput").ap()

    x0 = inp("x0", (S, D))
    w_qkv = inp("w_qkv", (6144, D))
    w_o = inp("w_o", (D, D))
    k_pos = inp("k_pos", (KVH, 16, DH))
    v_pos = inp("v_pos", (KVH, 16, DH))
    k_w1 = inp("k_w1", (2048, 2048))
    k_b1 = inp("k_b1", (2048,))
    k_w2 = inp("k_w2", (DH, 2048))
    k_b2 = inp("k_b2", (DH,))
    v_w1 = inp("v_w1", (2048, 2048))
    v_b1 = inp("v_b1", (2048,))
    v_w2 = inp("v_w2", (DH, 2048))
    v_b2 = inp("v_b2", (DH,))
    mem_kv = inp("mem_kv", (2, KVH, 1, DH))
    w_gate = inp("w_gate", (3 * H, D))
    b_gate = inp("b_gate", (3 * H,))
    wq_b = inp("wq_b", (D, D))
    wk_b = inp("wk_b", (D, 1024))
    wv_b = inp("wv_b", (D, 1024))
    wo_b = inp("wo_b", (D, D))
    w_ffn = inp("w_ffn", (D, D))
    consts_in = {k: inp("c_" + k, v.shape,
                        BF16 if v.dtype == ml_dtypes.bfloat16 else F32)
                 for k, v in CONST.items()}

    y = nc.dram_tensor("y", [S, D], F32, kind="ExternalOutput").ap()

    def scr(name, shape, dt=BF16):
        return nc.dram_tensor(name, list(shape), dt, kind="Internal").ap()

    x1 = scr("x1", (S, D), F32)
    x2 = scr("x2", (S, D), F32)
    x3 = scr("x3", (S, D), F32)
    h_scr = scr("h_scr", (NQT, 128, NIC * 128))
    q_scr = scr("q_scr", (H, 128, S))
    v_scr = scr("v_scr", (128, KVH, 16, 128))
    ao_scr = scr("ao_scr", (NQT, 128, 32, 128))
    wqkvT = scr("wqkvT", (48, 128, NIC * 128))
    woT = scr("woT", (4, 32, 128, 1024))
    bwT = scr("bwT", (4, NIC, 128, 1024))
    w1kT = scr("w1kT", (16, 128, 2048))
    w1vT = scr("w1vT", (16, 128, 2048))
    w2kT = scr("w2kT", (128, 2048))
    w2vT = scr("w2vT", (128, 2048))
    qb_q = scr("qb_q", (D, D))
    qb_k = scr("qb_k", (D, 1024))
    qb_v = scr("qb_v", (D, 1024))

    scratch_map = {"x1": x1, "x2": x2, "x3": x3, "h_scr": h_scr, "q_scr": q_scr,
                   "v_scr": v_scr, "ao_scr": ao_scr, "wqkvT": wqkvT, "woT": woT,
                   "bwT": bwT, "w1kT": w1kT, "w2kT": w2kT, "w1vT": w1vT,
                   "w2vT": w2vT}
    tap_outs = {}
    for name in taps:
        src = scratch_map[name]
        tap_outs[name] = nc.dram_tensor("tap_" + name, list(src.tensor.shape),
                                        src.dtype, kind="ExternalOutput").ap()
    sbuf_tap_list = list(sbuf_taps)

    es = ExitStack()
    sb = lambda n, shape, dt=BF16: es.enter_context(nc.sbuf_tensor(n, list(shape), dt))
    psalloc = lambda n, shape, dt=F32: es.enter_context(nc.psum_tensor(n, list(shape), dt))

    C_IDB = sb("c_idb", (128, 128))
    C_PERM = sb("c_perm", (128, 128))
    C_RC = sb("c_rc", (128, S))
    C_RS = sb("c_rs", (128, S))
    C_E = sb("c_e", (128, S))
    C_ODC = sb("c_odc", (128, 128))
    C_CM = sb("c_cm", (128, NQT * 132))
    C_BAND = sb("c_band", (128, 256))
    C_TRI = sb("c_tri", (128, 128))
    C_ONES = sb("c_ones", (1, 512))
    C_ONECOL = sb("c_onecol", (128, 1))
    WGT = sb("wgt", (128, 32 * 96))
    BGATE = sb("bgate", (1, 96))
    G_TM = sb("g_tm", (128, NQT * 96))
    CK = sb("ck", (128, KVH, 132))
    CVF = sb("cvf", (128, KVH, 128))
    CVK = sb("cvk", (128, KVH, 128))
    MEMV = sb("memv", (1, KVH, 128))
    KPF = sb("kpf", (128, 16 * 8))
    VPF = sb("vpf", (128, 16 * 8))
    B1K = sb("b1k", (128, 16), F32)
    B1V = sb("b1v", (128, 16), F32)
    B2K = sb("b2k", (128, 1), F32)
    B2V = sb("b2v", (128, 1), F32)
    B2KR = sb("b2kr", (1, 128))
    B2VR = sb("b2vr", (1, 128))
    SMALL = sb("small", (128, 1024), F32)
    CONSTV = sb("constv", (128, 4), F32)
    STG = sb("stg", (128, 4096))
    STG2 = sb("stg2", (128, 4096))
    PHAT = sb("phat", (128, 4096))
    PHC = sb("phc", (128, 4 * 132))
    PHS = sb("phs", (128, 512))
    SC4 = sb("sc4", (128, 4 * 132), F32)
    FWORK = sb("fwork", (128, 4096), F32)
    BIGA = sb("biga", (128, 53248))

    RK = BIGA[:, 0:16384]
    VKM = BIGA[:, 16384:32768]
    WSL = BIGA[:, 0:32768]
    PH = BIGA[:, 32768:53248]

    PA = psalloc("PA", (128, 2048))
    PB = psalloc("PB", (128, 512))
    PC = psalloc("PC", (128, 512))
    PT = psalloc("PT", (128, 2048), BF16)

    semnames = ["dve", "act", "pe"]
    for q in ("sq", "gq"):
        semnames += [q + str(i) for i in range(NLANES)]
    sems = {k: es.enter_context(nc.semaphore(k)) for k in semnames}
    P = Prog(nc, sems)

    sbuf_ap_map = {"RK": RK, "VKM": VKM, "CK": CK[:].rearrange("p h c -> p (h c)"),
                   "SMALL": SMALL[:], "FWORK": FWORK[:], "STG2": STG2[:],
                   "PHAT": PHAT[:], "PHC": PHC[:], "SC4": SC4[:], "PHS": PHS[:],
                   "CVF": CVF[:].rearrange("p h c -> p (h c)"),
                   "CVK": CVK[:].rearrange("p h c -> p (h c)"),
                   "G_TM": G_TM[:], "KPF": KPF[:], "B1K": B1K[:],
                   "WGT": WGT[:], "MEMV": MEMV[:].rearrange("p h c -> p (h c)")}
    sbuf_tap_outs = {}
    for name in sbuf_tap_list:
        ap = sbuf_ap_map[name]
        sbuf_tap_outs[name] = nc.dram_tensor(
            "stap_" + name, [ap.shape[0], ap.shape[1]], ap.dtype,
            kind="ExternalOutput").ap()

    def program():
        tok = None
        for nm, tile in [("IDB", C_IDB), ("PERM", C_PERM), ("RC", C_RC),
                         ("RS", C_RS), ("EBIG", C_E), ("ODC", C_ODC),
                         ("CMASK", C_CM), ("BAND", C_BAND), ("TRI", C_TRI),
                         ("ONES", C_ONES), ("ONECOL", C_ONECOL)]:
            tok = P.dma("sync", tile[:], consts_in[nm], deps=[tok])
        P.v(lambda e: e.memset(CONSTV[:, 0:1], float(EPS)))
        P.v(lambda e: e.memset(CONSTV[:, 1:2], -1000.0))
        P.v(lambda e: e.memset(CK[:].rearrange("p h c -> p (h c)"), 0.0))
        P.v(lambda e: e.memset(SMALL[:], 0.0))
        P.v(lambda e: e.memset(FWORK[:], 0.0))
        P.v(lambda e: e.memset(STG2[:], 0.0))
        P.v(lambda e: e.memset(PHAT[:], 0.0))
        P.v(lambda e: e.memset(PHC[:].rearrange("p c -> p c"), 0.0))
        P.v(lambda e: e.memset(SC4[:], 0.0))
        t_consts = P.bar()

        def scoped(name, fn, *a, **k):
            with nc.named_scope(name):
                return fn(*a, **k)
        if upto >= 1:
            scoped("P1_prep", phase_weight_prep, t_consts)
        if upto >= 2:
            scoped("P2_ln1", phase_ln, x0)
        if upto >= 3:
            scoped("P3_qkv_a", phase_qkv, is_a=True)
        if upto >= 4:
            scoped("P4_attn_a", phase_attn, is_a=True)
        if upto >= 5:
            scoped("P5_oproj_a", phase_oproj, w_kind="woT", x_in=x0, x_out=x1)
        if upto >= 6:
            scoped("P6_ln2", phase_ln, x1)
        if upto >= 7:
            scoped("P7_ffn1", phase_ffn, x_in=x1, x_out=x2)
        if upto >= 8:
            scoped("P8_ln3", phase_ln, x2)
        if upto >= 9:
            scoped("P9_qkv_b", phase_qkv, is_a=False)
        if upto >= 10:
            scoped("P10_attn_b", phase_attn, is_a=False)
        if upto >= 11:
            scoped("P11_oproj_b", phase_oproj, w_kind="nat", x_in=x2, x_out=x3)
        if upto >= 12:
            scoped("P12_ln4", phase_ln, x3)
        if upto >= 13:
            scoped("P13_ffn2", phase_ffn, x_in=x3, x_out=y)

        bb = P.bar()
        tok = None
        for name in sbuf_tap_list:
            ap = sbuf_ap_map[name]
            out_ap = sbuf_tap_outs[name]
            tok = P.dma("sync", out_ap, ap, deps=bb + [tok])
        for name, out_ap in tap_outs.items():
            src = scratch_map[name]
            flat_src = src.rearrange(
                " ".join("d%d" % i for i in range(len(src.tensor.shape))) +
                " -> (" + " ".join("d%d" % i for i in range(len(src.tensor.shape))) + ")")
            flat_dst = out_ap.rearrange(
                " ".join("d%d" % i for i in range(len(src.tensor.shape))) +
                " -> (" + " ".join("d%d" % i for i in range(len(src.tensor.shape))) + ")")
            tok = P.dma("sync", flat_dst, flat_src, deps=bb + [tok])
        P.op("sync", lambda e: e.nop(), deps=bb + ([tok] if tok else []))

    def phase_weight_prep(t_consts):
        IN = FWORK
        INB = STG
        OUT = STG2
        FW2 = PH.bitcast(F32)
        state = {"dma_out": [None, None], "pt_ev": [None, None], "in": None,
                 "cast": None}

        def do_chunk(src_rows, n_i, out_ap, sign=False):
            prev_out = state["dma_out"]
            if not sign:
                t_cast = P.dma("gpsimd", INB[:, 0:n_i * 128], src_rows,
                               deps=[state["in"], state["cast"]] + t_consts +
                                    prev_out)
                state["in"] = t_cast
                state["cast"] = t_cast
            else:
                t_in = P.dma("sync", IN[:, 0:n_i * 128], src_rows,
                             deps=[state["in"], state["cast"]] + t_consts)
                state["in"] = t_in
            if sign:
                t_abs = P.act(FW2[:, 0:n_i * 128], IN[:, 0:n_i * 128], AF.Abs,
                              accum_out=SMALL[:, 0:1],
                              deps=[t_in, state.get("fw2_read")], inc=1)
                t_sg = P.act(FW2[:, 0:n_i * 128], IN[:, 0:n_i * 128], AF.Sign,
                             deps=[t_abs], inc=1)
                t_cast = P.v(lambda e: e.tensor_scalar(
                    out=INB[:, 0:n_i * 128], in0=FW2[:, 0:n_i * 128],
                    scalar1=SMALL[:, 0:1], scalar2=1.0 / (n_i * 128),
                    op0=OP.mult, op1=OP.mult), deps=[t_sg, prev_out[0], prev_out[1]])
                state["cast"] = t_sg
                state["fw2_read"] = t_cast
            n_done = 0
            gi = 0
            last_ev = None
            while n_done < n_i:
                grp = min(8, n_i - n_done)
                half = gi % 2
                pt = PT[:, half * 1024:half * 1024 + 1024]
                t_pe = None
                for j in range(grp):
                    blk = n_done + j
                    t_pe = P.tr(pt[:, j * 128:(j + 1) * 128],
                                INB[:, blk * 128:(blk + 1) * 128], C_IDB[:],
                                deps=([t_cast, state["pt_ev"][half]] if j == 0 else []),
                                inc=(1 if j == grp - 1 else 0))
                t_ev = P.v(lambda e, pt=pt, n_done=n_done, grp=grp: e.tensor_copy(
                    out=OUT[:, n_done * 128:(n_done + grp) * 128],
                    in_=pt[:, 0:grp * 128]), deps=[t_pe] + prev_out)
                state["pt_ev"][half] = t_ev
                last_ev = t_ev
                n_done += grp
                gi += 1
            src_out = (OUT[:, 0:n_i * 128] if out_ap.ndim == 2
                       else sb3(OUT[:, 0:n_i * 128], out_ap.shape[1]))
            t_out = P.dma("gpsimd", out_ap, src_out, deps=[last_ev])
            state["dma_out"] = [t_out, state["dma_out"][0]]
            return t_out

        for oc in range(48):
            do_chunk(w_qkv[oc * 128:(oc + 1) * 128, :], NIC, wqkvT[oc])
        for oc in range(32):
            os_, ocl = oc // 8, oc % 8
            do_chunk(w_o[oc * 128:(oc + 1) * 128, :], 32,
                     pmaj(woT[os_, :, :, ocl * 128:(ocl + 1) * 128]))
        for oc in range(32):
            os_, ocl = oc // 8, oc % 8
            do_chunk(w_ffn[oc * 128:(oc + 1) * 128, :], 32,
                     pmaj(bwT[os_, :, :, ocl * 128:(ocl + 1) * 128]), sign=True)
        for oc in range(16):
            do_chunk(k_w1[oc * 128:(oc + 1) * 128, :], 16, w1kT[oc])
        for oc in range(16):
            do_chunk(v_w1[oc * 128:(oc + 1) * 128, :], 16, w1vT[oc])
        do_chunk(k_w2[:, :], 16, w2kT)
        do_chunk(v_w2[:, :], 16, w2vT)
        P.dma("gpsimd", qb_q, wq_b, deps=t_consts)
        P.dma("gpsimd", qb_k, wk_b, deps=t_consts)
        P.dma("gpsimd", qb_v, wv_b, deps=t_consts)

        bb = P.bar()
        tk = state["dma_out"][0]
        tkc = state["cast"]
        tk = P.dma("sync", IN[0:96, 0:4096], w_gate[:, :], deps=bb + [tk, tkc])
        tk = P.v(lambda e: e.tensor_copy(out=INB[0:96, 0:4096],
                                         in_=IN[0:96, 0:4096]), deps=bb + [tk])
        for ic in range(NIC):
            half = ic % 2
            pt = PT[:, half * 1024:half * 1024 + 96]
            tk = P.tr(pt, INB[0:96, ic * 128:(ic + 1) * 128], C_IDB[:],
                      deps=[tk], inc=1)
            tk = P.v(lambda e, pt=pt, ic=ic: e.tensor_copy(
                out=WGT[:, ic * 96:(ic + 1) * 96], in_=pt), deps=[tk])
        tk = P.dma("gpsimd", BGATE[0:1, :], b_gate[None, :], deps=[tk])
        for (posin, dst) in ((k_pos, KPF), (v_pos, VPF)):
            tk = P.dma("sync", IN[0:8, 0:2048],
                       posin.rearrange("h tb d -> h (tb d)"), deps=[tk])
            tk = P.v(lambda e: e.tensor_copy(out=INB[0:8, 0:2048],
                                             in_=IN[0:8, 0:2048]), deps=[tk])
            for tb in range(16):
                half = tb % 2
                pt = PT[:, half * 1024 + 512:half * 1024 + 520]
                tk = P.tr(pt, INB[0:8, tb * 128:(tb + 1) * 128],
                          C_IDB[:], deps=[tk], inc=1)
                tk = P.v(lambda e, pt=pt, dst=dst, tb=tb: e.tensor_copy(
                    out=dst[:, tb * 8:(tb + 1) * 8], in_=pt), deps=[tk])
        tk = P.dma("sync", IN[0:16, 0:128], k_b1.rearrange("(oc p) -> oc p", p=128),
                   deps=[tk])
        tk = P.dma("sync", IN[32:48, 0:128], v_b1.rearrange("(oc p) -> oc p", p=128),
                   deps=[tk])
        tk = P.v(lambda e: e.tensor_copy(out=INB[0:48, 0:128], in_=IN[0:48, 0:128]),
                 deps=[tk])
        tk = P.tr(PT[:, 0:16], INB[0:16, 0:128], C_IDB[:], deps=[tk], inc=1)
        tk = P.v(lambda e: e.tensor_copy(out=B1K[:, :], in_=PT[:, 0:16]),
                 deps=[tk])
        tk = P.tr(PT[:, 1024:1040], INB[32:48, 0:128], C_IDB[:], deps=[tk], inc=1)
        tk = P.v(lambda e: e.tensor_copy(out=B1V[:, :], in_=PT[:, 1024:1040]),
                 deps=[tk])
        tk = P.dma("sync", B2K[:, 0:1], k_b2[:, None], deps=[tk])
        tk = P.dma("sync", B2V[:, 0:1], v_b2[:, None], deps=[tk])
        tk = P.dma("sync", IN[0:1, 128:256], k_b2[None, :], deps=[tk])
        tk = P.dma("sync", IN[0:1, 256:384], v_b2[None, :], deps=[tk])
        tk = P.v(lambda e: e.tensor_copy(out=B2KR[0:1, :], in_=IN[0:1, 128:256]),
                 deps=[tk])
        tk = P.v(lambda e: e.tensor_copy(out=B2VR[0:1, :], in_=IN[0:1, 256:384]),
                 deps=[tk])
        tk = P.dma("sync", IN[:, 512:512 + KVH],
                   mem_kv[0].transpose([2, 1, 0]).squeeze(1), deps=[tk],
                   allow_slow_non_contiguous=True)
        tk = P.dma("sync", IN[0:1, 1024:1024 + KVH * 128],
                   mem_kv[1].transpose([1, 0, 2]).rearrange(
                       "one h d -> one (h d)"), deps=[tk])
        tk = P.v(lambda e: e.tensor_copy(
            out=CK[:, :, 128:129].squeeze(2),
            in_=IN[:, 512:512 + KVH]), deps=[tk])
        tk = P.v(lambda e: e.tensor_copy(
            out=MEMV[0:1, :, :].rearrange("one h d -> one (h d)"),
            in_=IN[0:1, 1024:1024 + KVH * 128]), deps=[tk])

    def phase_ln(x_src):
        bb = P.bar()
        XT = [BIGA[:, 0:8192].bitcast(F32), BIGA[:, 8192:16384].bitcast(F32)]
        APL = [BIGA[:, 16384:20480], BIGA[:, 20480:24576]]
        STA = [BIGA[:, 24576:28672], BIGA[:, 28672:32768]]
        TRASH = PH.bitcast(F32)[:, 0:4096]
        tok = {}
        prev_ev = [None, None]
        for qt in range(NQT):
            bf = qt % 2
            c0, c1, c2, c3, c4, c5, c6 = [8 * bf + j for j in range(7)]
            t_in = P.dma("sync", XT[bf], x_src[qt * 128:(qt + 1) * 128, :],
                         deps=bb + [tok.get(("apl", qt - 2))])
            t_s1 = P.act(TRASH, XT[bf], AF.Identity, accum_out=SMALL[:, c0:c0 + 1],
                         deps=[t_in, tok.get(("sm", qt - 2))] + bb, inc=1)
            t_s2 = P.act(TRASH, XT[bf], AF.Square, accum_out=SMALL[:, c1:c1 + 1],
                         deps=[t_s1], inc=1)
            t_v1 = P.v(lambda e, c0=c0, c2=c2: e.tensor_scalar(
                out=SMALL[:, c2:c2 + 1], in0=SMALL[:, c0:c0 + 1], scalar1=-1.0 / D,
                scalar2=None, op0=OP.mult), deps=[t_s2] + bb)
            t_v2 = P.v(lambda e, c2=c2, c3=c3: e.tensor_tensor(
                out=SMALL[:, c3:c3 + 1], in0=SMALL[:, c2:c2 + 1],
                in1=SMALL[:, c2:c2 + 1], op=OP.mult), deps=[t_v1])
            t_v3 = P.v(lambda e, c1=c1, c3=c3, c4=c4: e.scalar_tensor_tensor(
                out=SMALL[:, c4:c4 + 1], in0=SMALL[:, c1:c1 + 1], scalar=1.0 / D,
                in1=SMALL[:, c3:c3 + 1], op0=OP.mult, op1=OP.subtract),
                deps=[t_v2])
            t_sq = P.act(SMALL[:, c5:c5 + 1], SMALL[:, c4:c4 + 1], AF.Sqrt,
                         bias=CONSTV[:, 0:1], deps=[t_v3], inc=1)
            t_rs = P.v(lambda e, c5=c5, c6=c6: e.reciprocal(
                out=SMALL[:, c6:c6 + 1], in_=SMALL[:, c5:c5 + 1]), deps=[t_sq])
            tok[("sm", qt)] = t_rs
            t_h = P.v(lambda e, bf=bf, c2=c2, c6=c6: e.tensor_scalar(
                out=APL[bf], in0=XT[bf], scalar1=SMALL[:, c2:c2 + 1],
                scalar2=SMALL[:, c6:c6 + 1], op0=OP.add, op1=OP.mult),
                deps=[t_rs, tok.get(("tr", qt - 2))])
            tok[("apl", qt)] = t_h
            t_pe = None
            ev_toks = []
            for grp in range(4):
                half = grp % 2
                pt = PT[:, half * 1024:half * 1024 + 1024]
                for j in range(8):
                    ic = grp * 8 + j
                    t_pe = P.tr(pt[:, j * 128:(j + 1) * 128],
                                APL[bf][:, ic * 128:(ic + 1) * 128], C_IDB[:],
                                deps=([t_h, prev_ev[half]] if j == 0 else []),
                                inc=(1 if j == 7 else 0))
                t_ev = P.v(lambda e, pt=pt, grp=grp, bf=bf: e.tensor_copy(
                    out=STA[bf][:, grp * 1024:(grp + 1) * 1024], in_=pt[:, :]),
                    deps=[t_pe, tok.get(("st", qt - 2))])
                prev_ev[half] = t_ev
                ev_toks.append(t_ev)
            tok[("tr", qt)] = t_pe
            t_st = P.dma("gpsimd", h_scr[qt], STA[bf], deps=[ev_toks[-1]])
            tok[("st", qt)] = t_st

    def phase_qkv(is_a):
        bb = P.bar()
        H_Q = BIGA[:, 16384:32768]
        WST = [PH[:, 0:4096], PH[:, 4096:8192]]
        KQ = PH[:, 8192:12288]
        VQ = PH[:, 12288:16384]
        t_h = None
        t_wst = [None, None]
        t_kv_done = bb
        ev_tok = {}
        pa_free = [None, None]
        for tq in range(4):
            t_h = P.dma("sync", sb3(H_Q[:, :], 4),
                        h_scr[tq * 4:(tq + 1) * 4].transpose([1, 0, 2]),
                        deps=t_kv_done + [t_h])
            H_Qv = H_Q.rearrange("p (qtl ic t) -> p qtl ic t", qtl=4, ic=NIC)
            for oc in range(48):
                buf = oc % 2
                if is_a:
                    t_wst[buf] = P.dma("sync", WST[buf][:, :], wqkvT[oc],
                                       deps=[ev_tok.get(oc - 2), t_wst[buf]])
                else:
                    qsrcw, qoc = ((qb_q, oc) if oc < 32 else
                                  (qb_k, oc - 32) if oc < 40 else (qb_v, oc - 40))
                    t_wst[buf] = P.dma(
                        "sync", sb3(WST[buf][:, :], NIC),
                        qsrcw.rearrange("(ic p) o -> p ic o", p=128)[
                            :, :, qoc * 128:(qoc + 1) * 128],
                        deps=[ev_tok.get(oc - 2), t_wst[buf]])
                psl = PA[:, buf * 512:buf * 512 + 512]
                t_pe = None
                for ic in range(NIC):
                    t_pe = P.mm(psl, WST[buf][:, ic * 128:(ic + 1) * 128],
                                H_Qv[:, :, ic, :],
                                start=(ic == 0), stop=(ic == NIC - 1),
                                deps=([t_wst[buf], t_h, pa_free[buf]]
                                      if ic == 0 else []),
                                inc=(1 if ic == NIC - 1 else 0))
                if oc < 32:
                    qs = STG[:, buf * 512:buf * 512 + 512]
                    t_ev = P.v(lambda e, psl=psl, qs=qs: e.tensor_scalar(
                        out=qs, in0=psl, scalar1=float(SCALE),
                        scalar2=None, op0=OP.mult),
                        deps=[t_pe, ev_tok.get(("st", oc - 2))])
                    t_st = P.dma("gpsimd", q_scr[oc, :, tq * 512:(tq + 1) * 512],
                                 qs, deps=[t_ev])
                    ev_tok[("st", oc)] = t_st
                elif oc < 40:
                    kvh = oc - 32
                    dst = (KQ[:, kvh * 512:(kvh + 1) * 512] if is_a
                           else RK[:, kvh * 2048 + tq * 512:kvh * 2048 + tq * 512 + 512])
                    t_ev = P.v(lambda e, psl=psl, dst=dst: e.tensor_copy(
                        out=dst, in_=psl), deps=[t_pe])
                else:
                    kvh = oc - 40
                    t_ev = P.v(lambda e, psl=psl, kvh=kvh: e.tensor_copy(
                        out=VQ[:, kvh * 512:(kvh + 1) * 512], in_=psl),
                        deps=[t_pe, ev_tok.get(("vt", kvh, tq))])
                ev_tok[oc] = t_ev
                pa_free[buf] = t_ev
            prev_pt = [ev_tok.get(("gsg", tq - 1)), ev_tok.get(("gsg", tq - 1))]
            for kvh in range(KVH):
                t_evs = []
                for j in range(4):
                    half = j % 2
                    pt = PT[:, half * 1024:half * 1024 + 128]
                    t_pe = P.tr(pt, VQ[:, kvh * 512 + j * 128:kvh * 512 + (j + 1) * 128],
                                C_IDB[:], deps=[ev_tok[40 + kvh], prev_pt[half]],
                                inc=1)
                    t_ev = P.v(lambda e, pt=pt, j=j: e.tensor_copy(
                        out=STG2[:, j * 128:(j + 1) * 128], in_=pt),
                        deps=[t_pe, ev_tok.get(("vs_last",))])
                    prev_pt[half] = t_ev
                    t_evs.append(t_ev)
                t_st = P.dma("gpsimd",
                             v_scr[:, kvh, tq * 4:(tq + 1) * 4, :],
                             sb3(STG2[:, 0:512], 4), deps=[t_evs[-1]])
                ev_tok[("vs_last",)] = t_st
                ev_tok[("vt", kvh, tq)] = t_evs[-1]
            if is_a and 'gates' not in debug_no:
                t_pe = None
                for ic in range(NIC):
                    t_pe = P.mm(PB[0:96, 0:512], WGT[:, ic * 96:(ic + 1) * 96],
                                H_Qv[:, :, ic, :],
                                start=(ic == 0), stop=False,
                                deps=([t_h, ev_tok.get(("gev", tq - 1))]
                                      if ic == 0 else []))
                t_pe = P.mm(PB[0:96, 0:512], BGATE[0:1, 0:96], C_ONES[0:1, 0:512],
                            start=False, stop=True, deps=[], inc=1)
                t_gev = P.v(lambda e: e.tensor_copy(out=STG[0:96, 1024:1536],
                                                    in_=PB[0:96, 0:512]),
                            deps=[t_pe, ev_tok.get(("gsg", tq - 1))])
                ev_tok[("gev", tq)] = t_gev
                t_sg = None
                for j in range(4):
                    half = j % 2
                    pt = PT[:, half * 1024 + 256:half * 1024 + 256 + 96]
                    t_pe = P.tr(pt, STG[0:96, 1024 + j * 128:1024 + (j + 1) * 128],
                                C_IDB[:], deps=[t_gev, ev_tok.get(("gsg2", j - 2, tq))],
                                inc=1)
                    qt = tq * 4 + j
                    t_sg = P.act(G_TM[:, qt * 96:(qt + 1) * 96], pt, AF.Sigmoid,
                                 deps=[t_pe], inc=1)
                    ev_tok[("gsg2", j, tq)] = t_sg
                ev_tok[("gsg", tq)] = t_sg
            if is_a and 'kvprep' not in debug_no:
                t_kv_done = kvprep_quarter(tq, KQ, VQ, ev_tok)
            else:
                t_kv_done = P.bar()
        return P.bar()

    def kvprep_quarter(tq, KQ, VQ, ev_tok):
        kv_ready = [ev_tok[32 + j] for j in range(16)]
        entry_pe = ("pe", P.count["pe"])
        RL = BIGA[:, 16384:20480]
        WB = [BIGA[:, 20480:22528], BIGA[:, 22528:24576]]
        W2B = BIGA[:, 24576:26624]
        t_rope = []
        for kvh in (range(KVH) if "rope" not in DBG_NO else []):
            ksl = KQ[:, kvh * 512:(kvh + 1) * 512]
            t_pe = P.mm(PA[:, 1024:1536], C_PERM[:], ksl, start=True, stop=True,
                        deps=kv_ready + [ev_tok.get(("rp", kvh - 1))], inc=1)
            t_a = P.v(lambda e, ksl=ksl: e.tensor_tensor(
                out=STG[:, 1536:2048], in0=ksl,
                in1=C_RC[:, tq * 512:(tq + 1) * 512], op=OP.mult),
                deps=kv_ready + [ev_tok.get(("rp", kvh - 1))])
            t_b = P.v(lambda e: e.scalar_tensor_tensor(
                out=FWORK[:, 0:512], in0=PA[:, 1024:1536], scalar=1.0,
                in1=C_RS[:, tq * 512:(tq + 1) * 512], op0=OP.mult, op1=OP.mult),
                deps=[t_pe, t_a])
            t_rk = P.v(lambda e, kvh=kvh: e.tensor_tensor(
                out=RK[:, kvh * 2048 + tq * 512:kvh * 2048 + tq * 512 + 512],
                in0=STG[:, 1536:2048], in1=FWORK[:, 0:512], op=OP.add),
                deps=[t_b])
            ev_tok[("rp", kvh)] = t_rk
            t_rope.append(t_rk)
        t_last = t_rope[-1] if t_rope else P.bar()[0] and None
        mlp_list = [] if "mlp" in DBG_NO else [
                (KQ, w1kT, w2kT, B1K, B2K, B2KR, KPF, CK),
                (VQ, w1vT, w2vT, B1V, B2V, B2VR, VPF, CVF)]
        for (SRC, W1T, W2T, B1, B2c, B2r, PF, CDST) in mlp_list:
            t_w = [None, None]
            t_rl = None
            for oc in range(16):
                buf = oc % 2
                t_w[buf] = P.dma("sync", WB[buf][:, :], W1T[oc],
                                 deps=[t_w[buf], ev_tok.get(("rlw", oc - 2)),
                                       ev_tok.get(("rlw_carry",)) if oc < 2 else None,
                                       entry_pe if oc < 2 else None])
                rhs = SRC.rearrange("p (h t) -> p h t", h=KVH)[:, :, 0::16]
                t_pe = None
                for ic in range(16):
                    rhs_ic = SRC.rearrange("p (h t) -> p h t", h=KVH)[:, :, ic::16]
                    t_pe = P.mm(PA[:, 0:256], WB[buf][:, ic * 128:(ic + 1) * 128],
                                rhs_ic, start=(ic == 0), stop=False,
                                deps=([t_w[buf], t_last,
                                       ev_tok.get(("rlv", oc - 1))] if ic == 0 else []))
                    t_pe = P.mm(PA[:, 256:264], WB[buf][:, ic * 128:(ic + 1) * 128],
                                PF[:, ic * 8:(ic + 1) * 8],
                                start=False, stop=(ic == 15),
                                deps=[], inc=(1 if ic == 15 else 0))
                t_bv = P.v(lambda e, oc=oc, B1=B1: e.tensor_scalar(
                    out=SMALL[:, 8:16], in0=PA[:, 256:264],
                    scalar1=B1[:, oc:oc + 1], scalar2=None, op0=OP.add),
                    deps=[t_pe])
                for h8 in range(KVH):
                    t_rl = P.v(lambda e, oc=oc, h8=h8: e.tensor_scalar(
                        out=RL[:, oc * 256 + h8 * 32:oc * 256 + (h8 + 1) * 32],
                        in0=PA[:, h8 * 32:(h8 + 1) * 32],
                        scalar1=SMALL[:, 8 + h8:9 + h8], scalar2=0.0,
                        op0=OP.add, op1=OP.max), deps=[t_bv])
                ev_tok[("rlv", oc)] = t_rl
                ev_tok[("rlw", oc)] = t_rl
                if oc >= 14:
                    ev_tok[("rlw_carry",)] = t_rl
            t_w2 = P.dma("sync", W2B[:, :], W2T,
                         deps=[ev_tok.get(("w2last",)), entry_pe])
            t_pe = None
            for oc in range(16):
                t_pe = P.mm(PA[:, 512:768], W2B[:, oc * 128:(oc + 1) * 128],
                            RL[:, oc * 256:(oc + 1) * 256],
                            start=(oc == 0), stop=(oc == 15),
                            deps=([t_w2, t_rl, ev_tok.get(("cklast",))]
                                  if oc == 0 else []),
                            inc=(1 if oc == 15 else 0))
            t_ck = P.v(lambda e, CDST=CDST, B2c=B2c: e.tensor_scalar(
                out=CDST[:, :, tq * 32:(tq + 1) * 32],
                in0=PA[:, 512:768].rearrange("p (h c) -> p h c", h=KVH),
                scalar1=B2c[:, 0:1], scalar2=None,
                op0=OP.add), deps=[t_pe])
            ev_tok[("w2last",)] = t_ck
            ev_tok[("cklast",)] = t_ck
            t_last = t_ck
        return P.bar()

    def phase_attn(is_a):
        bb = P.bar()
        Q4 = PH[:, 0:8192]
        RQ4 = PH[:, 8192:16384]
        TS = [STG[:, 0:1024], STG[:, 1024:2048]]
        AOST = [STG[:, 2048:2560], STG[:, 2560:3072]]
        TSM = STG[:, 3072:3328]
        pts = {0: None, 1: None}
        tsa = {0: None, 1: None}
        st = {}

        t_vkm = P.dma("sync",
                      VKM.rearrange("p (kvh ch dh) -> p kvh ch dh", kvh=KVH, ch=16),
                      v_scr, deps=bb)
        t_cvk = None
        if is_a:
            for kvh in range(KVH):
                bank = kvh % 2
                pt = PT[:, bank * 1024:bank * 1024 + 128]
                t_pe = P.tr(pt, CVF[:, kvh, :], C_IDB[:],
                            deps=bb + [pts[bank]], inc=1)
                t_ev = P.v(lambda e, pt=pt, kvh=kvh: e.tensor_copy(
                    out=CVK[:, kvh, :], in_=pt), deps=[t_pe])
                pts[bank] = t_ev
            t_cvk = pts[1]

        def rownorm_max(src_bf, n, out_col, deps):
            t_sq = P.act(STG2[:, 0:n], src_bf, AF.Square,
                         deps=deps + [st.get(("nrow",))], inc=1)
            t_mm = None
            for j in range((n + 511) // 512):
                n0, n1 = j * 512, min(n, j * 512 + 512)
                t_mm = P.mm(PA[0:1, n0:n1], C_ONECOL[:], STG2[:, n0:n1],
                            start=True, stop=True,
                            deps=([t_sq, st.get(("nrow",)), st.get(("mkb",))]
                                  if j == 0 else []),
                            inc=(1 if n1 == n else 0))
            t_mx = P.v(lambda e, n=n: e.max(out=SMALL[0:1, 64:72],
                                            in_=PA[0:1, 0:n]),
                       deps=[t_mm, st.get(("ncp",))])
            st[("nrow",)] = t_mx
            t_cp = P.v(lambda e, out_col=out_col: e.tensor_copy(
                out=SMALL[0:1, out_col:out_col + 1], in_=SMALL[0:1, 64:65]),
                deps=[t_mx])
            st[("ncp",)] = t_cp
            return t_cp

        t_norm = []
        for kvh in range(KVH):
            t_norm.append(rownorm_max(RK[:, kvh * 2048:(kvh + 1) * 2048], 2048,
                                      76 + kvh, bb))
            if is_a:
                t_norm.append(rownorm_max(CK[:, kvh, 0:129], 129, 84 + kvh,
                                          bb + [t_cvk]))

        def make_bias(qcol, kcol, outcol, deps):
            t_p = P.v(lambda e: e.tensor_tensor(
                out=SMALL[0:1, 72:73], in0=SMALL[0:1, qcol:qcol + 1],
                in1=SMALL[0:1, kcol:kcol + 1], op=OP.mult),
                deps=deps + [st.get(("mkb",))])
            t_s = P.act(SMALL[0:1, 73:74], SMALL[0:1, 72:73], AF.Sqrt,
                        deps=[t_p], inc=1)
            t_n = P.v(lambda e: e.tensor_scalar(
                out=SMALL[0:1, 74:75], in0=SMALL[0:1, 73:74], scalar1=-1.02,
                scalar2=None, op0=OP.mult), deps=[t_s])
            t_c = P.v(lambda e: e.tensor_copy(out=STG2[0:1, 4000:4001],
                                              in_=SMALL[0:1, 74:75]), deps=[t_n])
            t_bm = P.mm(PA[:, 1032:1040][:, 0:1], C_ONES[0:1, 0:128],
                        STG2[0:1, 4000:4001], start=True, stop=True,
                        deps=[t_c, st.get(("mkbm",)), st.get(("ropeprev",))],
                        inc=1)
            t_e = P.v(lambda e, outcol=outcol: e.tensor_copy(
                out=SMALL[:, outcol:outcol + 1], in_=PA[:, 1032:1033]),
                deps=[t_bm])
            st[("mkb",)] = t_e
            st[("mkbm",)] = t_e
            return t_e

        t_q = None
        for kvh in range(KVH):
            t_q = P.dma("sync", sb3(Q4, 4),
                        pmaj(q_scr[kvh * 4:(kvh + 1) * 4]),
                        deps=bb + [t_q, st.get(("qfree", kvh - 1))])
            t_qn = rownorm_max(Q4[:, 0:2048], 2048, 75, [t_q])
            for j in range(1, 4):
                t_qn2 = rownorm_max(Q4[:, j * 2048:(j + 1) * 2048], 2048, 64, [t_q])
                t_qn = P.v(lambda e: e.tensor_tensor(
                    out=SMALL[0:1, 75:76], in0=SMALL[0:1, 75:76],
                    in1=SMALL[0:1, 64:65], op=OP.max), deps=[t_qn, t_qn2])
                st[("ncp",)] = t_qn
            t_bf = make_bias(75, 76 + kvh, 100, [t_qn] + t_norm[-1:])
            t_bc = make_bias(75, 84 + kvh, 101, [t_qn]) if is_a else None
            t_rq = [None] * 4
            if is_a:
                for g in range(4):
                    qsl = Q4[:, g * 2048:(g + 1) * 2048]
                    pab = 1024 + (g % 2) * 512
                    t_b = None
                    for j4 in range(4):
                        t_pe4 = P.mm(
                            PA[:, pab:pab + 512], C_PERM[:],
                            qsl[:, j4 * 512:(j4 + 1) * 512], start=True, stop=True,
                            deps=([t_q, st.get(("ropeb", kvh, g - 2)), t_bf, t_bc]
                                  if j4 == 0 else [t_b]),
                            inc=1)
                        t_b = P.v(lambda e, pab=pab, j4=j4, g=g: e.scalar_tensor_tensor(
                            out=FWORK[:, g % 2 * 2048 + j4 * 512:
                                      g % 2 * 2048 + (j4 + 1) * 512],
                            in0=PA[:, pab:pab + 512], scalar=1.0,
                            in1=C_RS[:, j4 * 512:(j4 + 1) * 512],
                            op0=OP.mult, op1=OP.mult),
                            deps=[t_pe4, st.get(("ropec", kvh, g - 2))])
                        st[("ropeb", kvh, g)] = t_b
                    t_a = P.v(lambda e, qsl=qsl, g=g: e.tensor_tensor(
                        out=STG2[:, g % 2 * 2048:g % 2 * 2048 + 2048], in0=qsl,
                        in1=C_RC[:, :], op=OP.mult),
                        deps=[t_q, st.get(("roped", kvh, g - 2)), t_bf])
                    t_rq[g] = P.v(lambda e, g=g: e.tensor_tensor(
                        out=RQ4[:, g * 2048:(g + 1) * 2048],
                        in0=STG2[:, g % 2 * 2048:g % 2 * 2048 + 2048],
                        in1=FWORK[:, g % 2 * 2048:g % 2 * 2048 + 2048],
                        op=OP.add), deps=[t_a, st.get(("ropeb", kvh, g))])
                    st[("ropec", kvh, g)] = t_rq[g]
                    st[("roped", kvh, g)] = t_rq[g]
            else:
                t_rq = [t_q] * 4
            st[("prologue", kvh)] = ("dve", P.count["dve"])
            st[("ropeprev",)] = ("dve", P.count["dve"])

            for qt in range(NQT):
                Cn = qt + 1
                Nk = Cn * 128
                half_ok = Nk <= 1024
                t_pb_free = st.get(("aoev", kvh, qt - 1)) or \
                    st.get(("aoev", kvh - 1, NQT - 1))
                n_sl_chunks = (qt + 1 - max(0, qt - 1))
                if is_a:
                    av_per_g = ((0 if "av_c" in DBG_NO else 2) +
                                (0 if "av_f" in DBG_NO else Cn) +
                                (0 if "av_s" in DBG_NO else n_sl_chunks))
                else:
                    av_per_g = Cn
                av_state = {"i": 0, "n": 4 * av_per_g}

                def av_mm(g, lhsT, rhs, deps=(), inc=0, stop=False):
                    s0 = av_state["i"] == 0
                    last = av_state["i"] == av_state["n"] - 1
                    av_state["i"] += 1
                    return P.mm(PB[:, g * 128:(g + 1) * 128], lhsT, rhs,
                                start=s0, stop=last,
                                deps=list(deps) + ([t_pb_free] if s0 else []),
                                inc=inc)

                def transpose_group(srcs, lhsTs, g, dep_first, extra_av_deps=()):
                    bank = st.get(("tgb",), 0)
                    st[("tgb",)] = 1 - bank
                    pt = PT[:, bank * 1024:bank * 1024 + 1024]
                    t_p = None
                    for j, ap in enumerate(srcs):
                        t_p = P.tr(pt[:, j * 128:(j + 1) * 128], ap, C_IDB[:],
                                   deps=([dep_first, pts[bank]] if j == 0 else []),
                                   inc=(1 if j == len(srcs) - 1 else 0))
                    t_e = P.v(lambda e, pt=pt, n=len(srcs): e.tensor_copy(
                        out=TS[bank][:, 0:n * 128], in_=pt[:, 0:n * 128]),
                        deps=[t_p, tsa[bank]])
                    pts[bank] = t_e
                    t_av = None
                    for j, lh in enumerate(lhsTs):
                        t_av = av_mm(g, lh, TS[bank][:, j * 128:(j + 1) * 128],
                                     deps=([t_e] + list(extra_av_deps)
                                           if j == 0 else []),
                                     inc=(1 if j == len(lhsTs) - 1 else 0))
                    tsa[bank] = t_av
                    return t_av

                t_ec = [None] * 4
                if is_a:
                    t_sc = [None] * 4
                    for g in range(4):
                        lq = Q4[:, g * 2048 + qt * 128:g * 2048 + qt * 128 + 128]
                        csl = PA[:, g * 512:g * 512 + 512]
                        t_pe = P.mm(csl[:, 0:129], lq, CK[:, kvh, 0:129],
                                    start=True, stop=False,
                                    deps=[t_q, t_bc,
                                          st.get(("paf", kvh, qt - 1, 3)),
                                          (st.get(("prologue", kvh))
                                           if qt == 0 else None)])
                        t_pe = P.mm(csl[:, 0:129], C_IDB[:],
                                    C_CM[:, qt * 132:qt * 132 + 129],
                                    start=False, stop=True, deps=[], inc=1)
                        t_ec[g] = P.act(PHC[:, g * 132:g * 132 + 129],
                                        csl[:, 0:129], AF.Exp,
                                        bias=SMALL[:, 101:102], _dep_bf=t_bc,
                                        accum_out=SMALL[:, 20 + 3 * g:21 + 3 * g],
                                        deps=[t_pe, st.get(("phc", kvh, qt - 1, g))],
                                        inc=1)
                        t_sc[g] = P.v(lambda e, csl=csl, g=g: e.tensor_copy(
                            out=SC4[:, g * 132:g * 132 + 129], in_=csl[:, 0:129]),
                            deps=[t_pe, t_ec[g], st.get(("sc4", kvh, qt - 1, g))])
                        st[("sc4", kvh, qt, g)] = t_ec[g]
                        st[("pacons", kvh, qt, g)] = t_ec[g]
                    t_i1 = P.v(lambda e: e.tensor_tensor(
                        out=FWORK[:, 0:128], in0=SC4[:, 0:128], in1=SC4[:, 132:260],
                        op=OP.add), deps=[t_sc[0], t_sc[1]])
                    t_i2 = P.v(lambda e: e.tensor_tensor(
                        out=FWORK[:, 128:256], in0=SC4[:, 264:392],
                        in1=SC4[:, 396:524], op=OP.add), deps=[t_sc[2], t_sc[3]])
                    t_imp = P.v(lambda e: e.tensor_tensor(
                        out=FWORK[:, 0:128], in0=FWORK[:, 0:128],
                        in1=FWORK[:, 128:256], op=OP.add), deps=[t_i1, t_i2])
                    t_top = P.v(lambda e: e.max(out=SMALL[:, 32:40],
                                                in_=FWORK[:, 0:128]), deps=[t_imp])
                    t_c1 = P.v(lambda e: e.tensor_scalar(
                        out=FWORK[:, 256:384], in0=FWORK[:, 0:128],
                        scalar1=SMALL[:, 35:36], scalar2=None, op0=OP.is_ge),
                        deps=[t_top])
                    t_nm = P.v(lambda e: e.tensor_reduce(
                        out=SMALL[:, 40:41], in_=FWORK[:, 0:128], axis=AX.X,
                        op=OP.max), deps=[t_imp])
                    t_nm2 = P.v(lambda e: e.tensor_scalar(
                        out=SMALL[:, 41:42], in0=SMALL[:, 40:41], scalar1=-0.25,
                        scalar2=1000.0, op0=OP.mult, op1=OP.min), deps=[t_nm])
                    t_ei = P.act(FWORK[:, 384:512], FWORK[:, 0:128], AF.Exp,
                                 scale=0.25, bias=SMALL[:, 41:42],
                                 accum_out=SMALL[:, 42:43], deps=[t_nm2], inc=1)
                    t_sent = P.act(SMALL[:, 43:44], SMALL[:, 41:42], AF.Exp,
                                   bias=CONSTV[:, 1:2], deps=[t_ei], inc=1)
                    t_z = P.v(lambda e: e.tensor_tensor(
                        out=SMALL[:, 44:45], in0=SMALL[:, 42:43],
                        in1=SMALL[:, 43:44], op=OP.add), deps=[t_sent])
                    t_thr = P.v(lambda e: e.tensor_scalar(
                        out=SMALL[:, 45:46], in0=SMALL[:, 44:45], scalar1=1e-10,
                        scalar2=None, op0=OP.mult), deps=[t_z])
                    t_am = P.v(lambda e: e.scalar_tensor_tensor(
                        out=FWORK[:, 512:640], in0=FWORK[:, 384:512],
                        scalar=SMALL[:, 45:46], in1=FWORK[:, 256:384],
                        op0=OP.is_gt, op1=OP.mult), deps=[t_thr, t_c1])
                    t_amb = P.v(lambda e: e.tensor_scalar(
                        out=STG[:, 3328:3456], in0=FWORK[:, 512:640],
                        scalar1=-1.0, scalar2=None, op0=OP.add),
                        deps=[t_am, st.get(("amt_used", kvh, qt - 1))])
                    bank = st.get(("tgb",), 0)
                    st[("tgb",)] = 1 - bank
                    t_pe = P.tr(PT[:, bank * 1024:bank * 1024 + 128],
                                STG[:, 3328:3456], C_IDB[:],
                                deps=[t_amb, pts[bank]], inc=1)
                    t_amt = P.v(lambda e, bank=bank: e.tensor_copy(
                        out=STG[:, 3456:3584],
                        in_=PT[:, bank * 1024:bank * 1024 + 128]),
                        deps=[t_pe, st.get(("amt_used", kvh, qt - 1))])
                    pts[bank] = t_amt

                gtok = {}

                def emit_scores(g):
                    qsrc = RQ4 if is_a else Q4
                    lq = qsrc[:, g * 2048 + qt * 128:g * 2048 + qt * 128 + 128]
                    if is_a:
                        c0 = max(0, qt - 1)
                        nsl = (qt + 1 - c0) * 128
                        psb = g % 2
                        phs = PHS[:, psb * 256:psb * 256 + nsl]
                        t_ps = P.mm(PC[:, 0:nsl], lq,
                                    RK[:, kvh * 2048 + c0 * 128:
                                       kvh * 2048 + (qt + 1) * 128],
                                    start=True, stop=False,
                                    deps=[t_rq[g],
                                          st.get(("pcs", kvh, qt, g - 1)) or
                                          st.get(("pcs", kvh, qt - 1, 3))])
                        t_ps = P.mm(PC[:, 0:nsl], C_IDB[:],
                                    C_BAND[:, 256 - nsl:256], start=False,
                                    stop=True, deps=[], inc=1)
                        t_es = P.act(phs, PC[:, 0:nsl], AF.Exp,
                                     bias=SMALL[:, 100:101], _dep_bf=t_bf,
                                     accum_out=SMALL[:, 22 + 3 * g:23 + 3 * g],
                                     deps=[t_ps, st.get(("phs_used", kvh, qt, g - 2))
                                           or st.get(("phs_used", kvh, qt - 1, 2 + g))],
                                     inc=1)
                        st[("pcs", kvh, qt, g)] = t_es
                        st[("esl", kvh, qt, g)] = t_es
                    pab = (g % 2) * 1024 if half_ok else 0
                    PAg = PA[:, pab:pab + Nk]
                    nblk = (Nk + 511) // 512
                    jod = qt // 4
                    od_off = qt * 128
                    dep0 = [t_rq[g], t_vkm,
                            (st.get(("paf", kvh, qt, g - 2)) or
                             st.get(("paf", kvh, qt - 1, 2 + g))) if half_ok else
                            (st.get(("paf", kvh, qt, g - 1)) or
                             st.get(("paf", kvh, qt - 1, 3))),
                            (t_ec[g] if is_a else None),
                            (st.get(("pacons", kvh, qt, 3)) if is_a else None),
                            (st.get(("prologue", kvh)) if qt == 0 else None)]
                    for j in range(nblk):
                        n0 = j * 512
                        n1 = min(Nk, n0 + 512)
                        t_pe = P.mm(PAg[:, n0:n1], lq,
                                    RK[:, kvh * 2048 + n0:kvh * 2048 + n1],
                                    start=True,
                                    stop=(not is_a and j != jod),
                                    deps=(dep0 if j == 0 else []))
                    if is_a:
                        for j in range(nblk):
                            n0 = j * 512
                            n1 = min(Nk, n0 + 512)
                            t_pe = P.mm(PAg[:, n0:n1], STG[:, 3456:3584],
                                        C_E[:, n0:n1], start=False,
                                        stop=(j != jod),
                                        deps=([t_amt] if j == 0 else []))
                        t_pe = P.mm(PAg[:, od_off:od_off + 128], C_IDB[:],
                                    C_ODC[:], start=False, stop=True, deps=[],
                                    inc=1)
                        st[("amt_used", kvh, qt)] = t_pe
                    else:
                        t_pe = P.mm(PAg[:, od_off:od_off + 128], C_IDB[:],
                                    C_TRI[:], start=False, stop=True, deps=[],
                                    inc=1)
                    phb = g % 2
                    pha = PHAT[:, phb * 2048:phb * 2048 + Nk]
                    t_ef = P.act(pha, PAg, AF.Exp,
                                 bias=SMALL[:, 100:101],
                                 accum_out=SMALL[:, 21 + 3 * g:22 + 3 * g]
                                 if is_a else SMALL[:, 49 + g % 2:50 + g % 2],
                                 deps=[t_pe, t_bf,
                                       st.get(("phat_used", kvh, qt, g - 2))
                                       or st.get(("phat_used", kvh, qt - 1, 2 + g))],
                                 inc=1)
                    st[("paf", kvh, qt, g)] = t_ef
                    if is_a:
                        h3 = (4 * kvh + g) * 3
                        rz0 = 60 + (g % 2) * 8
                        t_rz = P.v(lambda e, g=g, rz0=rz0: e.reciprocal(
                            out=SMALL[:, rz0:rz0 + 3],
                            in_=SMALL[:, 20 + 3 * g:23 + 3 * g]),
                            deps=[t_ec[g], t_ef, st[("esl", kvh, qt, g)]])
                        t_sc3 = P.v(lambda e, qt=qt, h3=h3, g=g, rz0=rz0:
                                    e.tensor_tensor(
                            out=SMALL[:, 56 + (g % 2) * 16:59 + (g % 2) * 16],
                            in0=SMALL[:, rz0:rz0 + 3],
                            in1=G_TM[:, qt * 96 + h3:qt * 96 + h3 + 3],
                            op=OP.mult), deps=[t_rz])
                        sc0 = 56 + (g % 2) * 16
                        t_pc = P.v(lambda e, g=g, sc0=sc0: e.tensor_scalar(
                            out=PHC[:, g * 132:g * 132 + 129],
                            in0=PHC[:, g * 132:g * 132 + 129],
                            scalar1=SMALL[:, sc0:sc0 + 1], scalar2=None,
                            op0=OP.mult), deps=[t_sc3])
                        st[("phc", kvh, qt, g)] = t_pc
                        t_pf = P.v(lambda e, pha=pha, sc0=sc0: e.tensor_scalar(
                            out=pha, in0=pha,
                            scalar1=SMALL[:, sc0 + 1:sc0 + 2], scalar2=None,
                            op0=OP.mult), deps=[t_sc3])
                        psb = g % 2
                        c0 = max(0, qt - 1)
                        nsl = (qt + 1 - c0) * 128
                        t_psc = P.v(lambda e, psb=psb, nsl=nsl, sc0=sc0:
                                    e.tensor_scalar(
                            out=PHS[:, psb * 256:psb * 256 + nsl],
                            in0=PHS[:, psb * 256:psb * 256 + nsl],
                            scalar1=SMALL[:, sc0 + 2:sc0 + 3], scalar2=None,
                            op0=OP.mult), deps=[t_sc3])
                    else:
                        t_rz = P.v(lambda e, g=g: e.reciprocal(
                            out=SMALL[:, 52 + (g % 2):53 + (g % 2)],
                            in_=SMALL[:, 49 + g % 2:50 + g % 2]),
                            deps=[t_ef, st.get(("rzu", kvh, qt, g - 2)) or
                                  st.get(("rzu", kvh, qt - 1, 2 + g))])
                        t_pf = P.v(lambda e, pha=pha, g=g: e.tensor_scalar(
                            out=pha, in0=pha,
                            scalar1=SMALL[:, 52 + (g % 2):53 + (g % 2)],
                            scalar2=None, op0=OP.mult), deps=[t_rz])
                        st[("rzu", kvh, qt, g)] = t_pf
                        t_pc = t_psc = None
                    gtok[g] = (t_pf, t_pc, t_psc)

                def emit_avs(g):
                    t_pf, t_pc, t_psc = gtok[g]
                    phb = g % 2
                    if is_a and "av_c" not in DBG_NO:
                        bank = st.get(("tgb",), 0)
                        st[("tgb",)] = 1 - bank
                        pt = PT[:, bank * 1024:bank * 1024 + 1024]
                        t_p1 = P.tr(pt[:, 0:128], PHC[:, g * 132:g * 132 + 128],
                                    C_IDB[:], deps=[t_pc, pts[bank]], inc=1)
                        t_p2 = P.tr(pt[0:1, 128:256],
                                    PHC[:, g * 132 + 128:g * 132 + 129],
                                    C_IDB[:], deps=[], inc=1)
                        t_e1 = P.v(lambda e, pt=pt: e.tensor_copy(
                            out=TS[bank][:, 0:128], in_=pt[:, 0:128]),
                            deps=[t_p2, tsa[bank]])
                        t_e2 = P.v(lambda e, pt=pt: e.tensor_copy(
                            out=TSM[0:1, 0:128], in_=pt[0:1, 128:256]),
                            deps=[t_p2, st.get(("tsm",))])
                        pts[bank] = t_e2
                        t_av = av_mm(g, CVK[:, kvh, :], TS[bank][:, 0:128],
                                     deps=[t_e1], inc=1)
                        tsa[bank] = t_av
                        t_av = av_mm(g, MEMV[0:1, kvh, :], TSM[0:1, 0:128],
                                     deps=[t_e2], inc=1)
                        st[("tsm",)] = t_av
                    if not (is_a and "av_f" in DBG_NO):
                        done = 0
                        t_last = None
                        while done < Cn:
                            grp = min(8, Cn - done)
                            srcs = [PHAT[:, phb * 2048 + (done + j) * 128:
                                         phb * 2048 + (done + j + 1) * 128]
                                    for j in range(grp)]
                            lhs = [VKM[:, (kvh * 16 + done + j) * 128:
                                       (kvh * 16 + done + j + 1) * 128]
                                   for j in range(grp)]
                            t_last = transpose_group(srcs, lhs, g, t_pf)
                            done += grp
                        st[("phat_used", kvh, qt, g)] = t_last
                    if is_a and "av_s" not in DBG_NO:
                        c0 = max(0, qt - 1)
                        nch = qt + 1 - c0
                        psb = g % 2
                        srcs = [PHS[:, psb * 256 + j * 128:psb * 256 + (j + 1) * 128]
                                for j in range(nch)]
                        lhs = [VKM[:, (kvh * 16 + c0 + j) * 128:
                                   (kvh * 16 + c0 + j + 1) * 128]
                               for j in range(nch)]
                        t_last = transpose_group(srcs, lhs, g, t_psc)
                        st[("phs_used", kvh, qt, g)] = t_last

                for g in range(4):
                    emit_scores(g)
                    if g >= 1:
                        emit_avs(g - 1)
                emit_avs(3)

                t_all_av = ("pe", P.count["pe"])
                abuf = qt % 2
                t_ev = P.v(lambda e, abuf=abuf: e.tensor_copy(
                    out=AOST[abuf], in_=PB[:, 0:512]),
                    deps=[t_all_av, st.get(("aodma", kvh, qt - 2)) or
                          st.get(("aodma", kvh - 1, NQT - 2 + qt))])
                t_st = P.dma("gpsimd",
                             ao_scr[qt, :, kvh * 4:(kvh + 1) * 4, :],
                             sb3(AOST[abuf], 4), deps=[t_ev])
                st[("aodma", kvh, qt)] = t_st
                st[("aoev", kvh, qt)] = t_ev
            st[("qfree", kvh)] = ("pe", P.count["pe"])
        return P.bar()

    def phase_oproj(w_kind, x_in, x_out):
        bb = P.bar()
        t_w = None
        AT = [STG, STG2]
        XT = [FWORK[:, 0:1024], FWORK[:, 1024:2048]]
        XO = [FWORK[:, 2048:3072], FWORK[:, 3072:4096]]
        t_at = [None, None]
        t_xt = [None, None]
        t_st = [None, None]
        t_evx = [None, None]
        for os_ in range(4):
            if w_kind == "woT":
                t_w = P.dma("sync", sb3(WSL, 32), pmaj(woT[os_]),
                            deps=bb + [t_w] + t_evx)
            else:
                t_wc = None
                t_wi = None
                for hd in range(32):
                    t_wi = P.dma("sync", SMALL[:, 0:1024],
                                 wo_b[hd * 128:(hd + 1) * 128,
                                      os_ * 1024:(os_ + 1) * 1024],
                                 deps=bb + [t_wc, t_wi] + t_evx)
                    t_wc = P.v(lambda e, hd=hd: e.tensor_copy(
                        out=WSL[:, hd * 1024:(hd + 1) * 1024],
                        in_=SMALL[:, 0:1024]), deps=[t_wi])
                t_w = t_wc
            for qt in range(NQT):
                buf = qt % 2
                t_at[buf] = P.dma("sync", AT[buf][:, :],
                                  ao_scr[qt].rearrange("p hd t -> p (hd t)"),
                                  deps=bb + [t_at[buf], t_st[buf]])
                t_xt[buf] = P.dma("sync", XT[buf],
                                  x_in[qt * 128:(qt + 1) * 128,
                                       os_ * 1024:(os_ + 1) * 1024],
                                  deps=[t_xt[buf], t_evx[buf]])
                psl = PA[:, buf * 1024:buf * 1024 + 1024]
                t_pe = None
                for hd in range(32):
                    for j in range(2):
                        t_pe = P.mm(psl[:, j * 512:(j + 1) * 512],
                                    AT[buf][:, hd * 128:(hd + 1) * 128],
                                    WSL[:, hd * 1024 + j * 512:hd * 1024 + (j + 1) * 512],
                                    start=(hd == 0), stop=(hd == 31),
                                    deps=([t_at[buf], t_w, t_evx[buf]]
                                          if hd == 0 and j == 0 else []),
                                    inc=(1 if hd == 31 and j == 1 else 0))
                t_evx[buf] = P.v(lambda e, psl=psl, buf=buf: e.tensor_tensor(
                    out=XO[buf], in0=psl, in1=XT[buf], op=OP.add),
                    deps=[t_pe, t_xt[buf], t_st[buf]])
                t_st[buf] = P.dma("gpsimd",
                                  x_out[qt * 128:(qt + 1) * 128,
                                        os_ * 1024:(os_ + 1) * 1024],
                                  XO[buf], deps=[t_evx[buf]])
        return P.bar()

    def phase_ffn(x_in, x_out):
        bb = P.bar()
        t_w = None
        HT = [STG, STG2]
        XT = [FWORK[:, 0:1024], FWORK[:, 1024:2048]]
        XO = [FWORK[:, 2048:3072], FWORK[:, 3072:4096]]
        t_ht = [None, None]
        t_xt = [None, None]
        t_st = [None, None]
        t_evx = [None, None]
        for os_ in range(4):
            t_w = P.dma("sync", sb3(WSL, 32), pmaj(bwT[os_]),
                        deps=bb + [t_w] + t_evx)
            for qt in range(NQT):
                buf = qt % 2
                t_ht[buf] = P.dma("sync", HT[buf][:, :], h_scr[qt],
                                  deps=bb + [t_ht[buf], t_evx[buf]])
                t_xt[buf] = P.dma("sync", XT[buf],
                                  x_in[qt * 128:(qt + 1) * 128,
                                       os_ * 1024:(os_ + 1) * 1024],
                                  deps=[t_xt[buf], t_evx[buf]])
                psl = PA[:, buf * 1024:buf * 1024 + 1024]
                t_pe = None
                for ic in range(NIC):
                    for j in range(2):
                        t_pe = P.mm(psl[:, j * 512:(j + 1) * 512],
                                    HT[buf][:, ic * 128:(ic + 1) * 128],
                                    WSL[:, ic * 1024 + j * 512:ic * 1024 + (j + 1) * 512],
                                    start=(ic == 0), stop=(ic == NIC - 1),
                                    deps=([t_ht[buf], t_w, t_evx[buf]]
                                          if ic == 0 and j == 0 else []),
                                    inc=(1 if ic == NIC - 1 and j == 1 else 0))
                t_evx[buf] = P.v(lambda e, psl=psl, buf=buf: e.tensor_tensor(
                    out=XO[buf], in0=psl, in1=XT[buf], op=OP.add),
                    deps=[t_pe, t_xt[buf], t_st[buf]])
                t_st[buf] = P.dma("gpsimd",
                                  x_out[qt * 128:(qt + 1) * 128,
                                        os_ * 1024:(os_ + 1) * 1024],
                                  XO[buf], deps=[t_evx[buf]])
        return P.bar()

    with es:
        block = es.enter_context(nc.Block())
        for engname in ENGINES:
            def emit(engobj, engname=engname):
                P.reset_for(engname, engobj)
                program()
            getattr(block, engname)(emit)

    return nc, tap_outs


def make_in_maps(inputs):
    CONST = host_constants()
    x_full = np.ascontiguousarray(np.asarray(inputs["L0_attn_a_hidden_states"],
                                             dtype=np.float32))
    base = {
        "w_qkv": inputs["L0_attn_a_w_qkv"], "w_o": inputs["L0_attn_a_w_o"],
        "k_pos": inputs["L0_attn_a_k_pos"], "v_pos": inputs["L0_attn_a_v_pos"],
        "k_w1": inputs["L0_attn_a_k_w1"], "k_b1": inputs["L0_attn_a_k_b1"],
        "k_w2": inputs["L0_attn_a_k_w2"], "k_b2": inputs["L0_attn_a_k_b2"],
        "v_w1": inputs["L0_attn_a_v_w1"], "v_b1": inputs["L0_attn_a_v_b1"],
        "v_w2": inputs["L0_attn_a_v_w2"], "v_b2": inputs["L0_attn_a_v_b2"],
        "mem_kv": inputs["L0_attn_a_mem_kv"], "w_gate": inputs["L0_attn_a_w_gate"],
        "b_gate": inputs["L0_attn_a_b_gate"],
        "wq_b": inputs["L0_attn_b_wq"], "wk_b": inputs["L0_attn_b_wk"],
        "wv_b": inputs["L0_attn_b_wv"], "wo_b": inputs["L0_attn_b_wo"],
        "w_ffn": inputs["L0_ffn_weight"],
    }
    base = {k: np.ascontiguousarray(np.asarray(v, dtype=np.float32))
            for k, v in base.items()}
    for k, v in CONST.items():
        base["c_" + k] = np.ascontiguousarray(v)
    return [dict(base, x0=np.ascontiguousarray(x_full[i])) for i in range(B)]


def kernel(**inputs):
    nc, _ = build_program()
    in_maps = make_in_maps(inputs)
    res = run_bass_kernel_spmd(nc, in_maps, core_ids=list(range(B)))
    return np.stack([np.asarray(r["y"], dtype=np.float32)
                     for r in res.results], axis=0)


# version 06 (final)
# speedup vs baseline: 1.1932x; 1.1932x over previous
import numpy as np
import ml_dtypes
import concourse.bass as bass
import concourse.mybir as mybir
from concourse.bass_utils import run_bass_kernel_spmd
from concourse.alu_op_type import AluOpType
from contextlib import ExitStack

F32 = mybir.dt.float32
BF16 = mybir.dt.bfloat16
AF = mybir.ActivationFunctionType
AX = mybir.AxisListType
OP = AluOpType

B = 8
S = 2048
D = 4096
H = 32
KVH = 8
G = 4
DH = 128
NQT = S // 128
NIC = D // 128
SCALE = DH ** -0.5
BIGM = float(2.0 ** 100)
EPS = 1e-5

ENGINES = ("sync", "gpsimd", "vector", "scalar", "tensor")
DBG_NO = ()


def pmaj(ap):
    return ap.transpose([1, 0, 2])


def sb3(ap, n):
    return ap.rearrange("p (a b) -> p a b", a=n)


def host_constants():
    bf = ml_dtypes.bfloat16
    c = {}
    c["IDB"] = np.eye(128, dtype=bf)
    perm = np.zeros((128, 128), dtype=bf)
    for j in range(64):
        perm[2 * j, 2 * j + 1] = 1
        perm[2 * j + 1, 2 * j] = 1
    c["PERM"] = perm
    j = np.arange(64, dtype=np.float64)
    inv = 10000.0 ** (-2.0 * j / 128.0)
    t = np.arange(S, dtype=np.float64)
    ang = t[None, :] * inv[:, None]
    c2 = np.zeros((128, S), dtype=np.float64)
    s2 = np.zeros((128, S), dtype=np.float64)
    c2[0::2] = np.cos(ang)
    c2[1::2] = np.cos(ang)
    s2[0::2] = -np.sin(ang)
    s2[1::2] = np.sin(ang)
    c["RC"] = c2.astype(bf)
    c["RS"] = s2.astype(bf)
    E = np.zeros((128, S), dtype=bf)
    for cc in range(128):
        E[cc, cc * 16:(cc + 1) * 16] = BIGM
    c["EBIG"] = E
    od = np.zeros((128, 128), dtype=bf)
    for p in range(128):
        od[p, (p // 16) * 16: p + 1] = BIGM
    c["ODC"] = od
    cm = np.zeros((128, NQT, 132), dtype=bf)
    for qt in range(NQT):
        for p in range(128):
            i = qt * 128 + p
            for cc in range(128):
                if i < 16 * (cc + 1):
                    cm[p, qt, cc] = -BIGM
            cm[p, qt, 129:132] = -BIGM
    c["CMASK"] = cm.reshape(128, NQT * 132)
    band = np.zeros((128, 256), dtype=bf)
    for p in range(128):
        for jc in range(256):
            if not (0 <= 128 + p - jc <= 64):
                band[p, jc] = -BIGM
    c["BAND"] = band
    tri = np.zeros((128, 128), dtype=bf)
    for p in range(128):
        tri[p, p + 1:] = -BIGM
    c["TRI"] = tri
    c["ONES"] = np.ones((1, 512), dtype=bf)
    c["ONECOL"] = np.ones((128, 1), dtype=bf)
    return c


NLANES = 8


class Prog:

    def __init__(self, nc, sems):
        self.nc = nc
        self.sems = sems
        self.count = {k: 0 for k in sems}
        self.cur = None
        self.eng = None
        self.wait_cache = {}
        self.engine_sem = {"sync": "sq", "gpsimd": "gq", "vector": "dve",
                           "scalar": "act", "tensor": "pe"}
        self.lane_idx = {"sync": 0, "gpsimd": 0}

    def reset_for(self, engname, engobj):
        self.cur = engname
        self.eng = engobj
        self.count = {k: 0 for k in self.sems}
        self.wait_cache = {}
        self.lane_idx = {"sync": 0, "gpsimd": 0}

    def op(self, engname, fn, deps=(), inc=0):
        semname = self.engine_sem[engname]
        if self.cur == engname:
            for d in deps:
                if d is None:
                    continue
                s, v = d
                if v <= 0:
                    continue
                key = s
                if self.wait_cache.get(key, 0) < v:
                    self.eng.wait_ge(self.sems[s], v)
                    self.wait_cache[key] = v
            inst = fn(self.eng)
            if inc:
                inst.then_inc(self.sems[semname], inc)
        if inc:
            self.count[semname] += inc
            return (semname, self.count[semname])
        return None

    def dma(self, q, out, in_, deps=(), **kw):
        lane = self.lane_idx[q]
        self.lane_idx[q] = (lane + 1) % NLANES
        semname = self.engine_sem[q] + str(lane)
        prev = self.count[semname]
        deps = list(deps) + ([(semname, prev)] if prev else [])
        if self.cur == q:
            for d in deps:
                if d is None:
                    continue
                s, v = d
                if v <= 0:
                    continue
                if self.wait_cache.get(s, 0) < v:
                    self.eng.wait_ge(self.sems[s], v)
                    self.wait_cache[s] = v
            inst = self.eng.dma_start(out=out, in_=in_, **kw)
            inst.then_inc(self.sems[semname], 16)
        self.count[semname] += 16
        return (semname, self.count[semname])

    def mm(self, out, lhsT, rhs, start, stop, deps=(), inc=0):
        return self.op("tensor",
                       lambda e: e.matmul(out=out, lhsT=lhsT, rhs=rhs,
                                          start=start, stop=stop),
                       deps=deps, inc=inc)

    def tr(self, out, in_, iden, deps=(), inc=0):
        k = in_.shape[0]
        b = in_.base_partition()
        if iden.shape[0] != k or b != 0:
            iden = iden[b:b + k, b:b + k]
        return self.op("tensor",
                       lambda e: e.transpose(out=out, in_=in_, identity=iden),
                       deps=deps, inc=inc)

    def act(self, out, in_, func, bias=None, scale=None, accum_out=None,
            deps=(), inc=0, _dep_bf=None):
        deps = list(deps) + ([_dep_bf] if _dep_bf is not None else [])
        kw = {}
        if bias is not None:
            kw["bias"] = bias
        if scale is not None:
            kw["scale"] = scale
        if accum_out is not None:
            kw["accum_out"] = accum_out
        return self.op("scalar",
                       lambda e: e.activation(out=out, in_=in_, func=func, **kw),
                       deps=deps, inc=inc)

    def v(self, fn, deps=(), inc=1):
        return self.op("vector", fn, deps=deps, inc=inc)

    def bar(self):
        return [(s, c) for s, c in self.count.items()]


def build_program(upto=13, taps=(), sbuf_taps=(), debug_no=()):
    global DBG_NO
    DBG_NO = debug_no
    nc = bass.Bass("TRN2", target_bir_lowering=False)
    CONST = host_constants()

    def inp(name, shape, dt=F32):
        return nc.dram_tensor(name, list(shape), dt, kind="ExternalInput").ap()

    x0 = inp("x0", (S, D))
    w_qkv = inp("w_qkv", (6144, D))
    w_o = inp("w_o", (D, D))
    k_pos = inp("k_pos", (KVH, 16, DH))
    v_pos = inp("v_pos", (KVH, 16, DH))
    k_w1 = inp("k_w1", (2048, 2048))
    k_b1 = inp("k_b1", (2048,))
    k_w2 = inp("k_w2", (DH, 2048))
    k_b2 = inp("k_b2", (DH,))
    v_w1 = inp("v_w1", (2048, 2048))
    v_b1 = inp("v_b1", (2048,))
    v_w2 = inp("v_w2", (DH, 2048))
    v_b2 = inp("v_b2", (DH,))
    mem_kv = inp("mem_kv", (2, KVH, 1, DH))
    w_gate = inp("w_gate", (3 * H, D))
    b_gate = inp("b_gate", (3 * H,))
    wq_b = inp("wq_b", (D, D))
    wk_b = inp("wk_b", (D, 1024))
    wv_b = inp("wv_b", (D, 1024))
    wo_b = inp("wo_b", (D, D))
    w_ffn = inp("w_ffn", (D, D))
    consts_in = {k: inp("c_" + k, v.shape,
                        BF16 if v.dtype == ml_dtypes.bfloat16 else F32)
                 for k, v in CONST.items()}

    y = nc.dram_tensor("y", [S, D], F32, kind="ExternalOutput").ap()

    def scr(name, shape, dt=BF16):
        return nc.dram_tensor(name, list(shape), dt, kind="Internal").ap()

    x1 = scr("x1", (S, D), F32)
    x2 = scr("x2", (S, D), F32)
    x3 = scr("x3", (S, D), F32)
    h_scr = scr("h_scr", (NQT, 128, NIC * 128))
    q_scr = scr("q_scr", (H, 128, S))
    v_scr = scr("v_scr", (128, KVH, 16, 128))
    ao_scr = scr("ao_scr", (NQT, 128, 32, 128))
    wqkvT = scr("wqkvT", (48, 128, NIC * 128))
    woT = scr("woT", (4, 32, 128, 1024))
    bwT = scr("bwT", (4, NIC, 128, 1024))
    w1kT = scr("w1kT", (16, 128, 2048))
    w1vT = scr("w1vT", (16, 128, 2048))
    w2kT = scr("w2kT", (128, 2048))
    w2vT = scr("w2vT", (128, 2048))
    qb_q = scr("qb_q", (D, D))
    qb_k = scr("qb_k", (D, 1024))
    qb_v = scr("qb_v", (D, 1024))

    scratch_map = {"x1": x1, "x2": x2, "x3": x3, "h_scr": h_scr, "q_scr": q_scr,
                   "v_scr": v_scr, "ao_scr": ao_scr, "wqkvT": wqkvT, "woT": woT,
                   "bwT": bwT, "w1kT": w1kT, "w2kT": w2kT, "w1vT": w1vT,
                   "w2vT": w2vT}
    tap_outs = {}
    for name in taps:
        src = scratch_map[name]
        tap_outs[name] = nc.dram_tensor("tap_" + name, list(src.tensor.shape),
                                        src.dtype, kind="ExternalOutput").ap()
    sbuf_tap_list = list(sbuf_taps)

    es = ExitStack()
    sb = lambda n, shape, dt=BF16: es.enter_context(nc.sbuf_tensor(n, list(shape), dt))
    psalloc = lambda n, shape, dt=F32: es.enter_context(nc.psum_tensor(n, list(shape), dt))

    C_IDB = sb("c_idb", (128, 128))
    C_PERM = sb("c_perm", (128, 128))
    C_RC = sb("c_rc", (128, S))
    C_RS = sb("c_rs", (128, S))
    C_E = sb("c_e", (128, S))
    C_ODC = sb("c_odc", (128, 128))
    C_CM = sb("c_cm", (128, NQT * 132))
    C_BAND = sb("c_band", (128, 256))
    C_TRI = sb("c_tri", (128, 128))
    C_ONES = sb("c_ones", (1, 512))
    C_ONECOL = sb("c_onecol", (128, 1))
    WGT = sb("wgt", (128, 32 * 96))
    BGATE = sb("bgate", (1, 96))
    G_TM = sb("g_tm", (128, NQT * 96))
    CK = sb("ck", (128, KVH, 132))
    CVF = sb("cvf", (128, KVH, 128))
    CVK = sb("cvk", (128, KVH, 128))
    MEMV = sb("memv", (1, KVH, 128))
    KPF = sb("kpf", (128, 16 * 8))
    VPF = sb("vpf", (128, 16 * 8))
    B1K = sb("b1k", (128, 16), F32)
    B1V = sb("b1v", (128, 16), F32)
    B2K = sb("b2k", (128, 1), F32)
    B2V = sb("b2v", (128, 1), F32)
    B2KR = sb("b2kr", (1, 128))
    B2VR = sb("b2vr", (1, 128))
    SMALL = sb("small", (128, 1024), F32)
    CONSTV = sb("constv", (128, 4), F32)
    STG = sb("stg", (128, 4096))
    STG2 = sb("stg2", (128, 4096))
    PHAT = sb("phat", (128, 4096))
    PHC = sb("phc", (128, 4 * 132))
    PHS = sb("phs", (128, 512))
    SC4 = sb("sc4", (128, 4 * 132), F32)
    FWORK = sb("fwork", (128, 4096), F32)
    BIGA = sb("biga", (128, 53248))

    RK = BIGA[:, 0:16384]
    VKM = BIGA[:, 16384:32768]
    WSL = BIGA[:, 0:32768]
    PH = BIGA[:, 32768:53248]

    PA = psalloc("PA", (128, 2048))
    PB = psalloc("PB", (128, 512))
    PC = psalloc("PC", (128, 512))
    PT = psalloc("PT", (128, 2048), BF16)

    semnames = ["dve", "act", "pe"]
    for q in ("sq", "gq"):
        semnames += [q + str(i) for i in range(NLANES)]
    sems = {k: es.enter_context(nc.semaphore(k)) for k in semnames}
    P = Prog(nc, sems)

    sbuf_ap_map = {"RK": RK, "VKM": VKM, "CK": CK[:].rearrange("p h c -> p (h c)"),
                   "SMALL": SMALL[:], "FWORK": FWORK[:], "STG2": STG2[:],
                   "PHAT": PHAT[:], "PHC": PHC[:], "SC4": SC4[:], "PHS": PHS[:],
                   "CVF": CVF[:].rearrange("p h c -> p (h c)"),
                   "CVK": CVK[:].rearrange("p h c -> p (h c)"),
                   "G_TM": G_TM[:], "KPF": KPF[:], "B1K": B1K[:],
                   "WGT": WGT[:], "MEMV": MEMV[:].rearrange("p h c -> p (h c)")}
    sbuf_tap_outs = {}
    for name in sbuf_tap_list:
        ap = sbuf_ap_map[name]
        sbuf_tap_outs[name] = nc.dram_tensor(
            "stap_" + name, [ap.shape[0], ap.shape[1]], ap.dtype,
            kind="ExternalOutput").ap()

    def program():
        tok = None
        for nm, tile in [("IDB", C_IDB), ("PERM", C_PERM), ("RC", C_RC),
                         ("RS", C_RS), ("EBIG", C_E), ("ODC", C_ODC),
                         ("CMASK", C_CM), ("BAND", C_BAND), ("TRI", C_TRI),
                         ("ONES", C_ONES), ("ONECOL", C_ONECOL)]:
            tok = P.dma("sync", tile[:], consts_in[nm], deps=[tok])
        P.v(lambda e: e.memset(CONSTV[:, 0:1], float(EPS)))
        P.v(lambda e: e.memset(CONSTV[:, 1:2], -1000.0))
        P.v(lambda e: e.memset(CK[:].rearrange("p h c -> p (h c)"), 0.0))
        P.v(lambda e: e.memset(SMALL[:], 0.0))
        P.v(lambda e: e.memset(FWORK[:], 0.0))
        P.v(lambda e: e.memset(STG2[:], 0.0))
        P.v(lambda e: e.memset(PHAT[:], 0.0))
        P.v(lambda e: e.memset(PHC[:].rearrange("p c -> p c"), 0.0))
        P.v(lambda e: e.memset(SC4[:], 0.0))
        t_consts = P.bar()

        def scoped(name, fn, *a, **k):
            with nc.named_scope(name):
                return fn(*a, **k)
        if upto >= 1:
            scoped("P1_prep", phase_weight_prep, t_consts)
        if upto >= 2:
            scoped("P2_ln1", phase_ln, x0)
        if upto >= 3:
            scoped("P3_qkv_a", phase_qkv, is_a=True)
        if upto >= 4:
            scoped("P4_attn_a", phase_attn, is_a=True)
        if upto >= 5:
            scoped("P5_oproj_a", phase_oproj, w_kind="woT", x_in=x0, x_out=x1)
        if upto >= 6:
            scoped("P6_ln2", phase_ln, x1)
        if upto >= 7:
            scoped("P7_ffn1", phase_ffn, x_in=x1, x_out=x2)
        if upto >= 8:
            scoped("P8_ln3", phase_ln, x2)
        if upto >= 9:
            scoped("P9_qkv_b", phase_qkv, is_a=False)
        if upto >= 10:
            scoped("P10_attn_b", phase_attn, is_a=False)
        if upto >= 11:
            scoped("P11_oproj_b", phase_oproj, w_kind="nat", x_in=x2, x_out=x3)
        if upto >= 12:
            scoped("P12_ln4", phase_ln, x3)
        if upto >= 13:
            scoped("P13_ffn2", phase_ffn, x_in=x3, x_out=y)

        bb = P.bar()
        tok = None
        for name in sbuf_tap_list:
            ap = sbuf_ap_map[name]
            out_ap = sbuf_tap_outs[name]
            tok = P.dma("sync", out_ap, ap, deps=bb + [tok])
        for name, out_ap in tap_outs.items():
            src = scratch_map[name]
            flat_src = src.rearrange(
                " ".join("d%d" % i for i in range(len(src.tensor.shape))) +
                " -> (" + " ".join("d%d" % i for i in range(len(src.tensor.shape))) + ")")
            flat_dst = out_ap.rearrange(
                " ".join("d%d" % i for i in range(len(src.tensor.shape))) +
                " -> (" + " ".join("d%d" % i for i in range(len(src.tensor.shape))) + ")")
            tok = P.dma("sync", flat_dst, flat_src, deps=bb + [tok])
        P.op("sync", lambda e: e.nop(), deps=bb + ([tok] if tok else []))

    def phase_weight_prep(t_consts):
        IN = FWORK
        INBB = [BIGA[:, 0:4096], BIGA[:, 4096:8192]]
        OUTB = [BIGA[:, 8192:12288], BIGA[:, 12288:16384]]
        FW2 = PH.bitcast(F32)
        state = {"dma_out": [None, None], "pt_ev": [None, None], "in": None,
                 "cast": None, "bi": 0, "tpo": {}, "evo": {}}

        def do_chunk(src_rows, n_i, out_ap, sign=False):
            bi = state["bi"]
            state["bi"] = 1 - bi
            INB = INBB[bi]
            OUT = OUTB[bi]
            prev_out = state["dma_out"]
            if not sign:
                t_cast = P.dma("gpsimd", INB[:, 0:n_i * 128], src_rows,
                               deps=[state["tpo"].get(bi)] + t_consts)
                state["cast"] = t_cast
            else:
                t_in = P.dma("sync", IN[:, 0:n_i * 128], src_rows,
                             deps=[state["in"], state["cast"]] + t_consts)
                state["in"] = t_in
            if sign:
                t_abs = P.act(FW2[:, 0:n_i * 128], IN[:, 0:n_i * 128], AF.Abs,
                              accum_out=SMALL[:, 0:1],
                              deps=[t_in, state.get("fw2_read")], inc=1)
                t_sg = P.act(FW2[:, 0:n_i * 128], IN[:, 0:n_i * 128], AF.Sign,
                             deps=[t_abs], inc=1)
                t_cast = P.v(lambda e, INB=INB: e.tensor_scalar(
                    out=INB[:, 0:n_i * 128], in0=FW2[:, 0:n_i * 128],
                    scalar1=SMALL[:, 0:1], scalar2=1.0 / (n_i * 128),
                    op0=OP.mult, op1=OP.mult),
                    deps=[t_sg, state["tpo"].get(bi)])
                state["cast"] = t_sg
                state["fw2_read"] = t_cast
            n_done = 0
            gi = 0
            last_ev = None
            while n_done < n_i:
                grp = min(8, n_i - n_done)
                half = gi % 2
                pt = PT[:, half * 1024:half * 1024 + 1024]
                t_pe = None
                for j in range(grp):
                    blk = n_done + j
                    t_pe = P.tr(pt[:, j * 128:(j + 1) * 128],
                                INB[:, blk * 128:(blk + 1) * 128], C_IDB[:],
                                deps=([t_cast, state["pt_ev"][half]]
                                      if j == 0 else []),
                                inc=(1 if j == grp - 1 else 0))
                t_ev = P.v(lambda e, pt=pt, n_done=n_done, grp=grp, OUT=OUT:
                           e.tensor_copy(
                    out=OUT[:, n_done * 128:(n_done + grp) * 128],
                    in_=pt[:, 0:grp * 128]),
                    deps=[t_pe, state["evo"].get(bi)])
                state["pt_ev"][half] = t_ev
                last_ev = t_ev
                n_done += grp
                gi += 1
            state["tpo"][bi] = ("pe", P.count["pe"])
            src_out = (OUT[:, 0:n_i * 128] if out_ap.ndim == 2
                       else sb3(OUT[:, 0:n_i * 128], out_ap.shape[1]))
            t_out = P.dma("gpsimd", out_ap, src_out, deps=[last_ev])
            state["evo"][bi] = t_out
            state["dma_out"] = [t_out, state["dma_out"][0]]
            return t_out

        for oc in range(48):
            do_chunk(w_qkv[oc * 128:(oc + 1) * 128, :], NIC, wqkvT[oc])
        for oc in range(32):
            os_, ocl = oc // 8, oc % 8
            do_chunk(w_o[oc * 128:(oc + 1) * 128, :], 32,
                     pmaj(woT[os_, :, :, ocl * 128:(ocl + 1) * 128]))
        for oc in range(32):
            os_, ocl = oc // 8, oc % 8
            do_chunk(w_ffn[oc * 128:(oc + 1) * 128, :], 32,
                     pmaj(bwT[os_, :, :, ocl * 128:(ocl + 1) * 128]), sign=True)
        for oc in range(16):
            do_chunk(k_w1[oc * 128:(oc + 1) * 128, :], 16, w1kT[oc])
        for oc in range(16):
            do_chunk(v_w1[oc * 128:(oc + 1) * 128, :], 16, w1vT[oc])
        do_chunk(k_w2[:, :], 16, w2kT)
        do_chunk(v_w2[:, :], 16, w2vT)
        P.dma("gpsimd", qb_q, wq_b, deps=t_consts)
        P.dma("gpsimd", qb_k, wk_b, deps=t_consts)
        P.dma("gpsimd", qb_v, wv_b, deps=t_consts)

        INB = INBB[0]
        bb = P.bar()
        tk = state["dma_out"][0]
        tkc = state["cast"]
        tk = P.dma("sync", IN[0:96, 0:4096], w_gate[:, :], deps=bb + [tk, tkc])
        tk = P.v(lambda e: e.tensor_copy(out=INB[0:96, 0:4096],
                                         in_=IN[0:96, 0:4096]), deps=bb + [tk])
        for ic in range(NIC):
            half = ic % 2
            pt = PT[:, half * 1024:half * 1024 + 96]
            tk = P.tr(pt, INB[0:96, ic * 128:(ic + 1) * 128], C_IDB[:],
                      deps=[tk], inc=1)
            tk = P.v(lambda e, pt=pt, ic=ic: e.tensor_copy(
                out=WGT[:, ic * 96:(ic + 1) * 96], in_=pt), deps=[tk])
        tk = P.dma("gpsimd", BGATE[0:1, :], b_gate[None, :], deps=[tk])
        for (posin, dst) in ((k_pos, KPF), (v_pos, VPF)):
            tk = P.dma("sync", IN[0:8, 0:2048],
                       posin.rearrange("h tb d -> h (tb d)"), deps=[tk])
            tk = P.v(lambda e: e.tensor_copy(out=INB[0:8, 0:2048],
                                             in_=IN[0:8, 0:2048]), deps=[tk])
            for tb in range(16):
                half = tb % 2
                pt = PT[:, half * 1024 + 512:half * 1024 + 520]
                tk = P.tr(pt, INB[0:8, tb * 128:(tb + 1) * 128],
                          C_IDB[:], deps=[tk], inc=1)
                tk = P.v(lambda e, pt=pt, dst=dst, tb=tb: e.tensor_copy(
                    out=dst[:, tb * 8:(tb + 1) * 8], in_=pt), deps=[tk])
        tk = P.dma("sync", IN[0:16, 0:128], k_b1.rearrange("(oc p) -> oc p", p=128),
                   deps=[tk])
        tk = P.dma("sync", IN[32:48, 0:128], v_b1.rearrange("(oc p) -> oc p", p=128),
                   deps=[tk])
        tk = P.v(lambda e: e.tensor_copy(out=INB[0:48, 0:128], in_=IN[0:48, 0:128]),
                 deps=[tk])
        tk = P.tr(PT[:, 0:16], INB[0:16, 0:128], C_IDB[:], deps=[tk], inc=1)
        tk = P.v(lambda e: e.tensor_copy(out=B1K[:, :], in_=PT[:, 0:16]),
                 deps=[tk])
        tk = P.tr(PT[:, 1024:1040], INB[32:48, 0:128], C_IDB[:], deps=[tk], inc=1)
        tk = P.v(lambda e: e.tensor_copy(out=B1V[:, :], in_=PT[:, 1024:1040]),
                 deps=[tk])
        tk = P.dma("sync", B2K[:, 0:1], k_b2[:, None], deps=[tk])
        tk = P.dma("sync", B2V[:, 0:1], v_b2[:, None], deps=[tk])
        tk = P.dma("sync", IN[0:1, 128:256], k_b2[None, :], deps=[tk])
        tk = P.dma("sync", IN[0:1, 256:384], v_b2[None, :], deps=[tk])
        tk = P.v(lambda e: e.tensor_copy(out=B2KR[0:1, :], in_=IN[0:1, 128:256]),
                 deps=[tk])
        tk = P.v(lambda e: e.tensor_copy(out=B2VR[0:1, :], in_=IN[0:1, 256:384]),
                 deps=[tk])
        tk = P.dma("sync", IN[:, 512:512 + KVH],
                   mem_kv[0].transpose([2, 1, 0]).squeeze(1), deps=[tk],
                   allow_slow_non_contiguous=True)
        tk = P.dma("sync", IN[0:1, 1024:1024 + KVH * 128],
                   mem_kv[1].transpose([1, 0, 2]).rearrange(
                       "one h d -> one (h d)"), deps=[tk])
        tk = P.v(lambda e: e.tensor_copy(
            out=CK[:, :, 128:129].squeeze(2),
            in_=IN[:, 512:512 + KVH]), deps=[tk])
        tk = P.v(lambda e: e.tensor_copy(
            out=MEMV[0:1, :, :].rearrange("one h d -> one (h d)"),
            in_=IN[0:1, 1024:1024 + KVH * 128]), deps=[tk])

    def phase_ln(x_src):
        bb = P.bar()
        XT = [BIGA[:, 0:8192].bitcast(F32), BIGA[:, 8192:16384].bitcast(F32)]
        APL = [BIGA[:, 16384:20480], BIGA[:, 20480:24576]]
        STA = [BIGA[:, 24576:28672], BIGA[:, 28672:32768]]
        TRASH = PH.bitcast(F32)[:, 0:4096]
        tok = {}
        prev_ev = [None, None]
        for qt in range(NQT):
            bf = qt % 2
            c0, c1, c2, c3, c4, c5, c6 = [8 * bf + j for j in range(7)]
            t_in = P.dma("sync", XT[bf], x_src[qt * 128:(qt + 1) * 128, :],
                         deps=bb + [tok.get(("apl", qt - 2))])
            t_s1 = P.act(TRASH, XT[bf], AF.Identity, accum_out=SMALL[:, c0:c0 + 1],
                         deps=[t_in, tok.get(("sm", qt - 2))] + bb, inc=1)
            t_s2 = P.act(TRASH, XT[bf], AF.Square, accum_out=SMALL[:, c1:c1 + 1],
                         deps=[t_s1], inc=1)
            t_v1 = P.v(lambda e, c0=c0, c2=c2: e.tensor_scalar(
                out=SMALL[:, c2:c2 + 1], in0=SMALL[:, c0:c0 + 1], scalar1=-1.0 / D,
                scalar2=None, op0=OP.mult), deps=[t_s2] + bb)
            t_v2 = P.v(lambda e, c2=c2, c3=c3: e.tensor_tensor(
                out=SMALL[:, c3:c3 + 1], in0=SMALL[:, c2:c2 + 1],
                in1=SMALL[:, c2:c2 + 1], op=OP.mult), deps=[t_v1])
            t_v3 = P.v(lambda e, c1=c1, c3=c3, c4=c4: e.scalar_tensor_tensor(
                out=SMALL[:, c4:c4 + 1], in0=SMALL[:, c1:c1 + 1], scalar=1.0 / D,
                in1=SMALL[:, c3:c3 + 1], op0=OP.mult, op1=OP.subtract),
                deps=[t_v2])
            t_sq = P.act(SMALL[:, c5:c5 + 1], SMALL[:, c4:c4 + 1], AF.Sqrt,
                         bias=CONSTV[:, 0:1], deps=[t_v3], inc=1)
            t_rs = P.v(lambda e, c5=c5, c6=c6: e.reciprocal(
                out=SMALL[:, c6:c6 + 1], in_=SMALL[:, c5:c5 + 1]), deps=[t_sq])
            tok[("sm", qt)] = t_rs
            t_h = P.v(lambda e, bf=bf, c2=c2, c6=c6: e.tensor_scalar(
                out=APL[bf], in0=XT[bf], scalar1=SMALL[:, c2:c2 + 1],
                scalar2=SMALL[:, c6:c6 + 1], op0=OP.add, op1=OP.mult),
                deps=[t_rs, tok.get(("tr", qt - 2))])
            tok[("apl", qt)] = t_h
            t_pe = None
            ev_toks = []
            for grp in range(4):
                half = grp % 2
                pt = PT[:, half * 1024:half * 1024 + 1024]
                for j in range(8):
                    ic = grp * 8 + j
                    t_pe = P.tr(pt[:, j * 128:(j + 1) * 128],
                                APL[bf][:, ic * 128:(ic + 1) * 128], C_IDB[:],
                                deps=([t_h, prev_ev[half]] if j == 0 else []),
                                inc=(1 if j == 7 else 0))
                t_ev = P.v(lambda e, pt=pt, grp=grp, bf=bf: e.tensor_copy(
                    out=STA[bf][:, grp * 1024:(grp + 1) * 1024], in_=pt[:, :]),
                    deps=[t_pe, tok.get(("st", qt - 2))])
                prev_ev[half] = t_ev
                ev_toks.append(t_ev)
            tok[("tr", qt)] = t_pe
            t_st = P.dma("gpsimd", h_scr[qt], STA[bf], deps=[ev_toks[-1]])
            tok[("st", qt)] = t_st

    def phase_qkv(is_a):
        bb = P.bar()
        H_Q = BIGA[:, 16384:32768]
        WST = [PH[:, 0:4096], PH[:, 4096:8192]]
        KQ = PH[:, 8192:12288]
        VQ = PH[:, 12288:16384]
        t_h = None
        t_wst = [None, None]
        t_kv_done = bb
        ev_tok = {}
        pa_free = [None, None]
        for tq in range(4):
            t_h = P.dma("sync", sb3(H_Q[:, :], 4),
                        h_scr[tq * 4:(tq + 1) * 4].transpose([1, 0, 2]),
                        deps=t_kv_done + [t_h])
            H_Qv = H_Q.rearrange("p (qtl ic t) -> p qtl ic t", qtl=4, ic=NIC)
            for oc in range(48):
                buf = oc % 2
                if is_a:
                    t_wst[buf] = P.dma("sync", WST[buf][:, :], wqkvT[oc],
                                       deps=[ev_tok.get(oc - 2), t_wst[buf]])
                else:
                    qsrcw, qoc = ((qb_q, oc) if oc < 32 else
                                  (qb_k, oc - 32) if oc < 40 else (qb_v, oc - 40))
                    t_wst[buf] = P.dma(
                        "sync", sb3(WST[buf][:, :], NIC),
                        qsrcw.rearrange("(ic p) o -> p ic o", p=128)[
                            :, :, qoc * 128:(qoc + 1) * 128],
                        deps=[ev_tok.get(oc - 2), t_wst[buf]])
                psl = PA[:, buf * 512:buf * 512 + 512]
                t_pe = None
                for ic in range(NIC):
                    t_pe = P.mm(psl, WST[buf][:, ic * 128:(ic + 1) * 128],
                                H_Qv[:, :, ic, :],
                                start=(ic == 0), stop=(ic == NIC - 1),
                                deps=([t_wst[buf], t_h, pa_free[buf]]
                                      if ic == 0 else []),
                                inc=(1 if ic == NIC - 1 else 0))
                if oc < 32:
                    qs = STG[:, buf * 512:buf * 512 + 512]
                    t_ev = P.v(lambda e, psl=psl, qs=qs: e.tensor_scalar(
                        out=qs, in0=psl, scalar1=float(SCALE),
                        scalar2=None, op0=OP.mult),
                        deps=[t_pe, ev_tok.get(("st", oc - 2))])
                    t_st = P.dma("gpsimd", q_scr[oc, :, tq * 512:(tq + 1) * 512],
                                 qs, deps=[t_ev])
                    ev_tok[("st", oc)] = t_st
                elif oc < 40:
                    kvh = oc - 32
                    dst = (KQ[:, kvh * 512:(kvh + 1) * 512] if is_a
                           else RK[:, kvh * 2048 + tq * 512:kvh * 2048 + tq * 512 + 512])
                    t_ev = P.v(lambda e, psl=psl, dst=dst: e.tensor_copy(
                        out=dst, in_=psl), deps=[t_pe])
                else:
                    kvh = oc - 40
                    t_ev = P.v(lambda e, psl=psl, kvh=kvh: e.tensor_copy(
                        out=VQ[:, kvh * 512:(kvh + 1) * 512], in_=psl),
                        deps=[t_pe, ev_tok.get(("vt", kvh, tq))])
                ev_tok[oc] = t_ev
                pa_free[buf] = t_ev
            prev_pt = [ev_tok.get(("gsg", tq - 1)), ev_tok.get(("gsg", tq - 1))]
            for kvh in range(KVH):
                t_evs = []
                for j in range(4):
                    half = j % 2
                    pt = PT[:, half * 1024:half * 1024 + 128]
                    t_pe = P.tr(pt, VQ[:, kvh * 512 + j * 128:kvh * 512 + (j + 1) * 128],
                                C_IDB[:], deps=[ev_tok[40 + kvh], prev_pt[half]],
                                inc=1)
                    t_ev = P.v(lambda e, pt=pt, j=j: e.tensor_copy(
                        out=STG2[:, j * 128:(j + 1) * 128], in_=pt),
                        deps=[t_pe, ev_tok.get(("vs_last",))])
                    prev_pt[half] = t_ev
                    t_evs.append(t_ev)
                t_st = P.dma("gpsimd",
                             v_scr[:, kvh, tq * 4:(tq + 1) * 4, :],
                             sb3(STG2[:, 0:512], 4), deps=[t_evs[-1]])
                ev_tok[("vs_last",)] = t_st
                ev_tok[("vt", kvh, tq)] = t_evs[-1]
            if is_a and 'gates' not in debug_no:
                t_pe = None
                for ic in range(NIC):
                    t_pe = P.mm(PB[0:96, 0:512], WGT[:, ic * 96:(ic + 1) * 96],
                                H_Qv[:, :, ic, :],
                                start=(ic == 0), stop=False,
                                deps=([t_h, ev_tok.get(("gev", tq - 1))]
                                      if ic == 0 else []))
                t_pe = P.mm(PB[0:96, 0:512], BGATE[0:1, 0:96], C_ONES[0:1, 0:512],
                            start=False, stop=True, deps=[], inc=1)
                t_gev = P.v(lambda e: e.tensor_copy(out=STG[0:96, 1024:1536],
                                                    in_=PB[0:96, 0:512]),
                            deps=[t_pe, ev_tok.get(("gsg", tq - 1))])
                ev_tok[("gev", tq)] = t_gev
                t_sg = None
                for j in range(4):
                    half = j % 2
                    pt = PT[:, half * 1024 + 256:half * 1024 + 256 + 96]
                    t_pe = P.tr(pt, STG[0:96, 1024 + j * 128:1024 + (j + 1) * 128],
                                C_IDB[:], deps=[t_gev, ev_tok.get(("gsg2", j - 2, tq))],
                                inc=1)
                    qt = tq * 4 + j
                    t_sg = P.act(G_TM[:, qt * 96:(qt + 1) * 96], pt, AF.Sigmoid,
                                 deps=[t_pe], inc=1)
                    ev_tok[("gsg2", j, tq)] = t_sg
                ev_tok[("gsg", tq)] = t_sg
            if is_a and 'kvprep' not in debug_no:
                t_kv_done = kvprep_quarter(tq, KQ, VQ, ev_tok)
            else:
                t_kv_done = P.bar()
        return P.bar()

    def kvprep_quarter(tq, KQ, VQ, ev_tok):
        kv_ready = [ev_tok[32 + j] for j in range(16)]
        entry_pe = ("pe", P.count["pe"])
        RL = BIGA[:, 16384:20480]
        WB = [BIGA[:, 20480:22528], BIGA[:, 22528:24576]]
        W2B = BIGA[:, 24576:26624]
        t_rope = []
        for kvh in (range(KVH) if "rope" not in DBG_NO else []):
            ksl = KQ[:, kvh * 512:(kvh + 1) * 512]
            t_pe = P.mm(PA[:, 1024:1536], C_PERM[:], ksl, start=True, stop=True,
                        deps=kv_ready + [ev_tok.get(("rp", kvh - 1))], inc=1)
            t_a = P.v(lambda e, ksl=ksl: e.tensor_tensor(
                out=STG[:, 1536:2048], in0=ksl,
                in1=C_RC[:, tq * 512:(tq + 1) * 512], op=OP.mult),
                deps=kv_ready + [ev_tok.get(("rp", kvh - 1))])
            t_b = P.v(lambda e: e.scalar_tensor_tensor(
                out=FWORK[:, 0:512], in0=PA[:, 1024:1536], scalar=1.0,
                in1=C_RS[:, tq * 512:(tq + 1) * 512], op0=OP.mult, op1=OP.mult),
                deps=[t_pe, t_a])
            t_rk = P.v(lambda e, kvh=kvh: e.tensor_tensor(
                out=RK[:, kvh * 2048 + tq * 512:kvh * 2048 + tq * 512 + 512],
                in0=STG[:, 1536:2048], in1=FWORK[:, 0:512], op=OP.add),
                deps=[t_b])
            ev_tok[("rp", kvh)] = t_rk
            t_rope.append(t_rk)
        t_last = t_rope[-1] if t_rope else P.bar()[0] and None
        mlp_list = [] if "mlp" in DBG_NO else [
                (KQ, w1kT, w2kT, B1K, B2K, B2KR, KPF, CK),
                (VQ, w1vT, w2vT, B1V, B2V, B2VR, VPF, CVF)]
        for (SRC, W1T, W2T, B1, B2c, B2r, PF, CDST) in mlp_list:
            t_w = [None, None]
            t_rl = None
            for oc in range(16):
                buf = oc % 2
                t_w[buf] = P.dma("sync", WB[buf][:, :], W1T[oc],
                                 deps=[t_w[buf], ev_tok.get(("rlw", oc - 2)),
                                       ev_tok.get(("rlw_carry",)) if oc < 2 else None,
                                       entry_pe if oc < 2 else None])
                pb0 = (oc % 2) * 512
                t_pe = None
                for ic in range(16):
                    rhs_ic = SRC.rearrange("p (h t) -> p h t", h=KVH)[:, :, ic::16]
                    t_pe = P.mm(PA[:, pb0:pb0 + 256],
                                WB[buf][:, ic * 128:(ic + 1) * 128],
                                rhs_ic, start=(ic == 0), stop=False,
                                deps=([t_w[buf], t_last,
                                       ev_tok.get(("rlv", oc - 2))] if ic == 0 else []))
                    t_pe = P.mm(PA[:, pb0 + 256:pb0 + 264],
                                WB[buf][:, ic * 128:(ic + 1) * 128],
                                PF[:, ic * 8:(ic + 1) * 8],
                                start=False, stop=(ic == 15),
                                deps=[], inc=(1 if ic == 15 else 0))
                t_bv = P.v(lambda e, oc=oc, B1=B1, pb0=pb0: e.tensor_scalar(
                    out=SMALL[:, 8 + (oc % 2) * 96:16 + (oc % 2) * 96],
                    in0=PA[:, pb0 + 256:pb0 + 264],
                    scalar1=B1[:, oc:oc + 1], scalar2=None, op0=OP.add),
                    deps=[t_pe, ev_tok.get(("bvu", oc - 2))])
                for h8 in range(KVH):
                    t_rl = P.v(lambda e, oc=oc, h8=h8, pb0=pb0: e.tensor_scalar(
                        out=RL[:, oc * 256 + h8 * 32:oc * 256 + (h8 + 1) * 32],
                        in0=PA[:, pb0 + h8 * 32:pb0 + (h8 + 1) * 32],
                        scalar1=SMALL[:, 8 + (oc % 2) * 96 + h8:
                                      9 + (oc % 2) * 96 + h8], scalar2=0.0,
                        op0=OP.add, op1=OP.max), deps=[t_bv])
                ev_tok[("bvu", oc)] = t_rl
                ev_tok[("rlv", oc)] = t_rl
                ev_tok[("rlw", oc)] = t_rl
                if oc >= 14:
                    ev_tok[("rlw_carry",)] = t_rl
            t_w2 = P.dma("sync", W2B[:, :], W2T,
                         deps=[ev_tok.get(("w2last",)), entry_pe])
            t_pe = None
            for oc in range(16):
                t_pe = P.mm(PC[:, 0:256], W2B[:, oc * 128:(oc + 1) * 128],
                            RL[:, oc * 256:(oc + 1) * 256],
                            start=(oc == 0), stop=(oc == 15),
                            deps=([t_w2, t_rl, ev_tok.get(("cklast",))]
                                  if oc == 0 else []),
                            inc=(1 if oc == 15 else 0))
            t_ck = P.v(lambda e, CDST=CDST, B2c=B2c: e.tensor_scalar(
                out=CDST[:, :, tq * 32:(tq + 1) * 32],
                in0=PC[:, 0:256].rearrange("p (h c) -> p h c", h=KVH),
                scalar1=B2c[:, 0:1], scalar2=None,
                op0=OP.add), deps=[t_pe])
            ev_tok[("w2last",)] = t_ck
            ev_tok[("cklast",)] = t_ck
            t_last = t_ck
        return P.bar()

    def phase_attn(is_a):
        bb = P.bar()
        Q4 = PH[:, 0:8192]
        RQ4 = PH[:, 8192:16384]
        TS = [STG[:, 0:1024], STG[:, 1024:2048]]
        AOST = [STG[:, 2048:2560], STG[:, 2560:3072]]
        TSM = STG[:, 3072:3328]
        pts = {0: None, 1: None}
        tsa = {0: None, 1: None}
        st = {}

        t_vkm = P.dma("sync",
                      VKM.rearrange("p (kvh ch dh) -> p kvh ch dh", kvh=KVH, ch=16),
                      v_scr, deps=bb)
        t_cvk = None
        if is_a:
            for kvh in range(KVH):
                bank = kvh % 2
                pt = PT[:, bank * 1024:bank * 1024 + 128]
                t_pe = P.tr(pt, CVF[:, kvh, :], C_IDB[:],
                            deps=bb + [pts[bank]], inc=1)
                t_ev = P.v(lambda e, pt=pt, kvh=kvh: e.tensor_copy(
                    out=CVK[:, kvh, :], in_=pt), deps=[t_pe])
                pts[bank] = t_ev
            t_cvk = pts[1]

        def rownorm_max(src_bf, n, out_col, deps):
            t_sq = P.act(STG2[:, 0:n], src_bf, AF.Square,
                         deps=deps + [st.get(("nrow",))], inc=1)
            t_mm = None
            for j in range((n + 511) // 512):
                n0, n1 = j * 512, min(n, j * 512 + 512)
                t_mm = P.mm(PA[0:1, n0:n1], C_ONECOL[:], STG2[:, n0:n1],
                            start=True, stop=True,
                            deps=([t_sq, st.get(("nrow",)), st.get(("mkb",))]
                                  if j == 0 else []),
                            inc=(1 if n1 == n else 0))
            t_mx = P.v(lambda e, n=n: e.max(out=SMALL[0:1, 64:72],
                                            in_=PA[0:1, 0:n]),
                       deps=[t_mm, st.get(("ncp",))])
            st[("nrow",)] = t_mx
            t_cp = P.v(lambda e, out_col=out_col: e.tensor_copy(
                out=SMALL[0:1, out_col:out_col + 1], in_=SMALL[0:1, 64:65]),
                deps=[t_mx])
            st[("ncp",)] = t_cp
            return t_cp

        t_norm = []
        for kvh in range(KVH):
            t_norm.append(rownorm_max(RK[:, kvh * 2048:(kvh + 1) * 2048], 2048,
                                      76 + kvh, bb))
            if is_a:
                t_norm.append(rownorm_max(CK[:, kvh, 0:129], 129, 84 + kvh,
                                          bb + [t_cvk]))

        def make_bias(qcol, kcol, outcol, deps):
            t_p = P.v(lambda e: e.tensor_tensor(
                out=SMALL[0:1, 72:73], in0=SMALL[0:1, qcol:qcol + 1],
                in1=SMALL[0:1, kcol:kcol + 1], op=OP.mult),
                deps=deps + [st.get(("mkb",))])
            t_s = P.act(SMALL[0:1, 73:74], SMALL[0:1, 72:73], AF.Sqrt,
                        deps=[t_p], inc=1)
            t_n = P.v(lambda e: e.tensor_scalar(
                out=SMALL[0:1, 74:75], in0=SMALL[0:1, 73:74], scalar1=-1.02,
                scalar2=None, op0=OP.mult), deps=[t_s])
            t_c = P.v(lambda e: e.tensor_copy(out=STG2[0:1, 4000:4001],
                                              in_=SMALL[0:1, 74:75]), deps=[t_n])
            t_bm = P.mm(PA[:, 1032:1040][:, 0:1], C_ONES[0:1, 0:128],
                        STG2[0:1, 4000:4001], start=True, stop=True,
                        deps=[t_c, st.get(("mkbm",)), st.get(("ropeprev",))],
                        inc=1)
            t_e = P.v(lambda e, outcol=outcol: e.tensor_copy(
                out=SMALL[:, outcol:outcol + 1], in_=PA[:, 1032:1033]),
                deps=[t_bm])
            st[("mkb",)] = t_e
            st[("mkbm",)] = t_e
            return t_e

        t_q = None
        for kvh in range(KVH):
            t_q = P.dma("sync", sb3(Q4, 4),
                        pmaj(q_scr[kvh * 4:(kvh + 1) * 4]),
                        deps=bb + [t_q, st.get(("qfree", kvh - 1))])
            t_qn = rownorm_max(Q4[:, 0:2048], 2048, 75, [t_q])
            for j in range(1, 4):
                t_qn2 = rownorm_max(Q4[:, j * 2048:(j + 1) * 2048], 2048, 64, [t_q])
                t_qn = P.v(lambda e: e.tensor_tensor(
                    out=SMALL[0:1, 75:76], in0=SMALL[0:1, 75:76],
                    in1=SMALL[0:1, 64:65], op=OP.max), deps=[t_qn, t_qn2])
                st[("ncp",)] = t_qn
            t_bf = make_bias(75, 76 + kvh, 100, [t_qn] + t_norm[-1:])
            t_bc = make_bias(75, 84 + kvh, 101, [t_qn]) if is_a else None
            t_rq = [None] * 4
            if is_a:
                for g in range(4):
                    qsl = Q4[:, g * 2048:(g + 1) * 2048]
                    pab = 1024 + (g % 2) * 512
                    t_b = None
                    for j4 in range(4):
                        t_pe4 = P.mm(
                            PA[:, pab:pab + 512], C_PERM[:],
                            qsl[:, j4 * 512:(j4 + 1) * 512], start=True, stop=True,
                            deps=([t_q, st.get(("ropeb", kvh, g - 2)), t_bf, t_bc]
                                  if j4 == 0 else [t_b]),
                            inc=1)
                        t_b = P.v(lambda e, pab=pab, j4=j4, g=g: e.scalar_tensor_tensor(
                            out=FWORK[:, g % 2 * 2048 + j4 * 512:
                                      g % 2 * 2048 + (j4 + 1) * 512],
                            in0=PA[:, pab:pab + 512], scalar=1.0,
                            in1=C_RS[:, j4 * 512:(j4 + 1) * 512],
                            op0=OP.mult, op1=OP.mult),
                            deps=[t_pe4, st.get(("ropec", kvh, g - 2))])
                        st[("ropeb", kvh, g)] = t_b
                    t_a = P.v(lambda e, qsl=qsl, g=g: e.tensor_tensor(
                        out=STG2[:, g % 2 * 2048:g % 2 * 2048 + 2048], in0=qsl,
                        in1=C_RC[:, :], op=OP.mult),
                        deps=[t_q, st.get(("roped", kvh, g - 2)), t_bf])
                    t_rq[g] = P.v(lambda e, g=g: e.tensor_tensor(
                        out=RQ4[:, g * 2048:(g + 1) * 2048],
                        in0=STG2[:, g % 2 * 2048:g % 2 * 2048 + 2048],
                        in1=FWORK[:, g % 2 * 2048:g % 2 * 2048 + 2048],
                        op=OP.add), deps=[t_a, st.get(("ropeb", kvh, g))])
                    st[("ropec", kvh, g)] = t_rq[g]
                    st[("roped", kvh, g)] = t_rq[g]
            else:
                t_rq = [t_q] * 4
            st[("prologue", kvh)] = ("dve", P.count["dve"])
            st[("ropeprev",)] = ("dve", P.count["dve"])

            for qt in range(NQT):
                Cn = qt + 1
                Nk = Cn * 128
                half_ok = Nk <= 1024
                t_pb_free = st.get(("aoev", kvh, qt - 1)) or \
                    st.get(("aoev", kvh - 1, NQT - 1))
                n_sl_chunks = (qt + 1 - max(0, qt - 1))
                if is_a:
                    av_per_g = ((0 if "av_c" in DBG_NO else 2) +
                                (0 if "av_f" in DBG_NO else Cn) +
                                (0 if "av_s" in DBG_NO else n_sl_chunks))
                else:
                    av_per_g = Cn
                av_state = {"i": 0, "n": 4 * av_per_g}

                def av_mm(g, lhsT, rhs, deps=(), inc=0, stop=False):
                    s0 = av_state["i"] == 0
                    last = av_state["i"] == av_state["n"] - 1
                    av_state["i"] += 1
                    return P.mm(PB[:, g * 128:(g + 1) * 128], lhsT, rhs,
                                start=s0, stop=last,
                                deps=list(deps) + ([t_pb_free] if s0 else []),
                                inc=inc)

                def transpose_group(srcs, lhsTs, g, dep_first, extra_av_deps=()):
                    bank = st.get(("tgb",), 0)
                    st[("tgb",)] = 1 - bank
                    pt = PT[:, bank * 1024:bank * 1024 + 1024]
                    t_p = None
                    for j, ap in enumerate(srcs):
                        t_p = P.tr(pt[:, j * 128:(j + 1) * 128], ap, C_IDB[:],
                                   deps=([dep_first, pts[bank]] if j == 0 else []),
                                   inc=(1 if j == len(srcs) - 1 else 0))
                    t_e = P.v(lambda e, pt=pt, n=len(srcs): e.tensor_copy(
                        out=TS[bank][:, 0:n * 128], in_=pt[:, 0:n * 128]),
                        deps=[t_p, tsa[bank]])
                    pts[bank] = t_e
                    t_av = None
                    for j, lh in enumerate(lhsTs):
                        t_av = av_mm(g, lh, TS[bank][:, j * 128:(j + 1) * 128],
                                     deps=([t_e] + list(extra_av_deps)
                                           if j == 0 else []),
                                     inc=(1 if j == len(lhsTs) - 1 else 0))
                    tsa[bank] = t_av
                    return t_av

                t_ec = [None] * 4
                if is_a:
                    t_sc = [None] * 4
                    for g in range(4):
                        lq = Q4[:, g * 2048 + qt * 128:g * 2048 + qt * 128 + 128]
                        csl = PA[:, g * 512:g * 512 + 512]
                        t_pe = P.mm(csl[:, 0:129], lq, CK[:, kvh, 0:129],
                                    start=True, stop=False,
                                    deps=[t_q, t_bc,
                                          st.get(("paf", kvh, qt - 1, 3)),
                                          (st.get(("prologue", kvh))
                                           if qt == 0 else None)])
                        t_pe = P.mm(csl[:, 0:129], C_IDB[:],
                                    C_CM[:, qt * 132:qt * 132 + 129],
                                    start=False, stop=True, deps=[], inc=1)
                        t_ec[g] = P.act(PHC[:, g * 132:g * 132 + 129],
                                        csl[:, 0:129], AF.Exp,
                                        bias=SMALL[:, 101:102], _dep_bf=t_bc,
                                        accum_out=SMALL[:, 20 + 3 * g:21 + 3 * g],
                                        deps=[t_pe, st.get(("phc", kvh, qt - 1, g))],
                                        inc=1)
                        t_sc[g] = P.v(lambda e, csl=csl, g=g: e.tensor_copy(
                            out=SC4[:, g * 132:g * 132 + 129], in_=csl[:, 0:129]),
                            deps=[t_pe, t_ec[g], st.get(("sc4", kvh, qt - 1, g))])
                        st[("sc4", kvh, qt, g)] = t_ec[g]
                        st[("pacons", kvh, qt, g)] = t_ec[g]
                    t_i1 = P.v(lambda e: e.tensor_tensor(
                        out=FWORK[:, 0:128], in0=SC4[:, 0:128], in1=SC4[:, 132:260],
                        op=OP.add), deps=[t_sc[0], t_sc[1]])
                    t_i2 = P.v(lambda e: e.tensor_tensor(
                        out=FWORK[:, 128:256], in0=SC4[:, 264:392],
                        in1=SC4[:, 396:524], op=OP.add), deps=[t_sc[2], t_sc[3]])
                    t_imp = P.v(lambda e: e.tensor_tensor(
                        out=FWORK[:, 0:128], in0=FWORK[:, 0:128],
                        in1=FWORK[:, 128:256], op=OP.add), deps=[t_i1, t_i2])
                    t_top = P.v(lambda e: e.max(out=SMALL[:, 32:40],
                                                in_=FWORK[:, 0:128]), deps=[t_imp])
                    t_c1 = P.v(lambda e: e.tensor_scalar(
                        out=FWORK[:, 256:384], in0=FWORK[:, 0:128],
                        scalar1=SMALL[:, 35:36], scalar2=None, op0=OP.is_ge),
                        deps=[t_top])
                    t_nm = P.v(lambda e: e.tensor_reduce(
                        out=SMALL[:, 40:41], in_=FWORK[:, 0:128], axis=AX.X,
                        op=OP.max), deps=[t_imp])
                    t_nm2 = P.v(lambda e: e.tensor_scalar(
                        out=SMALL[:, 41:42], in0=SMALL[:, 40:41], scalar1=-0.25,
                        scalar2=1000.0, op0=OP.mult, op1=OP.min), deps=[t_nm])
                    t_ei = P.act(FWORK[:, 384:512], FWORK[:, 0:128], AF.Exp,
                                 scale=0.25, bias=SMALL[:, 41:42],
                                 accum_out=SMALL[:, 42:43], deps=[t_nm2], inc=1)
                    t_sent = P.act(SMALL[:, 43:44], SMALL[:, 41:42], AF.Exp,
                                   bias=CONSTV[:, 1:2], deps=[t_ei], inc=1)
                    t_z = P.v(lambda e: e.tensor_tensor(
                        out=SMALL[:, 44:45], in0=SMALL[:, 42:43],
                        in1=SMALL[:, 43:44], op=OP.add), deps=[t_sent])
                    t_thr = P.v(lambda e: e.tensor_scalar(
                        out=SMALL[:, 45:46], in0=SMALL[:, 44:45], scalar1=1e-10,
                        scalar2=None, op0=OP.mult), deps=[t_z])
                    t_am = P.v(lambda e: e.scalar_tensor_tensor(
                        out=FWORK[:, 512:640], in0=FWORK[:, 384:512],
                        scalar=SMALL[:, 45:46], in1=FWORK[:, 256:384],
                        op0=OP.is_gt, op1=OP.mult), deps=[t_thr, t_c1])
                    t_amb = P.v(lambda e: e.tensor_scalar(
                        out=STG[:, 3328:3456], in0=FWORK[:, 512:640],
                        scalar1=-1.0, scalar2=None, op0=OP.add),
                        deps=[t_am, st.get(("amt_used", kvh, qt - 1))])
                    bank = st.get(("tgb",), 0)
                    st[("tgb",)] = 1 - bank
                    t_pe = P.tr(PT[:, bank * 1024:bank * 1024 + 128],
                                STG[:, 3328:3456], C_IDB[:],
                                deps=[t_amb, pts[bank]], inc=1)
                    t_amt = P.v(lambda e, bank=bank: e.tensor_copy(
                        out=STG[:, 3456:3584],
                        in_=PT[:, bank * 1024:bank * 1024 + 128]),
                        deps=[t_pe, st.get(("amt_used", kvh, qt - 1))])
                    pts[bank] = t_amt

                gtok = {}

                def emit_scores(g):
                    qsrc = RQ4 if is_a else Q4
                    lq = qsrc[:, g * 2048 + qt * 128:g * 2048 + qt * 128 + 128]
                    if is_a:
                        c0 = max(0, qt - 1)
                        nsl = (qt + 1 - c0) * 128
                        psb = g % 2
                        phs = PHS[:, psb * 256:psb * 256 + nsl]
                        t_ps = P.mm(PC[:, 0:nsl], lq,
                                    RK[:, kvh * 2048 + c0 * 128:
                                       kvh * 2048 + (qt + 1) * 128],
                                    start=True, stop=False,
                                    deps=[t_rq[g],
                                          st.get(("pcs", kvh, qt, g - 1)) or
                                          st.get(("pcs", kvh, qt - 1, 3))])
                        t_ps = P.mm(PC[:, 0:nsl], C_IDB[:],
                                    C_BAND[:, 256 - nsl:256], start=False,
                                    stop=True, deps=[], inc=1)
                        t_es = P.act(phs, PC[:, 0:nsl], AF.Exp,
                                     bias=SMALL[:, 100:101], _dep_bf=t_bf,
                                     accum_out=SMALL[:, 22 + 3 * g:23 + 3 * g],
                                     deps=[t_ps, st.get(("phs_used", kvh, qt, g - 2))
                                           or st.get(("phs_used", kvh, qt - 1, 2 + g))],
                                     inc=1)
                        st[("pcs", kvh, qt, g)] = t_es
                        st[("esl", kvh, qt, g)] = t_es
                    pab = (g % 2) * 1024
                    nhalf = 1 if Nk <= 1024 else 2
                    jod_abs = qt // 4
                    phb = g % 2
                    t_efh = None
                    for h2 in range(nhalf):
                        m0 = h2 * 1024
                        m1 = min(Nk, m0 + 1024)
                        PAg = PA[:, pab:pab + (m1 - m0)]
                        nblk = (m1 - m0 + 511) // 512
                        dep0 = [t_rq[g], t_vkm,
                                (st.get(("paf", kvh, qt, g - 2)) or
                                 st.get(("paf", kvh, qt - 1, 2 + g))),
                                (t_ec[g] if is_a else None),
                                (st.get(("pacons", kvh, qt, 3)) if is_a else None),
                                (st.get(("prologue", kvh)) if qt == 0 else None),
                                st.get(("pafh", kvh, qt, g))]
                        has_od = (m0 <= qt * 128 < m1)
                        t_pe = None
                        for j in range(nblk):
                            n0 = j * 512
                            n1 = min(m1 - m0, n0 + 512)
                            jabs = (m0 + n0) // 512
                            last_mm = (not is_a) and (j == nblk - 1) and not has_od
                            t_pe = P.mm(PAg[:, n0:n1], lq,
                                        RK[:, kvh * 2048 + m0 + n0:
                                           kvh * 2048 + m0 + n1],
                                        start=True,
                                        stop=(not is_a and jabs != jod_abs),
                                        deps=(dep0 if j == 0 else []),
                                        inc=(1 if last_mm else 0))
                        if is_a:
                            for j in range(nblk):
                                n0 = j * 512
                                n1 = min(m1 - m0, n0 + 512)
                                jabs = (m0 + n0) // 512
                                last_mm = (j == nblk - 1) and not has_od
                                t_pe = P.mm(PAg[:, n0:n1], STG[:, 3456:3584],
                                            C_E[:, m0 + n0:m0 + n1], start=False,
                                            stop=(jabs != jod_abs),
                                            deps=([t_amt] if j == 0 and h2 == 0
                                                  else []),
                                            inc=(1 if last_mm else 0))
                        if has_od:
                            od0 = qt * 128 - m0
                            t_pe = P.mm(PAg[:, od0:od0 + 128], C_IDB[:],
                                        C_ODC[:] if is_a else C_TRI[:],
                                        start=False, stop=True, deps=[],
                                        inc=1)
                            if is_a:
                                st[("amt_used", kvh, qt)] = t_pe
                        pha_h = PHAT[:, phb * 2048 + m0:phb * 2048 + m1]
                        zcol = (21 + 3 * g) if is_a else (49 + g % 2)
                        if nhalf == 2 and h2 == 0:
                            zcol = 102 + g % 2
                        t_efh = P.act(pha_h, PAg, AF.Exp,
                                      bias=SMALL[:, 100:101],
                                      accum_out=SMALL[:, zcol:zcol + 1],
                                      deps=[t_pe, t_bf,
                                            st.get(("phat_used", kvh, qt, g - 2))
                                            or st.get(("phat_used", kvh, qt - 1,
                                                       2 + g)),
                                            ((st.get(("zpart", kvh, qt, g - 2)) or
                                              st.get(("zpart", kvh, qt - 1, 2 + g)))
                                             if (nhalf == 2 and h2 == 0)
                                             else None)],
                                      inc=1)
                        st[("pafh", kvh, qt, g)] = t_efh
                    if nhalf == 2:
                        zf = (21 + 3 * g) if is_a else (49 + g % 2)
                        t_ef = P.v(lambda e, zf=zf, g=g: e.tensor_tensor(
                            out=SMALL[:, zf:zf + 1],
                            in0=SMALL[:, zf:zf + 1],
                            in1=SMALL[:, 102 + g % 2:103 + g % 2], op=OP.add),
                            deps=[t_efh])
                        st[("zpart", kvh, qt, g)] = t_ef
                    else:
                        t_ef = t_efh
                    st[("paf", kvh, qt, g)] = t_efh
                    pha = PHAT[:, phb * 2048:phb * 2048 + Nk]
                    if is_a:
                        h3 = (4 * kvh + g) * 3
                        rz0 = 60 + (g % 2) * 8
                        t_rz = P.v(lambda e, g=g, rz0=rz0: e.reciprocal(
                            out=SMALL[:, rz0:rz0 + 3],
                            in_=SMALL[:, 20 + 3 * g:23 + 3 * g]),
                            deps=[t_ec[g], t_ef, st[("esl", kvh, qt, g)]])
                        t_sc3 = P.v(lambda e, qt=qt, h3=h3, g=g, rz0=rz0:
                                    e.tensor_tensor(
                            out=SMALL[:, 56 + (g % 2) * 16:59 + (g % 2) * 16],
                            in0=SMALL[:, rz0:rz0 + 3],
                            in1=G_TM[:, qt * 96 + h3:qt * 96 + h3 + 3],
                            op=OP.mult), deps=[t_rz])
                        sc0 = 56 + (g % 2) * 16
                        t_pc = P.v(lambda e, g=g, sc0=sc0: e.tensor_scalar(
                            out=PHC[:, g * 132:g * 132 + 129],
                            in0=PHC[:, g * 132:g * 132 + 129],
                            scalar1=SMALL[:, sc0:sc0 + 1], scalar2=None,
                            op0=OP.mult), deps=[t_sc3])
                        st[("phc", kvh, qt, g)] = t_pc
                        t_pf = P.v(lambda e, pha=pha, sc0=sc0: e.tensor_scalar(
                            out=pha, in0=pha,
                            scalar1=SMALL[:, sc0 + 1:sc0 + 2], scalar2=None,
                            op0=OP.mult), deps=[t_sc3])
                        psb = g % 2
                        c0 = max(0, qt - 1)
                        nsl = (qt + 1 - c0) * 128
                        t_psc = P.v(lambda e, psb=psb, nsl=nsl, sc0=sc0:
                                    e.tensor_scalar(
                            out=PHS[:, psb * 256:psb * 256 + nsl],
                            in0=PHS[:, psb * 256:psb * 256 + nsl],
                            scalar1=SMALL[:, sc0 + 2:sc0 + 3], scalar2=None,
                            op0=OP.mult), deps=[t_sc3])
                    else:
                        t_rz = P.v(lambda e, g=g: e.reciprocal(
                            out=SMALL[:, 52 + (g % 2):53 + (g % 2)],
                            in_=SMALL[:, 49 + g % 2:50 + g % 2]),
                            deps=[t_ef, st.get(("rzu", kvh, qt, g - 2)) or
                                  st.get(("rzu", kvh, qt - 1, 2 + g))])
                        t_pf = P.v(lambda e, pha=pha, g=g: e.tensor_scalar(
                            out=pha, in0=pha,
                            scalar1=SMALL[:, 52 + (g % 2):53 + (g % 2)],
                            scalar2=None, op0=OP.mult), deps=[t_rz])
                        st[("rzu", kvh, qt, g)] = t_pf
                        t_pc = t_psc = None
                    gtok[g] = (t_pf, t_pc, t_psc)

                def emit_avs(g):
                    t_pf, t_pc, t_psc = gtok[g]
                    phb = g % 2
                    if is_a and "av_c" not in DBG_NO:
                        bank = st.get(("tgb",), 0)
                        st[("tgb",)] = 1 - bank
                        pt = PT[:, bank * 1024:bank * 1024 + 1024]
                        t_p1 = P.tr(pt[:, 0:128], PHC[:, g * 132:g * 132 + 128],
                                    C_IDB[:], deps=[t_pc, pts[bank]], inc=1)
                        t_p2 = P.tr(pt[0:1, 128:256],
                                    PHC[:, g * 132 + 128:g * 132 + 129],
                                    C_IDB[:], deps=[], inc=1)
                        t_e1 = P.v(lambda e, pt=pt: e.tensor_copy(
                            out=TS[bank][:, 0:128], in_=pt[:, 0:128]),
                            deps=[t_p2, tsa[bank]])
                        t_e2 = P.v(lambda e, pt=pt: e.tensor_copy(
                            out=TSM[0:1, 0:128], in_=pt[0:1, 128:256]),
                            deps=[t_p2, st.get(("tsm",))])
                        pts[bank] = t_e2
                        t_av = av_mm(g, CVK[:, kvh, :], TS[bank][:, 0:128],
                                     deps=[t_e1], inc=1)
                        tsa[bank] = t_av
                        t_av = av_mm(g, MEMV[0:1, kvh, :], TSM[0:1, 0:128],
                                     deps=[t_e2], inc=1)
                        st[("tsm",)] = t_av
                    if not (is_a and "av_f" in DBG_NO):
                        done = 0
                        t_last = None
                        while done < Cn:
                            grp = min(8, Cn - done)
                            srcs = [PHAT[:, phb * 2048 + (done + j) * 128:
                                         phb * 2048 + (done + j + 1) * 128]
                                    for j in range(grp)]
                            lhs = [VKM[:, (kvh * 16 + done + j) * 128:
                                       (kvh * 16 + done + j + 1) * 128]
                                   for j in range(grp)]
                            t_last = transpose_group(srcs, lhs, g, t_pf)
                            done += grp
                        st[("phat_used", kvh, qt, g)] = t_last
                    if is_a and "av_s" not in DBG_NO:
                        c0 = max(0, qt - 1)
                        nch = qt + 1 - c0
                        psb = g % 2
                        srcs = [PHS[:, psb * 256 + j * 128:psb * 256 + (j + 1) * 128]
                                for j in range(nch)]
                        lhs = [VKM[:, (kvh * 16 + c0 + j) * 128:
                                   (kvh * 16 + c0 + j + 1) * 128]
                               for j in range(nch)]
                        t_last = transpose_group(srcs, lhs, g, t_psc)
                        st[("phs_used", kvh, qt, g)] = t_last

                for g in range(4):
                    emit_scores(g)
                    if g >= 1:
                        emit_avs(g - 1)
                emit_avs(3)

                t_all_av = ("pe", P.count["pe"])
                abuf = qt % 2
                t_ev = P.v(lambda e, abuf=abuf: e.tensor_copy(
                    out=AOST[abuf], in_=PB[:, 0:512]),
                    deps=[t_all_av, st.get(("aodma", kvh, qt - 2)) or
                          st.get(("aodma", kvh - 1, NQT - 2 + qt))])
                t_st = P.dma("gpsimd",
                             ao_scr[qt, :, kvh * 4:(kvh + 1) * 4, :],
                             sb3(AOST[abuf], 4), deps=[t_ev])
                st[("aodma", kvh, qt)] = t_st
                st[("aoev", kvh, qt)] = t_ev
            st[("qfree", kvh)] = ("pe", P.count["pe"])
        return P.bar()

    def phase_oproj(w_kind, x_in, x_out):
        bb = P.bar()
        t_w = None
        AT = [STG, STG2]
        XT = [FWORK[:, 0:1024], FWORK[:, 1024:2048]]
        XO = [FWORK[:, 2048:3072], FWORK[:, 3072:4096]]
        t_at = [None, None]
        t_xt = [None, None]
        t_st = [None, None]
        t_evx = [None, None]
        for os_ in range(4):
            if w_kind == "woT":
                t_w = P.dma("sync", sb3(WSL, 32), pmaj(woT[os_]),
                            deps=bb + [t_w] + t_evx)
            else:
                t_w = P.dma("gpsimd", sb3(WSL, 32),
                            wo_b.rearrange("(hd p) o -> p hd o", p=128)[
                                :, :, os_ * 1024:(os_ + 1) * 1024],
                            deps=bb + [t_w] + t_evx)
            for qt in range(NQT):
                buf = qt % 2
                t_at[buf] = P.dma("sync", AT[buf][:, :],
                                  ao_scr[qt].rearrange("p hd t -> p (hd t)"),
                                  deps=bb + [t_at[buf], t_st[buf]])
                t_xt[buf] = P.dma("sync", XT[buf],
                                  x_in[qt * 128:(qt + 1) * 128,
                                       os_ * 1024:(os_ + 1) * 1024],
                                  deps=[t_xt[buf], t_evx[buf]])
                psl = PA[:, buf * 1024:buf * 1024 + 1024]
                t_pe = None
                for hd in range(32):
                    for j in range(2):
                        t_pe = P.mm(psl[:, j * 512:(j + 1) * 512],
                                    AT[buf][:, hd * 128:(hd + 1) * 128],
                                    WSL[:, hd * 1024 + j * 512:hd * 1024 + (j + 1) * 512],
                                    start=(hd == 0), stop=(hd == 31),
                                    deps=([t_at[buf], t_w, t_evx[buf]]
                                          if hd == 0 and j == 0 else []),
                                    inc=(1 if hd == 31 and j == 1 else 0))
                t_evx[buf] = P.v(lambda e, psl=psl, buf=buf: e.tensor_tensor(
                    out=XO[buf], in0=psl, in1=XT[buf], op=OP.add),
                    deps=[t_pe, t_xt[buf], t_st[buf]])
                t_st[buf] = P.dma("gpsimd",
                                  x_out[qt * 128:(qt + 1) * 128,
                                        os_ * 1024:(os_ + 1) * 1024],
                                  XO[buf], deps=[t_evx[buf]])
        return P.bar()

    def phase_ffn(x_in, x_out):
        bb = P.bar()
        t_w = None
        HT = [STG, STG2]
        XT = [FWORK[:, 0:1024], FWORK[:, 1024:2048]]
        XO = [FWORK[:, 2048:3072], FWORK[:, 3072:4096]]
        t_ht = [None, None]
        t_xt = [None, None]
        t_st = [None, None]
        t_evx = [None, None]
        for os_ in range(4):
            t_w = P.dma("sync", sb3(WSL, 32), pmaj(bwT[os_]),
                        deps=bb + [t_w] + t_evx)
            for qt in range(NQT):
                buf = qt % 2
                t_ht[buf] = P.dma("sync", HT[buf][:, :], h_scr[qt],
                                  deps=bb + [t_ht[buf], t_evx[buf]])
                t_xt[buf] = P.dma("sync", XT[buf],
                                  x_in[qt * 128:(qt + 1) * 128,
                                       os_ * 1024:(os_ + 1) * 1024],
                                  deps=[t_xt[buf], t_evx[buf]])
                psl = PA[:, buf * 1024:buf * 1024 + 1024]
                t_pe = None
                for ic in range(NIC):
                    for j in range(2):
                        t_pe = P.mm(psl[:, j * 512:(j + 1) * 512],
                                    HT[buf][:, ic * 128:(ic + 1) * 128],
                                    WSL[:, ic * 1024 + j * 512:ic * 1024 + (j + 1) * 512],
                                    start=(ic == 0), stop=(ic == NIC - 1),
                                    deps=([t_ht[buf], t_w, t_evx[buf]]
                                          if ic == 0 and j == 0 else []),
                                    inc=(1 if ic == NIC - 1 and j == 1 else 0))
                t_evx[buf] = P.v(lambda e, psl=psl, buf=buf: e.tensor_tensor(
                    out=XO[buf], in0=psl, in1=XT[buf], op=OP.add),
                    deps=[t_pe, t_xt[buf], t_st[buf]])
                t_st[buf] = P.dma("gpsimd",
                                  x_out[qt * 128:(qt + 1) * 128,
                                        os_ * 1024:(os_ + 1) * 1024],
                                  XO[buf], deps=[t_evx[buf]])
        return P.bar()

    with es:
        block = es.enter_context(nc.Block())
        for engname in ENGINES:
            def emit(engobj, engname=engname):
                P.reset_for(engname, engobj)
                program()
            getattr(block, engname)(emit)

    return nc, tap_outs


def make_in_maps(inputs):
    CONST = host_constants()
    x_full = np.ascontiguousarray(np.asarray(inputs["L0_attn_a_hidden_states"],
                                             dtype=np.float32))
    base = {
        "w_qkv": inputs["L0_attn_a_w_qkv"], "w_o": inputs["L0_attn_a_w_o"],
        "k_pos": inputs["L0_attn_a_k_pos"], "v_pos": inputs["L0_attn_a_v_pos"],
        "k_w1": inputs["L0_attn_a_k_w1"], "k_b1": inputs["L0_attn_a_k_b1"],
        "k_w2": inputs["L0_attn_a_k_w2"], "k_b2": inputs["L0_attn_a_k_b2"],
        "v_w1": inputs["L0_attn_a_v_w1"], "v_b1": inputs["L0_attn_a_v_b1"],
        "v_w2": inputs["L0_attn_a_v_w2"], "v_b2": inputs["L0_attn_a_v_b2"],
        "mem_kv": inputs["L0_attn_a_mem_kv"], "w_gate": inputs["L0_attn_a_w_gate"],
        "b_gate": inputs["L0_attn_a_b_gate"],
        "wq_b": inputs["L0_attn_b_wq"], "wk_b": inputs["L0_attn_b_wk"],
        "wv_b": inputs["L0_attn_b_wv"], "wo_b": inputs["L0_attn_b_wo"],
        "w_ffn": inputs["L0_ffn_weight"],
    }
    base = {k: np.ascontiguousarray(np.asarray(v, dtype=np.float32))
            for k, v in base.items()}
    for k, v in CONST.items():
        base["c_" + k] = np.ascontiguousarray(v)
    return [dict(base, x0=np.ascontiguousarray(x_full[i])) for i in range(B)]


def kernel(**inputs):
    nc, _ = build_program()
    in_maps = make_in_maps(inputs)
    res = run_bass_kernel_spmd(nc, in_maps, core_ids=list(range(B)))
    return np.stack([np.asarray(r["y"], dtype=np.float32)
                     for r in res.results], axis=0)


# version 07
# speedup vs baseline: 1.2252x; 1.0269x over previous
import numpy as np
import ml_dtypes
import concourse.bass as bass
import concourse.mybir as mybir
from concourse.bass_utils import run_bass_kernel_spmd
from concourse.alu_op_type import AluOpType
from contextlib import ExitStack

F32 = mybir.dt.float32
BF16 = mybir.dt.bfloat16
AF = mybir.ActivationFunctionType
AX = mybir.AxisListType
OP = AluOpType

B = 8
S = 2048
D = 4096
H = 32
KVH = 8
G = 4
DH = 128
NQT = S // 128
NIC = D // 128
SCALE = DH ** -0.5
BIGM = float(2.0 ** 100)
EPS = 1e-5

ENGINES = ("sync", "gpsimd", "vector", "scalar", "tensor")
DBG_NO = ()


def pmaj(ap):
    return ap.transpose([1, 0, 2])


def sb3(ap, n):
    return ap.rearrange("p (a b) -> p a b", a=n)


def host_constants():
    bf = ml_dtypes.bfloat16
    c = {}
    c["IDB"] = np.eye(128, dtype=bf)
    perm = np.zeros((128, 128), dtype=bf)
    for j in range(64):
        perm[2 * j, 2 * j + 1] = 1
        perm[2 * j + 1, 2 * j] = 1
    c["PERM"] = perm
    j = np.arange(64, dtype=np.float64)
    inv = 10000.0 ** (-2.0 * j / 128.0)
    t = np.arange(S, dtype=np.float64)
    ang = t[None, :] * inv[:, None]
    c2 = np.zeros((128, S), dtype=np.float64)
    s2 = np.zeros((128, S), dtype=np.float64)
    c2[0::2] = np.cos(ang)
    c2[1::2] = np.cos(ang)
    s2[0::2] = -np.sin(ang)
    s2[1::2] = np.sin(ang)
    c["RC"] = c2.astype(bf)
    c["RS"] = s2.astype(bf)
    E = np.zeros((128, S), dtype=bf)
    for cc in range(128):
        E[cc, cc * 16:(cc + 1) * 16] = BIGM
    c["EBIG"] = E
    od = np.zeros((128, 128), dtype=bf)
    for p in range(128):
        od[p, (p // 16) * 16: p + 1] = BIGM
    c["ODC"] = od
    cm = np.zeros((128, NQT, 132), dtype=bf)
    for qt in range(NQT):
        for p in range(128):
            i = qt * 128 + p
            for cc in range(128):
                if i < 16 * (cc + 1):
                    cm[p, qt, cc] = -BIGM
            cm[p, qt, 129:132] = -BIGM
    c["CMASK"] = cm.reshape(128, NQT * 132)
    band = np.zeros((128, 256), dtype=bf)
    for p in range(128):
        for jc in range(256):
            if not (0 <= 128 + p - jc <= 64):
                band[p, jc] = -BIGM
    c["BAND"] = band
    tri = np.zeros((128, 128), dtype=bf)
    for p in range(128):
        tri[p, p + 1:] = -BIGM
    c["TRI"] = tri
    c["ONES"] = np.ones((1, 512), dtype=bf)
    c["ONECOL"] = np.ones((128, 1), dtype=bf)
    return c


NLANES = 8


class Prog:

    def __init__(self, nc, sems):
        self.nc = nc
        self.sems = sems
        self.count = {k: 0 for k in sems}
        self.cur = None
        self.eng = None
        self.wait_cache = {}
        self.engine_sem = {"sync": "sq", "gpsimd": "gq", "vector": "dve",
                           "scalar": "act", "tensor": "pe"}
        self.lane_idx = {"sync": 0, "gpsimd": 0}

    def reset_for(self, engname, engobj):
        self.cur = engname
        self.eng = engobj
        self.count = {k: 0 for k in self.sems}
        self.wait_cache = {}
        self.lane_idx = {"sync": 0, "gpsimd": 0}

    def op(self, engname, fn, deps=(), inc=0):
        semname = self.engine_sem[engname]
        if self.cur == engname:
            for d in deps:
                if d is None:
                    continue
                s, v = d
                if v <= 0:
                    continue
                key = s
                if self.wait_cache.get(key, 0) < v:
                    self.eng.wait_ge(self.sems[s], v)
                    self.wait_cache[key] = v
            inst = fn(self.eng)
            if inc:
                inst.then_inc(self.sems[semname], inc)
        if inc:
            self.count[semname] += inc
            return (semname, self.count[semname])
        return None

    def dma(self, q, out, in_, deps=(), **kw):
        lane = self.lane_idx[q]
        self.lane_idx[q] = (lane + 1) % NLANES
        semname = self.engine_sem[q] + str(lane)
        prev = self.count[semname]
        deps = list(deps) + ([(semname, prev)] if prev else [])
        if self.cur == q:
            for d in deps:
                if d is None:
                    continue
                s, v = d
                if v <= 0:
                    continue
                if self.wait_cache.get(s, 0) < v:
                    self.eng.wait_ge(self.sems[s], v)
                    self.wait_cache[s] = v
            inst = self.eng.dma_start(out=out, in_=in_, **kw)
            inst.then_inc(self.sems[semname], 16)
        self.count[semname] += 16
        return (semname, self.count[semname])

    def mm(self, out, lhsT, rhs, start, stop, deps=(), inc=0):
        return self.op("tensor",
                       lambda e: e.matmul(out=out, lhsT=lhsT, rhs=rhs,
                                          start=start, stop=stop),
                       deps=deps, inc=inc)

    def tr(self, out, in_, iden, deps=(), inc=0):
        k = in_.shape[0]
        b = in_.base_partition()
        if iden.shape[0] != k or b != 0:
            iden = iden[b:b + k, b:b + k]
        return self.op("tensor",
                       lambda e: e.transpose(out=out, in_=in_, identity=iden),
                       deps=deps, inc=inc)

    def act(self, out, in_, func, bias=None, scale=None, accum_out=None,
            deps=(), inc=0, _dep_bf=None):
        deps = list(deps) + ([_dep_bf] if _dep_bf is not None else [])
        kw = {}
        if bias is not None:
            kw["bias"] = bias
        if scale is not None:
            kw["scale"] = scale
        if accum_out is not None:
            kw["accum_out"] = accum_out
        return self.op("scalar",
                       lambda e: e.activation(out=out, in_=in_, func=func, **kw),
                       deps=deps, inc=inc)

    def v(self, fn, deps=(), inc=1):
        return self.op("vector", fn, deps=deps, inc=inc)

    def bar(self):
        return [(s, c) for s, c in self.count.items()]


def build_program(upto=13, taps=(), sbuf_taps=(), debug_no=()):
    global DBG_NO
    DBG_NO = debug_no
    nc = bass.Bass("TRN2", target_bir_lowering=False)
    CONST = host_constants()

    def inp(name, shape, dt=F32):
        return nc.dram_tensor(name, list(shape), dt, kind="ExternalInput").ap()

    x0 = inp("x0", (S, D))
    w_qkv = inp("w_qkv", (6144, D))
    w_o = inp("w_o", (D, D))
    k_pos = inp("k_pos", (KVH, 16, DH))
    v_pos = inp("v_pos", (KVH, 16, DH))
    k_w1 = inp("k_w1", (2048, 2048))
    k_b1 = inp("k_b1", (2048,))
    k_w2 = inp("k_w2", (DH, 2048))
    k_b2 = inp("k_b2", (DH,))
    v_w1 = inp("v_w1", (2048, 2048))
    v_b1 = inp("v_b1", (2048,))
    v_w2 = inp("v_w2", (DH, 2048))
    v_b2 = inp("v_b2", (DH,))
    mem_kv = inp("mem_kv", (2, KVH, 1, DH))
    w_gate = inp("w_gate", (3 * H, D))
    b_gate = inp("b_gate", (3 * H,))
    wq_b = inp("wq_b", (D, D))
    wk_b = inp("wk_b", (D, 1024))
    wv_b = inp("wv_b", (D, 1024))
    wo_b = inp("wo_b", (D, D))
    w_ffn = inp("w_ffn", (D, D))
    consts_in = {k: inp("c_" + k, v.shape,
                        BF16 if v.dtype == ml_dtypes.bfloat16 else F32)
                 for k, v in CONST.items()}

    y = nc.dram_tensor("y", [S, D], F32, kind="ExternalOutput").ap()

    def scr(name, shape, dt=BF16):
        return nc.dram_tensor(name, list(shape), dt, kind="Internal").ap()

    x1 = scr("x1", (S, D), F32)
    x2 = scr("x2", (S, D), F32)
    x3 = scr("x3", (S, D), F32)
    h_scr = scr("h_scr", (NQT, 128, NIC * 128))
    q_scr = scr("q_scr", (H, 128, S))
    v_scr = scr("v_scr", (128, KVH, 16, 128))
    ao_scr = scr("ao_scr", (NQT, 128, 32, 128))
    wqkvT = scr("wqkvT", (48, 128, NIC * 128))
    woT = scr("woT", (4, 32, 128, 1024))
    bwT = scr("bwT", (4, NIC, 128, 1024))
    w1kT = scr("w1kT", (16, 128, 2048))
    w1vT = scr("w1vT", (16, 128, 2048))
    w2kT = scr("w2kT", (128, 2048))
    w2vT = scr("w2vT", (128, 2048))
    qb_q = scr("qb_q", (D, D))
    qb_k = scr("qb_k", (D, 1024))
    qb_v = scr("qb_v", (D, 1024))

    scratch_map = {"x1": x1, "x2": x2, "x3": x3, "h_scr": h_scr, "q_scr": q_scr,
                   "v_scr": v_scr, "ao_scr": ao_scr, "wqkvT": wqkvT, "woT": woT,
                   "bwT": bwT, "w1kT": w1kT, "w2kT": w2kT, "w1vT": w1vT,
                   "w2vT": w2vT}
    tap_outs = {}
    for name in taps:
        src = scratch_map[name]
        tap_outs[name] = nc.dram_tensor("tap_" + name, list(src.tensor.shape),
                                        src.dtype, kind="ExternalOutput").ap()
    sbuf_tap_list = list(sbuf_taps)

    es = ExitStack()
    sb = lambda n, shape, dt=BF16: es.enter_context(nc.sbuf_tensor(n, list(shape), dt))
    psalloc = lambda n, shape, dt=F32: es.enter_context(nc.psum_tensor(n, list(shape), dt))

    C_IDB = sb("c_idb", (128, 128))
    C_PERM = sb("c_perm", (128, 128))
    C_RC = sb("c_rc", (128, S))
    C_RS = sb("c_rs", (128, S))
    C_E = sb("c_e", (128, S))
    C_ODC = sb("c_odc", (128, 128))
    C_CM = sb("c_cm", (128, NQT * 132))
    C_BAND = sb("c_band", (128, 256))
    C_TRI = sb("c_tri", (128, 128))
    C_ONES = sb("c_ones", (1, 512))
    C_ONECOL = sb("c_onecol", (128, 1))
    BGATE = sb("bgate", (1, 96))
    G_TM = sb("g_tm", (128, NQT * 96))
    CK = sb("ck", (128, KVH, 132))
    CVF = sb("cvf", (128, KVH, 128))
    CVK = sb("cvk", (128, KVH, 128))
    KPF = sb("kpf", (128, 16 * 8))
    VPF = sb("vpf", (128, 16 * 8))
    B1K = sb("b1k", (128, 16), F32)
    B1V = sb("b1v", (128, 16), F32)
    B2K = sb("b2k", (128, 1), F32)
    B2V = sb("b2v", (128, 1), F32)
    B2KR = sb("b2kr", (1, 128))
    B2VR = sb("b2vr", (1, 128))
    SMALL = sb("small", (128, 1024), F32)
    CONSTV = sb("constv", (128, 4), F32)
    STG = sb("stg", (128, 4096))
    STG2 = sb("stg2", (128, 4096))
    PHAT = sb("phat", (128, 8192))
    PHC = sb("phc", (128, 4 * 132))
    PHS = sb("phs", (128, 1024))
    SC4 = sb("sc4", (128, 4 * 132), F32)
    FWORK = sb("fwork", (128, 4096), F32)
    BIGA = sb("biga", (128, 53248))

    RK = BIGA[:, 0:16384]
    WGT = BIGA[:, 49152:52224]
    MEMV = BIGA[0:1, 52224:53248].rearrange("p (h d) -> p h d", h=KVH)
    VKM = BIGA[:, 16384:32768]
    WSL = BIGA[:, 0:32768]
    PH = BIGA[:, 32768:53248]

    PA = psalloc("PA", (128, 2048))
    PB = psalloc("PB", (128, 512))
    PC = psalloc("PC", (128, 512))
    PT = psalloc("PT", (128, 2048), BF16)

    semnames = ["dve", "act", "pe"]
    for q in ("sq", "gq"):
        semnames += [q + str(i) for i in range(NLANES)]
    sems = {k: es.enter_context(nc.semaphore(k)) for k in semnames}
    P = Prog(nc, sems)

    sbuf_ap_map = {"RK": RK, "VKM": VKM, "CK": CK[:].rearrange("p h c -> p (h c)"),
                   "SMALL": SMALL[:], "FWORK": FWORK[:], "STG2": STG2[:],
                   "PHAT": PHAT[:], "PHC": PHC[:], "SC4": SC4[:], "PHS": PHS[:],
                   "CVF": CVF[:].rearrange("p h c -> p (h c)"),
                   "CVK": CVK[:].rearrange("p h c -> p (h c)"),
                   "G_TM": G_TM[:], "KPF": KPF[:], "B1K": B1K[:],
                   "WGT": WGT[:], "MEMV": MEMV[:].rearrange("p h c -> p (h c)")}
    sbuf_tap_outs = {}
    for name in sbuf_tap_list:
        ap = sbuf_ap_map[name]
        sbuf_tap_outs[name] = nc.dram_tensor(
            "stap_" + name, [ap.shape[0], ap.shape[1]], ap.dtype,
            kind="ExternalOutput").ap()

    def program():
        tok = None
        for nm, tile in [("IDB", C_IDB), ("PERM", C_PERM), ("RC", C_RC),
                         ("RS", C_RS), ("EBIG", C_E), ("ODC", C_ODC),
                         ("CMASK", C_CM), ("BAND", C_BAND), ("TRI", C_TRI),
                         ("ONES", C_ONES), ("ONECOL", C_ONECOL)]:
            tok = P.dma("sync", tile[:], consts_in[nm], deps=[tok])
        P.v(lambda e: e.memset(CONSTV[:, 0:1], float(EPS)))
        P.v(lambda e: e.memset(CONSTV[:, 1:2], -1000.0))
        P.v(lambda e: e.memset(CK[:].rearrange("p h c -> p (h c)"), 0.0))
        P.v(lambda e: e.memset(SMALL[:], 0.0))
        P.v(lambda e: e.memset(FWORK[:], 0.0))
        P.v(lambda e: e.memset(STG2[:], 0.0))
        P.v(lambda e: e.memset(PHAT[:], 0.0))
        P.v(lambda e: e.memset(PHC[:].rearrange("p c -> p c"), 0.0))
        P.v(lambda e: e.memset(SC4[:], 0.0))
        t_consts = P.bar()

        def scoped(name, fn, *a, **k):
            with nc.named_scope(name):
                return fn(*a, **k)
        if upto >= 1:
            scoped("P1_prep", phase_weight_prep, t_consts)
        if upto >= 2:
            scoped("P2_ln1", phase_ln, x0)
        if upto >= 3:
            scoped("P3_qkv_a", phase_qkv, is_a=True)
        if upto >= 4:
            scoped("P4_attn_a", phase_attn, is_a=True)
        if upto >= 5:
            scoped("P5_oproj_a", phase_oproj, w_kind="woT", x_in=x0, x_out=x1)
        if upto >= 6:
            scoped("P6_ln2", phase_ln, x1)
        if upto >= 7:
            scoped("P7_ffn1", phase_ffn, x_in=x1, x_out=x2)
        if upto >= 8:
            scoped("P8_ln3", phase_ln, x2)
        if upto >= 9:
            scoped("P9_qkv_b", phase_qkv, is_a=False)
        if upto >= 10:
            scoped("P10_attn_b", phase_attn, is_a=False)
        if upto >= 11:
            scoped("P11_oproj_b", phase_oproj, w_kind="nat", x_in=x2, x_out=x3)
        if upto >= 12:
            scoped("P12_ln4", phase_ln, x3)
        if upto >= 13:
            scoped("P13_ffn2", phase_ffn, x_in=x3, x_out=y)

        bb = P.bar()
        tok = None
        for name in sbuf_tap_list:
            ap = sbuf_ap_map[name]
            out_ap = sbuf_tap_outs[name]
            tok = P.dma("sync", out_ap, ap, deps=bb + [tok])
        for name, out_ap in tap_outs.items():
            src = scratch_map[name]
            flat_src = src.rearrange(
                " ".join("d%d" % i for i in range(len(src.tensor.shape))) +
                " -> (" + " ".join("d%d" % i for i in range(len(src.tensor.shape))) + ")")
            flat_dst = out_ap.rearrange(
                " ".join("d%d" % i for i in range(len(src.tensor.shape))) +
                " -> (" + " ".join("d%d" % i for i in range(len(src.tensor.shape))) + ")")
            tok = P.dma("sync", flat_dst, flat_src, deps=bb + [tok])
        P.op("sync", lambda e: e.nop(), deps=bb + ([tok] if tok else []))

    def phase_weight_prep(t_consts):
        IN = FWORK
        INBB = [BIGA[:, 0:4096], BIGA[:, 4096:8192]]
        OUTB = [BIGA[:, 8192:12288], BIGA[:, 12288:16384]]
        FW2 = PH.bitcast(F32)
        state = {"dma_out": [None, None], "pt_ev": [None, None], "in": None,
                 "cast": None, "bi": 0, "tpo": {}, "evo": {}}

        def do_chunk(src_rows, n_i, out_ap, sign=False):
            bi = state["bi"]
            state["bi"] = 1 - bi
            INB = INBB[bi]
            OUT = OUTB[bi]
            prev_out = state["dma_out"]
            if not sign:
                t_cast = P.dma("gpsimd", INB[:, 0:n_i * 128], src_rows,
                               deps=[state["tpo"].get(bi)] + t_consts)
                state["cast"] = t_cast
            else:
                t_in = P.dma("sync", IN[:, 0:n_i * 128], src_rows,
                             deps=[state["in"], state["cast"]] + t_consts)
                state["in"] = t_in
            if sign:
                t_abs = P.act(FW2[:, 0:n_i * 128], IN[:, 0:n_i * 128], AF.Abs,
                              accum_out=SMALL[:, 0:1],
                              deps=[t_in, state.get("fw2_read")], inc=1)
                t_sg = P.act(FW2[:, 0:n_i * 128], IN[:, 0:n_i * 128], AF.Sign,
                             deps=[t_abs], inc=1)
                t_cast = P.v(lambda e, INB=INB: e.tensor_scalar(
                    out=INB[:, 0:n_i * 128], in0=FW2[:, 0:n_i * 128],
                    scalar1=SMALL[:, 0:1], scalar2=1.0 / (n_i * 128),
                    op0=OP.mult, op1=OP.mult),
                    deps=[t_sg, state["tpo"].get(bi)])
                state["cast"] = t_sg
                state["fw2_read"] = t_cast
            n_done = 0
            gi = 0
            last_ev = None
            while n_done < n_i:
                grp = min(8, n_i - n_done)
                half = gi % 2
                pt = PT[:, half * 1024:half * 1024 + 1024]
                t_pe = None
                for j in range(grp):
                    blk = n_done + j
                    t_pe = P.tr(pt[:, j * 128:(j + 1) * 128],
                                INB[:, blk * 128:(blk + 1) * 128], C_IDB[:],
                                deps=([t_cast, state["pt_ev"][half]]
                                      if j == 0 else []),
                                inc=(1 if j == grp - 1 else 0))
                t_ev = P.v(lambda e, pt=pt, n_done=n_done, grp=grp, OUT=OUT:
                           e.tensor_copy(
                    out=OUT[:, n_done * 128:(n_done + grp) * 128],
                    in_=pt[:, 0:grp * 128]),
                    deps=[t_pe, state["evo"].get(bi)])
                state["pt_ev"][half] = t_ev
                last_ev = t_ev
                n_done += grp
                gi += 1
            state["tpo"][bi] = ("pe", P.count["pe"])
            src_out = (OUT[:, 0:n_i * 128] if out_ap.ndim == 2
                       else sb3(OUT[:, 0:n_i * 128], out_ap.shape[1]))
            t_out = P.dma("gpsimd", out_ap, src_out, deps=[last_ev])
            state["evo"][bi] = t_out
            state["dma_out"] = [t_out, state["dma_out"][0]]
            return t_out

        for oc in range(48):
            do_chunk(w_qkv[oc * 128:(oc + 1) * 128, :], NIC, wqkvT[oc])
        for oc in range(32):
            os_, ocl = oc // 8, oc % 8
            do_chunk(w_o[oc * 128:(oc + 1) * 128, :], 32,
                     pmaj(woT[os_, :, :, ocl * 128:(ocl + 1) * 128]))
        for oc in range(32):
            os_, ocl = oc // 8, oc % 8
            do_chunk(w_ffn[oc * 128:(oc + 1) * 128, :], 32,
                     pmaj(bwT[os_, :, :, ocl * 128:(ocl + 1) * 128]), sign=True)
        for oc in range(16):
            do_chunk(k_w1[oc * 128:(oc + 1) * 128, :], 16, w1kT[oc])
        for oc in range(16):
            do_chunk(v_w1[oc * 128:(oc + 1) * 128, :], 16, w1vT[oc])
        do_chunk(k_w2[:, :], 16, w2kT)
        do_chunk(v_w2[:, :], 16, w2vT)
        P.dma("gpsimd", qb_q, wq_b, deps=t_consts)
        P.dma("gpsimd", qb_k, wk_b, deps=t_consts)
        P.dma("gpsimd", qb_v, wv_b, deps=t_consts)

        INB = INBB[0]
        bb = P.bar()
        tk = state["dma_out"][0]
        tkc = state["cast"]
        tk = P.dma("sync", IN[0:96, 0:4096], w_gate[:, :], deps=bb + [tk, tkc])
        tk = P.v(lambda e: e.tensor_copy(out=INB[0:96, 0:4096],
                                         in_=IN[0:96, 0:4096]), deps=bb + [tk])
        for ic in range(NIC):
            half = ic % 2
            pt = PT[:, half * 1024:half * 1024 + 96]
            tk = P.tr(pt, INB[0:96, ic * 128:(ic + 1) * 128], C_IDB[:],
                      deps=[tk], inc=1)
            tk = P.v(lambda e, pt=pt, ic=ic: e.tensor_copy(
                out=WGT[:, ic * 96:(ic + 1) * 96], in_=pt), deps=[tk])
        tk = P.dma("gpsimd", BGATE[0:1, :], b_gate[None, :], deps=[tk])
        for (posin, dst) in ((k_pos, KPF), (v_pos, VPF)):
            tk = P.dma("sync", IN[0:8, 0:2048],
                       posin.rearrange("h tb d -> h (tb d)"), deps=[tk])
            tk = P.v(lambda e: e.tensor_copy(out=INB[0:8, 0:2048],
                                             in_=IN[0:8, 0:2048]), deps=[tk])
            for tb in range(16):
                half = tb % 2
                pt = PT[:, half * 1024 + 512:half * 1024 + 520]
                tk = P.tr(pt, INB[0:8, tb * 128:(tb + 1) * 128],
                          C_IDB[:], deps=[tk], inc=1)
                tk = P.v(lambda e, pt=pt, dst=dst, tb=tb: e.tensor_copy(
                    out=dst[:, tb * 8:(tb + 1) * 8], in_=pt), deps=[tk])
        tk = P.dma("sync", IN[0:16, 0:128], k_b1.rearrange("(oc p) -> oc p", p=128),
                   deps=[tk])
        tk = P.dma("sync", IN[32:48, 0:128], v_b1.rearrange("(oc p) -> oc p", p=128),
                   deps=[tk])
        tk = P.v(lambda e: e.tensor_copy(out=INB[0:48, 0:128], in_=IN[0:48, 0:128]),
                 deps=[tk])
        tk = P.tr(PT[:, 0:16], INB[0:16, 0:128], C_IDB[:], deps=[tk], inc=1)
        tk = P.v(lambda e: e.tensor_copy(out=B1K[:, :], in_=PT[:, 0:16]),
                 deps=[tk])
        tk = P.tr(PT[:, 1024:1040], INB[32:48, 0:128], C_IDB[:], deps=[tk], inc=1)
        tk = P.v(lambda e: e.tensor_copy(out=B1V[:, :], in_=PT[:, 1024:1040]),
                 deps=[tk])
        tk = P.dma("sync", B2K[:, 0:1], k_b2[:, None], deps=[tk])
        tk = P.dma("sync", B2V[:, 0:1], v_b2[:, None], deps=[tk])
        tk = P.dma("sync", IN[0:1, 128:256], k_b2[None, :], deps=[tk])
        tk = P.dma("sync", IN[0:1, 256:384], v_b2[None, :], deps=[tk])
        tk = P.v(lambda e: e.tensor_copy(out=B2KR[0:1, :], in_=IN[0:1, 128:256]),
                 deps=[tk])
        tk = P.v(lambda e: e.tensor_copy(out=B2VR[0:1, :], in_=IN[0:1, 256:384]),
                 deps=[tk])
        tk = P.dma("sync", IN[:, 512:512 + KVH],
                   mem_kv[0].transpose([2, 1, 0]).squeeze(1), deps=[tk],
                   allow_slow_non_contiguous=True)
        tk = P.dma("sync", IN[0:1, 1024:1024 + KVH * 128],
                   mem_kv[1].transpose([1, 0, 2]).rearrange(
                       "one h d -> one (h d)"), deps=[tk])
        tk = P.v(lambda e: e.tensor_copy(
            out=CK[:, :, 128:129].squeeze(2),
            in_=IN[:, 512:512 + KVH]), deps=[tk])
        tk = P.v(lambda e: e.tensor_copy(
            out=MEMV[0:1, :, :].rearrange("one h d -> one (h d)"),
            in_=IN[0:1, 1024:1024 + KVH * 128]), deps=[tk])

    def phase_ln(x_src):
        bb = P.bar()
        XT = [BIGA[:, 0:8192].bitcast(F32), BIGA[:, 8192:16384].bitcast(F32)]
        APL = [BIGA[:, 16384:20480], BIGA[:, 20480:24576]]
        STA = [BIGA[:, 24576:28672], BIGA[:, 28672:32768]]
        TRASH = PH.bitcast(F32)[:, 0:4096]
        tok = {}
        prev_ev = [None, None]
        for qt in range(NQT):
            bf = qt % 2
            c0, c1, c2, c3, c4, c5, c6 = [8 * bf + j for j in range(7)]
            t_in = P.dma("sync", XT[bf], x_src[qt * 128:(qt + 1) * 128, :],
                         deps=bb + [tok.get(("apl", qt - 2))])
            t_s1 = P.act(TRASH, XT[bf], AF.Identity, accum_out=SMALL[:, c0:c0 + 1],
                         deps=[t_in, tok.get(("sm", qt - 2))] + bb, inc=1)
            t_s2 = P.act(TRASH, XT[bf], AF.Square, accum_out=SMALL[:, c1:c1 + 1],
                         deps=[t_s1], inc=1)
            t_v1 = P.v(lambda e, c0=c0, c2=c2: e.tensor_scalar(
                out=SMALL[:, c2:c2 + 1], in0=SMALL[:, c0:c0 + 1], scalar1=-1.0 / D,
                scalar2=None, op0=OP.mult), deps=[t_s2] + bb)
            t_v2 = P.v(lambda e, c2=c2, c3=c3: e.tensor_tensor(
                out=SMALL[:, c3:c3 + 1], in0=SMALL[:, c2:c2 + 1],
                in1=SMALL[:, c2:c2 + 1], op=OP.mult), deps=[t_v1])
            t_v3 = P.v(lambda e, c1=c1, c3=c3, c4=c4: e.scalar_tensor_tensor(
                out=SMALL[:, c4:c4 + 1], in0=SMALL[:, c1:c1 + 1], scalar=1.0 / D,
                in1=SMALL[:, c3:c3 + 1], op0=OP.mult, op1=OP.subtract),
                deps=[t_v2])
            t_sq = P.act(SMALL[:, c5:c5 + 1], SMALL[:, c4:c4 + 1], AF.Sqrt,
                         bias=CONSTV[:, 0:1], deps=[t_v3], inc=1)
            t_rs = P.v(lambda e, c5=c5, c6=c6: e.reciprocal(
                out=SMALL[:, c6:c6 + 1], in_=SMALL[:, c5:c5 + 1]), deps=[t_sq])
            tok[("sm", qt)] = t_rs
            t_h = P.v(lambda e, bf=bf, c2=c2, c6=c6: e.tensor_scalar(
                out=APL[bf], in0=XT[bf], scalar1=SMALL[:, c2:c2 + 1],
                scalar2=SMALL[:, c6:c6 + 1], op0=OP.add, op1=OP.mult),
                deps=[t_rs, tok.get(("tr", qt - 2))])
            tok[("apl", qt)] = t_h
            t_pe = None
            ev_toks = []
            for grp in range(4):
                half = grp % 2
                pt = PT[:, half * 1024:half * 1024 + 1024]
                for j in range(8):
                    ic = grp * 8 + j
                    t_pe = P.tr(pt[:, j * 128:(j + 1) * 128],
                                APL[bf][:, ic * 128:(ic + 1) * 128], C_IDB[:],
                                deps=([t_h, prev_ev[half]] if j == 0 else []),
                                inc=(1 if j == 7 else 0))
                t_ev = P.v(lambda e, pt=pt, grp=grp, bf=bf: e.tensor_copy(
                    out=STA[bf][:, grp * 1024:(grp + 1) * 1024], in_=pt[:, :]),
                    deps=[t_pe, tok.get(("st", qt - 2))])
                prev_ev[half] = t_ev
                ev_toks.append(t_ev)
            tok[("tr", qt)] = t_pe
            t_st = P.dma("gpsimd", h_scr[qt], STA[bf], deps=[ev_toks[-1]])
            tok[("st", qt)] = t_st

    def phase_qkv(is_a):
        bb = P.bar()
        H_Q = BIGA[:, 16384:32768]
        WST = [PH[:, 0:4096], PH[:, 4096:8192]]
        KQ = PH[:, 8192:12288]
        VQ = PH[:, 12288:16384]
        t_h = None
        t_wst = [None, None]
        t_kv_done = bb
        ev_tok = {}
        pa_free = [None, None]
        for tq in range(4):
            t_h = P.dma("sync", sb3(H_Q[:, :], 4),
                        h_scr[tq * 4:(tq + 1) * 4].transpose([1, 0, 2]),
                        deps=t_kv_done + [t_h])
            H_Qv = H_Q.rearrange("p (qtl ic t) -> p qtl ic t", qtl=4, ic=NIC)
            for oc in range(48):
                buf = oc % 2
                if is_a:
                    t_wst[buf] = P.dma("sync", WST[buf][:, :], wqkvT[oc],
                                       deps=[ev_tok.get(oc - 2), t_wst[buf]])
                else:
                    qsrcw, qoc = ((qb_q, oc) if oc < 32 else
                                  (qb_k, oc - 32) if oc < 40 else (qb_v, oc - 40))
                    t_wst[buf] = P.dma(
                        "sync", sb3(WST[buf][:, :], NIC),
                        qsrcw.rearrange("(ic p) o -> p ic o", p=128)[
                            :, :, qoc * 128:(qoc + 1) * 128],
                        deps=[ev_tok.get(oc - 2), t_wst[buf]])
                psl = PA[:, buf * 512:buf * 512 + 512]
                t_pe = None
                for ic in range(NIC):
                    t_pe = P.mm(psl, WST[buf][:, ic * 128:(ic + 1) * 128],
                                H_Qv[:, :, ic, :],
                                start=(ic == 0), stop=(ic == NIC - 1),
                                deps=([t_wst[buf], t_h, pa_free[buf]]
                                      if ic == 0 else []),
                                inc=(1 if ic == NIC - 1 else 0))
                if oc < 32:
                    qs = STG[:, buf * 512:buf * 512 + 512]
                    t_ev = P.v(lambda e, psl=psl, qs=qs: e.tensor_scalar(
                        out=qs, in0=psl, scalar1=float(SCALE),
                        scalar2=None, op0=OP.mult),
                        deps=[t_pe, ev_tok.get(("st", oc - 2))])
                    t_st = P.dma("gpsimd", q_scr[oc, :, tq * 512:(tq + 1) * 512],
                                 qs, deps=[t_ev])
                    ev_tok[("st", oc)] = t_st
                elif oc < 40:
                    kvh = oc - 32
                    dst = (KQ[:, kvh * 512:(kvh + 1) * 512] if is_a
                           else RK[:, kvh * 2048 + tq * 512:kvh * 2048 + tq * 512 + 512])
                    t_ev = P.v(lambda e, psl=psl, dst=dst: e.tensor_copy(
                        out=dst, in_=psl), deps=[t_pe])
                else:
                    kvh = oc - 40
                    t_ev = P.v(lambda e, psl=psl, kvh=kvh: e.tensor_copy(
                        out=VQ[:, kvh * 512:(kvh + 1) * 512], in_=psl),
                        deps=[t_pe, ev_tok.get(("vt", kvh, tq))])
                ev_tok[oc] = t_ev
                pa_free[buf] = t_ev
            prev_pt = [ev_tok.get(("gsg", tq - 1)), ev_tok.get(("gsg", tq - 1))]
            for kvh in range(KVH):
                t_evs = []
                for j in range(4):
                    half = j % 2
                    pt = PT[:, half * 1024:half * 1024 + 128]
                    t_pe = P.tr(pt, VQ[:, kvh * 512 + j * 128:kvh * 512 + (j + 1) * 128],
                                C_IDB[:], deps=[ev_tok[40 + kvh], prev_pt[half]],
                                inc=1)
                    t_ev = P.v(lambda e, pt=pt, j=j: e.tensor_copy(
                        out=STG2[:, j * 128:(j + 1) * 128], in_=pt),
                        deps=[t_pe, ev_tok.get(("vs_last",))])
                    prev_pt[half] = t_ev
                    t_evs.append(t_ev)
                t_st = P.dma("gpsimd",
                             v_scr[:, kvh, tq * 4:(tq + 1) * 4, :],
                             sb3(STG2[:, 0:512], 4), deps=[t_evs[-1]])
                ev_tok[("vs_last",)] = t_st
                ev_tok[("vt", kvh, tq)] = t_evs[-1]
            if is_a and 'gates' not in debug_no:
                t_pe = None
                for ic in range(NIC):
                    t_pe = P.mm(PB[0:96, 0:512], WGT[:, ic * 96:(ic + 1) * 96],
                                H_Qv[:, :, ic, :],
                                start=(ic == 0), stop=False,
                                deps=([t_h, ev_tok.get(("gev", tq - 1))]
                                      if ic == 0 else []))
                t_pe = P.mm(PB[0:96, 0:512], BGATE[0:1, 0:96], C_ONES[0:1, 0:512],
                            start=False, stop=True, deps=[], inc=1)
                t_gev = P.v(lambda e: e.tensor_copy(out=STG[0:96, 1024:1536],
                                                    in_=PB[0:96, 0:512]),
                            deps=[t_pe, ev_tok.get(("gsg", tq - 1))])
                ev_tok[("gev", tq)] = t_gev
                t_sg = None
                for j in range(4):
                    half = j % 2
                    pt = PT[:, half * 1024 + 256:half * 1024 + 256 + 96]
                    t_pe = P.tr(pt, STG[0:96, 1024 + j * 128:1024 + (j + 1) * 128],
                                C_IDB[:], deps=[t_gev, ev_tok.get(("gsg2", j - 2, tq))],
                                inc=1)
                    qt = tq * 4 + j
                    t_sg = P.act(G_TM[:, qt * 96:(qt + 1) * 96], pt, AF.Sigmoid,
                                 deps=[t_pe], inc=1)
                    ev_tok[("gsg2", j, tq)] = t_sg
                ev_tok[("gsg", tq)] = t_sg
            if is_a and 'kvprep' not in debug_no:
                t_kv_done = kvprep_quarter(tq, KQ, VQ, ev_tok)
            else:
                t_kv_done = P.bar()
        return P.bar()

    def kvprep_quarter(tq, KQ, VQ, ev_tok):
        kv_ready = [ev_tok[32 + j] for j in range(16)]
        entry_pe = ("pe", P.count["pe"])
        RL = BIGA[:, 16384:20480]
        WB = [BIGA[:, 20480:22528], BIGA[:, 22528:24576]]
        W2B = BIGA[:, 24576:26624]
        t_rope = []
        for kvh in (range(KVH) if "rope" not in DBG_NO else []):
            ksl = KQ[:, kvh * 512:(kvh + 1) * 512]
            t_pe = P.mm(PA[:, 1024:1536], C_PERM[:], ksl, start=True, stop=True,
                        deps=kv_ready + [ev_tok.get(("rp", kvh - 1))], inc=1)
            t_a = P.v(lambda e, ksl=ksl: e.tensor_tensor(
                out=STG[:, 1536:2048], in0=ksl,
                in1=C_RC[:, tq * 512:(tq + 1) * 512], op=OP.mult),
                deps=kv_ready + [ev_tok.get(("rp", kvh - 1))])
            t_b = P.v(lambda e: e.scalar_tensor_tensor(
                out=FWORK[:, 0:512], in0=PA[:, 1024:1536], scalar=1.0,
                in1=C_RS[:, tq * 512:(tq + 1) * 512], op0=OP.mult, op1=OP.mult),
                deps=[t_pe, t_a])
            t_rk = P.v(lambda e, kvh=kvh: e.tensor_tensor(
                out=RK[:, kvh * 2048 + tq * 512:kvh * 2048 + tq * 512 + 512],
                in0=STG[:, 1536:2048], in1=FWORK[:, 0:512], op=OP.add),
                deps=[t_b])
            ev_tok[("rp", kvh)] = t_rk
            t_rope.append(t_rk)
        t_last = t_rope[-1] if t_rope else P.bar()[0] and None
        mlp_list = [] if "mlp" in DBG_NO else [
                (KQ, w1kT, w2kT, B1K, B2K, B2KR, KPF, CK),
                (VQ, w1vT, w2vT, B1V, B2V, B2VR, VPF, CVF)]
        for (SRC, W1T, W2T, B1, B2c, B2r, PF, CDST) in mlp_list:
            t_w = [None, None]
            t_rl = None
            for oc in range(16):
                buf = oc % 2
                t_w[buf] = P.dma("sync", WB[buf][:, :], W1T[oc],
                                 deps=[t_w[buf], ev_tok.get(("rlw", oc - 2)),
                                       ev_tok.get(("rlw_carry",)) if oc < 2 else None,
                                       entry_pe if oc < 2 else None])
                pb0 = (oc % 2) * 512
                t_pe = None
                for ic in range(16):
                    rhs_ic = SRC.rearrange("p (h t) -> p h t", h=KVH)[:, :, ic::16]
                    t_pe = P.mm(PA[:, pb0:pb0 + 256],
                                WB[buf][:, ic * 128:(ic + 1) * 128],
                                rhs_ic, start=(ic == 0), stop=False,
                                deps=([t_w[buf], t_last,
                                       ev_tok.get(("rlv", oc - 2))] if ic == 0 else []))
                    t_pe = P.mm(PA[:, pb0 + 256:pb0 + 264],
                                WB[buf][:, ic * 128:(ic + 1) * 128],
                                PF[:, ic * 8:(ic + 1) * 8],
                                start=False, stop=(ic == 15),
                                deps=[], inc=(1 if ic == 15 else 0))
                t_bv = P.v(lambda e, oc=oc, B1=B1, pb0=pb0: e.tensor_scalar(
                    out=SMALL[:, 8 + (oc % 2) * 96:16 + (oc % 2) * 96],
                    in0=PA[:, pb0 + 256:pb0 + 264],
                    scalar1=B1[:, oc:oc + 1], scalar2=None, op0=OP.add),
                    deps=[t_pe, ev_tok.get(("bvu", oc - 2))])
                for h8 in range(KVH):
                    t_rl = P.v(lambda e, oc=oc, h8=h8, pb0=pb0: e.tensor_scalar(
                        out=RL[:, oc * 256 + h8 * 32:oc * 256 + (h8 + 1) * 32],
                        in0=PA[:, pb0 + h8 * 32:pb0 + (h8 + 1) * 32],
                        scalar1=SMALL[:, 8 + (oc % 2) * 96 + h8:
                                      9 + (oc % 2) * 96 + h8], scalar2=0.0,
                        op0=OP.add, op1=OP.max), deps=[t_bv])
                ev_tok[("bvu", oc)] = t_rl
                ev_tok[("rlv", oc)] = t_rl
                ev_tok[("rlw", oc)] = t_rl
                if oc >= 14:
                    ev_tok[("rlw_carry",)] = t_rl
            t_w2 = P.dma("sync", W2B[:, :], W2T,
                         deps=[ev_tok.get(("w2last",)), entry_pe])
            t_pe = None
            for oc in range(16):
                t_pe = P.mm(PC[:, 0:256], W2B[:, oc * 128:(oc + 1) * 128],
                            RL[:, oc * 256:(oc + 1) * 256],
                            start=(oc == 0), stop=(oc == 15),
                            deps=([t_w2, t_rl, ev_tok.get(("cklast",))]
                                  if oc == 0 else []),
                            inc=(1 if oc == 15 else 0))
            t_ck = P.v(lambda e, CDST=CDST, B2c=B2c: e.tensor_scalar(
                out=CDST[:, :, tq * 32:(tq + 1) * 32],
                in0=PC[:, 0:256].rearrange("p (h c) -> p h c", h=KVH),
                scalar1=B2c[:, 0:1], scalar2=None,
                op0=OP.add), deps=[t_pe])
            ev_tok[("w2last",)] = t_ck
            ev_tok[("cklast",)] = t_ck
            t_last = t_ck
        return P.bar()

    def phase_attn(is_a):
        bb = P.bar()
        Q4 = PH[:, 0:8192]
        RQ4 = PH[:, 8192:16384]
        TS = [STG[:, 0:1024], STG[:, 1024:2048]]
        AOST = [STG[:, 2048:2560], STG[:, 2560:3072]]
        TSM = STG[:, 3072:3584]
        pts = {0: None, 1: None}
        tsa = {0: None, 1: None}
        st = {}

        P.v(lambda e: e.memset(PB[:], 0.0), deps=bb)
        t_vkm = P.dma("sync",
                      VKM.rearrange("p (kvh ch dh) -> p kvh ch dh", kvh=KVH, ch=16),
                      v_scr, deps=bb)
        t_cvk = None
        if is_a:
            for kvh in range(KVH):
                bank = kvh % 2
                pt = PT[:, bank * 1024:bank * 1024 + 128]
                t_pe = P.tr(pt, CVF[:, kvh, :], C_IDB[:],
                            deps=bb + [pts[bank]], inc=1)
                t_ev = P.v(lambda e, pt=pt, kvh=kvh: e.tensor_copy(
                    out=CVK[:, kvh, :], in_=pt), deps=[t_pe])
                pts[bank] = t_ev
            t_cvk = pts[1]

        def rownorm_max(src_bf, n, out_col, deps):
            t_sq = P.act(STG2[:, 0:n], src_bf, AF.Square,
                         deps=deps + [st.get(("nrow",))], inc=1)
            t_mm = None
            for j in range((n + 511) // 512):
                n0, n1 = j * 512, min(n, j * 512 + 512)
                t_mm = P.mm(PA[0:1, n0:n1], C_ONECOL[:], STG2[:, n0:n1],
                            start=True, stop=True,
                            deps=([t_sq, st.get(("nrow",)), st.get(("mkb",))]
                                  if j == 0 else []),
                            inc=(1 if n1 == n else 0))
            t_mx = P.v(lambda e, n=n: e.max(out=SMALL[0:1, 64:72],
                                            in_=PA[0:1, 0:n]),
                       deps=[t_mm, st.get(("ncp",))])
            st[("nrow",)] = t_mx
            t_cp = P.v(lambda e, out_col=out_col: e.tensor_copy(
                out=SMALL[0:1, out_col:out_col + 1], in_=SMALL[0:1, 64:65]),
                deps=[t_mx])
            st[("ncp",)] = t_cp
            return t_cp

        t_norm = []
        for kvh in range(KVH):
            t_norm.append(rownorm_max(RK[:, kvh * 2048:(kvh + 1) * 2048], 2048,
                                      76 + kvh, bb))
            if is_a:
                t_norm.append(rownorm_max(CK[:, kvh, 0:129], 129, 84 + kvh,
                                          bb + [t_cvk]))

        def make_bias(qcol, kcol, outcol, deps):
            t_p = P.v(lambda e: e.tensor_tensor(
                out=SMALL[0:1, 72:73], in0=SMALL[0:1, qcol:qcol + 1],
                in1=SMALL[0:1, kcol:kcol + 1], op=OP.mult),
                deps=deps + [st.get(("mkb",))])
            t_s = P.act(SMALL[0:1, 73:74], SMALL[0:1, 72:73], AF.Sqrt,
                        deps=[t_p], inc=1)
            t_n = P.v(lambda e: e.tensor_scalar(
                out=SMALL[0:1, 74:75], in0=SMALL[0:1, 73:74], scalar1=-1.02,
                scalar2=None, op0=OP.mult), deps=[t_s])
            t_c = P.v(lambda e: e.tensor_copy(out=STG2[0:1, 4000:4001],
                                              in_=SMALL[0:1, 74:75]), deps=[t_n])
            t_bm = P.mm(PA[:, 1032:1040][:, 0:1], C_ONES[0:1, 0:128],
                        STG2[0:1, 4000:4001], start=True, stop=True,
                        deps=[t_c, st.get(("mkbm",)), st.get(("ropeprev",))],
                        inc=1)
            t_e = P.v(lambda e, outcol=outcol: e.tensor_copy(
                out=SMALL[:, outcol:outcol + 1], in_=PA[:, 1032:1033]),
                deps=[t_bm])
            st[("mkb",)] = t_e
            st[("mkbm",)] = t_e
            return t_e

        t_q = None
        for kvh in (range(KVH) if "short" not in DBG_NO else [0]):
            t_q = P.dma("sync", sb3(Q4, 4),
                        pmaj(q_scr[kvh * 4:(kvh + 1) * 4]),
                        deps=bb + [t_q, st.get(("qfree", kvh - 1))])
            t_qn = rownorm_max(Q4[:, 0:2048], 2048, 75, [t_q])
            for j in range(1, 4):
                t_qn2 = rownorm_max(Q4[:, j * 2048:(j + 1) * 2048], 2048, 64, [t_q])
                t_qn = P.v(lambda e: e.tensor_tensor(
                    out=SMALL[0:1, 75:76], in0=SMALL[0:1, 75:76],
                    in1=SMALL[0:1, 64:65], op=OP.max), deps=[t_qn, t_qn2])
                st[("ncp",)] = t_qn
            t_bf = make_bias(75, 76 + kvh, 100, [t_qn] + t_norm[-1:])
            t_bc = make_bias(75, 84 + kvh, 101, [t_qn]) if is_a else None
            t_rq = [None] * 4
            if is_a:
                for g in range(4):
                    qsl = Q4[:, g * 2048:(g + 1) * 2048]
                    pab = 1024 + (g % 2) * 512
                    t_b = None
                    for j4 in range(4):
                        t_pe4 = P.mm(
                            PA[:, pab:pab + 512], C_PERM[:],
                            qsl[:, j4 * 512:(j4 + 1) * 512], start=True, stop=True,
                            deps=([t_q, st.get(("ropeb", kvh, g - 2)), t_bf, t_bc]
                                  if j4 == 0 else [t_b]),
                            inc=1)
                        t_b = P.v(lambda e, pab=pab, j4=j4, g=g: e.scalar_tensor_tensor(
                            out=FWORK[:, g % 2 * 2048 + j4 * 512:
                                      g % 2 * 2048 + (j4 + 1) * 512],
                            in0=PA[:, pab:pab + 512], scalar=1.0,
                            in1=C_RS[:, j4 * 512:(j4 + 1) * 512],
                            op0=OP.mult, op1=OP.mult),
                            deps=[t_pe4, st.get(("ropec", kvh, g - 2))])
                        st[("ropeb", kvh, g)] = t_b
                    t_a = P.v(lambda e, qsl=qsl, g=g: e.tensor_tensor(
                        out=STG2[:, g % 2 * 2048:g % 2 * 2048 + 2048], in0=qsl,
                        in1=C_RC[:, :], op=OP.mult),
                        deps=[t_q, st.get(("roped", kvh, g - 2)), t_bf])
                    t_rq[g] = P.v(lambda e, g=g: e.tensor_tensor(
                        out=RQ4[:, g * 2048:(g + 1) * 2048],
                        in0=STG2[:, g % 2 * 2048:g % 2 * 2048 + 2048],
                        in1=FWORK[:, g % 2 * 2048:g % 2 * 2048 + 2048],
                        op=OP.add), deps=[t_a, st.get(("ropeb", kvh, g))])
                    st[("ropec", kvh, g)] = t_rq[g]
                    st[("roped", kvh, g)] = t_rq[g]
            else:
                t_rq = [t_q] * 4
            st[("prologue", kvh)] = ("dve", P.count["dve"])
            st[("ropeprev",)] = ("dve", P.count["dve"])

            for qt in (range(NQT) if "short" not in DBG_NO else [0, 1]):
                Cn = qt + 1
                Nk = Cn * 128
                half_ok = Nk <= 1024
                t_pb_free = st.get(("aoev", kvh, qt - 1)) or \
                    st.get(("aoev", kvh - 1, NQT - 1))
                n_sl_chunks = (qt + 1 - max(0, qt - 1))
                if is_a:
                    n_avs = ((0 if "av_c" in DBG_NO else 2) +
                             (0 if "av_f" in DBG_NO else Cn) +
                             (0 if "av_s" in DBG_NO else n_sl_chunks))
                else:
                    n_avs = Cn
                av_state = {"i": 0, "n": n_avs}

                def av_mm(g, lhsT, rhs, deps=(), inc=0, stop=False, full=False):
                    s0 = av_state["i"] == 0
                    last = av_state["i"] == av_state["n"] - 1
                    av_state["i"] += 1
                    out = PB[:, 0:512] if full else PB[:, g * 128:(g + 1) * 128]
                    return P.mm(out, lhsT, rhs,
                                start=s0, stop=last,
                                deps=list(deps) + ([t_pb_free] if s0 else []),
                                inc=inc)

                def transpose_group(srcs, lhsTs, g, dep_first, extra_av_deps=()):
                    bank = st.get(("tgb",), 0)
                    st[("tgb",)] = 1 - bank
                    pt = PT[:, bank * 1024:bank * 1024 + 1024]
                    t_p = None
                    for j, ap in enumerate(srcs):
                        t_p = P.tr(pt[:, j * 128:(j + 1) * 128], ap, C_IDB[:],
                                   deps=([dep_first, pts[bank]] if j == 0 else []),
                                   inc=(1 if j == len(srcs) - 1 else 0))
                    t_e = P.v(lambda e, pt=pt, n=len(srcs): e.tensor_copy(
                        out=TS[bank][:, 0:n * 128], in_=pt[:, 0:n * 128]),
                        deps=[t_p, tsa[bank]])
                    pts[bank] = t_e
                    t_av = None
                    for j, lh in enumerate(lhsTs):
                        t_av = av_mm(g, lh, TS[bank][:, j * 128:(j + 1) * 128],
                                     deps=([t_e] + list(extra_av_deps)
                                           if j == 0 else []),
                                     inc=(1 if j == len(lhsTs) - 1 else 0))
                    tsa[bank] = t_av
                    return t_av

                t_ec = [None] * 4
                if is_a:
                    _sid, _ = nc.enter_named_scope("sub_comp", False)
                    t_sc = [None] * 4
                    for g in range(4):
                        lq = Q4[:, g * 2048 + qt * 128:g * 2048 + qt * 128 + 128]
                        csl = PA[:, g * 512:g * 512 + 512]
                        t_pe = P.mm(csl[:, 0:129], lq, CK[:, kvh, 0:129],
                                    start=True, stop=False,
                                    deps=[t_q, t_bc,
                                          st.get(("paf", kvh, qt - 1, 3)),
                                          (st.get(("prologue", kvh))
                                           if qt == 0 else None)])
                        t_pe = P.mm(csl[:, 0:129], C_IDB[:],
                                    C_CM[:, qt * 132:qt * 132 + 129],
                                    start=False, stop=True, deps=[], inc=1)
                        t_ec[g] = P.act(PHC[:, g * 132:g * 132 + 129],
                                        csl[:, 0:129], AF.Exp,
                                        bias=SMALL[:, 101:102], _dep_bf=t_bc,
                                        accum_out=SMALL[:, 20 + 3 * g:21 + 3 * g],
                                        deps=[t_pe, st.get(("phc", kvh, qt - 1, g))],
                                        inc=1)
                        t_sc[g] = P.v(lambda e, csl=csl, g=g: e.tensor_copy(
                            out=SC4[:, g * 132:g * 132 + 129], in_=csl[:, 0:129]),
                            deps=[t_pe, t_ec[g], st.get(("sc4", kvh, qt - 1, g))])
                        st[("sc4", kvh, qt, g)] = t_ec[g]
                        st[("pacons", kvh, qt, g)] = t_ec[g]
                    nc.leave_named_scope("sub_comp", _sid, False)
                    _sid, _ = nc.enter_named_scope("sub_sel", False)
                    t_i1 = P.v(lambda e: e.tensor_tensor(
                        out=FWORK[:, 0:128], in0=SC4[:, 0:128], in1=SC4[:, 132:260],
                        op=OP.add), deps=[t_sc[0], t_sc[1]])
                    t_i2 = P.v(lambda e: e.tensor_tensor(
                        out=FWORK[:, 128:256], in0=SC4[:, 264:392],
                        in1=SC4[:, 396:524], op=OP.add), deps=[t_sc[2], t_sc[3]])
                    t_imp = P.v(lambda e: e.tensor_tensor(
                        out=FWORK[:, 0:128], in0=FWORK[:, 0:128],
                        in1=FWORK[:, 128:256], op=OP.add), deps=[t_i1, t_i2])
                    t_top = P.v(lambda e: e.max(out=SMALL[:, 32:40],
                                                in_=FWORK[:, 0:128]), deps=[t_imp])
                    t_c1 = P.v(lambda e: e.tensor_scalar(
                        out=FWORK[:, 256:384], in0=FWORK[:, 0:128],
                        scalar1=SMALL[:, 35:36], scalar2=None, op0=OP.is_ge),
                        deps=[t_top])
                    t_nm = P.v(lambda e: e.tensor_reduce(
                        out=SMALL[:, 40:41], in_=FWORK[:, 0:128], axis=AX.X,
                        op=OP.max), deps=[t_imp])
                    t_nm2 = P.v(lambda e: e.tensor_scalar(
                        out=SMALL[:, 41:42], in0=SMALL[:, 40:41], scalar1=-0.25,
                        scalar2=1000.0, op0=OP.mult, op1=OP.min), deps=[t_nm])
                    t_ei = P.act(FWORK[:, 384:512], FWORK[:, 0:128], AF.Exp,
                                 scale=0.25, bias=SMALL[:, 41:42],
                                 accum_out=SMALL[:, 42:43], deps=[t_nm2], inc=1)
                    t_sent = P.act(SMALL[:, 43:44], SMALL[:, 41:42], AF.Exp,
                                   bias=CONSTV[:, 1:2], deps=[t_ei], inc=1)
                    t_z = P.v(lambda e: e.tensor_tensor(
                        out=SMALL[:, 44:45], in0=SMALL[:, 42:43],
                        in1=SMALL[:, 43:44], op=OP.add), deps=[t_sent])
                    t_thr = P.v(lambda e: e.tensor_scalar(
                        out=SMALL[:, 45:46], in0=SMALL[:, 44:45], scalar1=1e-10,
                        scalar2=None, op0=OP.mult), deps=[t_z])
                    t_am = P.v(lambda e: e.scalar_tensor_tensor(
                        out=FWORK[:, 512:640], in0=FWORK[:, 384:512],
                        scalar=SMALL[:, 45:46], in1=FWORK[:, 256:384],
                        op0=OP.is_gt, op1=OP.mult), deps=[t_thr, t_c1])
                    t_amb = P.v(lambda e: e.tensor_scalar(
                        out=STG2[:, 2048:2176], in0=FWORK[:, 512:640],
                        scalar1=-1.0, scalar2=None, op0=OP.add),
                        deps=[t_am, st.get(("amt_used", kvh, qt - 1)),
                              (st.get(("prologue", kvh)) if qt == 0 else None)])
                    bank = st.get(("tgb",), 0)
                    st[("tgb",)] = 1 - bank
                    t_pe = P.tr(PT[:, bank * 1024:bank * 1024 + 128],
                                STG2[:, 2048:2176], C_IDB[:],
                                deps=[t_amb, pts[bank]], inc=1)
                    t_amt = P.v(lambda e, bank=bank: e.tensor_copy(
                        out=STG2[:, 2176:2304],
                        in_=PT[:, bank * 1024:bank * 1024 + 128]),
                        deps=[t_pe, st.get(("amt_used", kvh, qt - 1))])
                    pts[bank] = t_amt
                    nc.leave_named_scope("sub_sel", _sid, False)

                gtok = {}

                def emit_scores(g):
                    qsrc = RQ4 if is_a else Q4
                    lq = qsrc[:, g * 2048 + qt * 128:g * 2048 + qt * 128 + 128]
                    if is_a:
                        c0 = max(0, qt - 1)
                        nsl = (qt + 1 - c0) * 128
                        psb = g
                        phs = PHS[:, psb * 256:psb * 256 + nsl]
                        t_ps = P.mm(PC[:, 0:nsl], lq,
                                    RK[:, kvh * 2048 + c0 * 128:
                                       kvh * 2048 + (qt + 1) * 128],
                                    start=True, stop=False,
                                    deps=[t_rq[g],
                                          st.get(("pcs", kvh, qt, g - 1)) or
                                          st.get(("pcs", kvh, qt - 1, 3))])
                        t_ps = P.mm(PC[:, 0:nsl], C_IDB[:],
                                    C_BAND[:, 256 - nsl:256], start=False,
                                    stop=True, deps=[], inc=1)
                        t_es = P.act(phs, PC[:, 0:nsl], AF.Exp,
                                     bias=SMALL[:, 100:101], _dep_bf=t_bf,
                                     accum_out=SMALL[:, 22 + 3 * g:23 + 3 * g],
                                     deps=[t_ps,
                                           st.get(("phs_used", kvh, qt - 1)),
                                           (st.get(("prologue", kvh)) if qt == 0
                                            else st.get(("rzdone", kvh, qt - 1, g)))],
                                     inc=1)
                        st[("pcs", kvh, qt, g)] = t_es
                        st[("esl", kvh, qt, g)] = t_es
                    pab = (g % 2) * 1024
                    nhalf = 1 if Nk <= 1024 else 2
                    jod_abs = qt // 4
                    phb = g
                    t_efh = None
                    for h2 in range(nhalf):
                        m0 = h2 * 1024
                        m1 = min(Nk, m0 + 1024)
                        PAg = PA[:, pab:pab + (m1 - m0)]
                        nblk = (m1 - m0 + 511) // 512
                        dep0 = [t_rq[g], t_vkm,
                                (st.get(("paf", kvh, qt, g - 2)) or
                                 st.get(("paf", kvh, qt - 1, 2 + g))),
                                (t_ec[g] if is_a else None),
                                (st.get(("pacons", kvh, qt, 3)) if is_a else None),
                                (st.get(("prologue", kvh)) if qt == 0 else None),
                                st.get(("pafh", kvh, qt, g))]
                        has_od = (m0 <= qt * 128 < m1)
                        t_pe = None
                        for j in range(nblk):
                            n0 = j * 512
                            n1 = min(m1 - m0, n0 + 512)
                            jabs = (m0 + n0) // 512
                            skipm = (not is_a) or ("nomask" in DBG_NO)
                            last_mm = skipm and (j == nblk - 1) and not has_od
                            t_pe = P.mm(PAg[:, n0:n1], lq,
                                        RK[:, kvh * 2048 + m0 + n0:
                                           kvh * 2048 + m0 + n1],
                                        start=True,
                                        stop=(skipm and jabs != jod_abs),
                                        deps=(dep0 if j == 0 else []),
                                        inc=(1 if last_mm else 0))
                        if is_a and "nomask" not in DBG_NO:
                            for j in range(nblk):
                                n0 = j * 512
                                n1 = min(m1 - m0, n0 + 512)
                                jabs = (m0 + n0) // 512
                                last_mm = (j == nblk - 1) and not has_od
                                t_pe = P.mm(PAg[:, n0:n1], STG2[:, 2176:2304],
                                            C_E[:, m0 + n0:m0 + n1], start=False,
                                            stop=(jabs != jod_abs),
                                            deps=([t_amt] if j == 0 and h2 == 0
                                                  else []),
                                            inc=(1 if last_mm else 0))
                        if has_od:
                            od0 = qt * 128 - m0
                            t_pe = P.mm(PAg[:, od0:od0 + 128], C_IDB[:],
                                        C_ODC[:] if (is_a and "nomask" not in DBG_NO)
                                        else C_TRI[:],
                                        start=False, stop=True, deps=[],
                                        inc=1)
                            if is_a:
                                st[("amt_used", kvh, qt)] = t_pe
                        pha_h = PHAT[:, phb * 2048 + m0:phb * 2048 + m1]
                        zcol = (21 + 3 * g) if is_a else (49 + g % 2)
                        if nhalf == 2 and h2 == 0:
                            zcol = 102 + g % 2
                        final_z = not (nhalf == 2 and h2 == 0)
                        if not final_z:
                            t_zwar = (st.get(("zpart", kvh, qt, g - 2)) or
                                      st.get(("zpart", kvh, qt - 1, 2 + g)))
                        elif is_a:
                            t_zwar = (st.get(("prologue", kvh)) if qt == 0
                                      else st.get(("rzdone", kvh, qt - 1, g)))
                        else:
                            t_zwar = (st.get(("rzdone", kvh, qt, g - 2)) or
                                      st.get(("rzdone", kvh, qt - 1, 2 + g)) or
                                      st.get(("prologue", kvh)))
                        t_efh = P.act(pha_h, PAg, AF.Exp,
                                      bias=SMALL[:, 100:101],
                                      accum_out=SMALL[:, zcol:zcol + 1],
                                      deps=[t_pe, t_bf,
                                            st.get(("phat_used", kvh, qt - 1)),
                                            t_zwar],
                                      inc=1)
                        st[("pafh", kvh, qt, g)] = t_efh
                    if nhalf == 2:
                        zf = (21 + 3 * g) if is_a else (49 + g % 2)
                        t_ef = P.v(lambda e, zf=zf, g=g: e.tensor_tensor(
                            out=SMALL[:, zf:zf + 1],
                            in0=SMALL[:, zf:zf + 1],
                            in1=SMALL[:, 102 + g % 2:103 + g % 2], op=OP.add),
                            deps=[t_efh])
                        st[("zpart", kvh, qt, g)] = t_ef
                    else:
                        t_ef = t_efh
                    st[("paf", kvh, qt, g)] = t_efh
                    pha = PHAT[:, phb * 2048:phb * 2048 + Nk]
                    if is_a:
                        h3 = (4 * kvh + g) * 3
                        rz0 = 60 + (g % 2) * 8
                        t_rz = P.v(lambda e, g=g, rz0=rz0: e.reciprocal(
                            out=SMALL[:, rz0:rz0 + 3],
                            in_=SMALL[:, 20 + 3 * g:23 + 3 * g]),
                            deps=[t_ec[g], t_ef, st[("esl", kvh, qt, g)]])
                        st[("rzdone", kvh, qt, g)] = t_rz
                        t_sc3 = P.v(lambda e, qt=qt, h3=h3, g=g, rz0=rz0:
                                    e.tensor_tensor(
                            out=SMALL[:, 56 + (g % 2) * 16:59 + (g % 2) * 16],
                            in0=SMALL[:, rz0:rz0 + 3],
                            in1=G_TM[:, qt * 96 + h3:qt * 96 + h3 + 3],
                            op=OP.mult), deps=[t_rz])
                        sc0 = 56 + (g % 2) * 16
                        t_pc = P.v(lambda e, g=g, sc0=sc0: e.tensor_scalar(
                            out=PHC[:, g * 132:g * 132 + 129],
                            in0=PHC[:, g * 132:g * 132 + 129],
                            scalar1=SMALL[:, sc0:sc0 + 1], scalar2=None,
                            op0=OP.mult), deps=[t_sc3])
                        st[("phc", kvh, qt, g)] = t_pc
                        t_pf = P.v(lambda e, pha=pha, sc0=sc0: e.tensor_scalar(
                            out=pha, in0=pha,
                            scalar1=SMALL[:, sc0 + 1:sc0 + 2], scalar2=None,
                            op0=OP.mult), deps=[t_sc3])
                        psb = g
                        c0 = max(0, qt - 1)
                        nsl = (qt + 1 - c0) * 128
                        t_psc = P.v(lambda e, psb=psb, nsl=nsl, sc0=sc0:
                                    e.tensor_scalar(
                            out=PHS[:, psb * 256:psb * 256 + nsl],
                            in0=PHS[:, psb * 256:psb * 256 + nsl],
                            scalar1=SMALL[:, sc0 + 2:sc0 + 3], scalar2=None,
                            op0=OP.mult), deps=[t_sc3])
                    else:
                        t_rz = P.v(lambda e, g=g: e.reciprocal(
                            out=SMALL[:, 52 + (g % 2):53 + (g % 2)],
                            in_=SMALL[:, 49 + g % 2:50 + g % 2]),
                            deps=[t_ef, st.get(("rzu", kvh, qt, g - 2)) or
                                  st.get(("rzu", kvh, qt - 1, 2 + g))])
                        st[("rzdone", kvh, qt, g)] = t_rz
                        t_pf = P.v(lambda e, pha=pha, g=g: e.tensor_scalar(
                            out=pha, in0=pha,
                            scalar1=SMALL[:, 52 + (g % 2):53 + (g % 2)],
                            scalar2=None, op0=OP.mult), deps=[t_rz])
                        st[("rzu", kvh, qt, g)] = t_pf
                        t_pc = t_psc = None
                    gtok[g] = (t_pf, t_pc, t_psc)

                def emit_avs_all():
                    scs = [gtok[g] for g in range(4)]

                    def av_group(srcs4, lhsT, rhs_small=False, extra=()):
                        bank = st.get(("tgb",), 0)
                        st[("tgb",)] = 1 - bank
                        pt = PT[:, bank * 1024:bank * 1024 + 1024]
                        t_p = None
                        for j, ap in enumerate(srcs4):
                            if rhs_small:
                                dst = pt[0:1, j * 128:(j + 1) * 128]
                            else:
                                dst = pt[:, j * 128:(j + 1) * 128]
                            t_p = P.tr(dst, ap, C_IDB[:],
                                       deps=([pts[bank]] + list(extra)
                                             if j == 0 else []),
                                       inc=(1 if j == 3 else 0))
                        if rhs_small:
                            t_e = P.v(lambda e, pt=pt: e.tensor_copy(
                                out=TSM[0:1, 0:512], in_=pt[0:1, 0:512]),
                                deps=[t_p, st.get(("tsm",))])
                            pts[bank] = t_e
                            t_av = av_mm(0, lhsT, TSM[0:1, 0:512],
                                         deps=[t_e], inc=1, full=True)
                            st[("tsm",)] = t_av
                        else:
                            t_e = P.v(lambda e, pt=pt, bank=bank: e.tensor_copy(
                                out=TS[bank][:, 0:512], in_=pt[:, 0:512]),
                                deps=[t_p, tsa[bank]])
                            pts[bank] = t_e
                            t_av = av_mm(0, lhsT, TS[bank][:, 0:512],
                                         deps=[t_e], inc=1, full=True)
                            tsa[bank] = t_av
                        return t_av

                    if is_a and "av_c" not in DBG_NO:
                        av_group([PHC[:, g * 132:g * 132 + 128] for g in range(4)],
                                 CVK[:, kvh, :], extra=[scs[g][1] for g in range(4)])
                        av_group([PHC[:, g * 132 + 128:g * 132 + 129]
                                  for g in range(4)],
                                 MEMV[0:1, kvh, :], rhs_small=True,
                                 extra=[scs[g][1] for g in range(4)])
                    t_last = None
                    if not (is_a and "av_f" in DBG_NO):
                        for c in range(Cn):
                            t_last = av_group(
                                [PHAT[:, g * 2048 + c * 128:g * 2048 + (c + 1) * 128]
                                 for g in range(4)],
                                VKM[:, (kvh * 16 + c) * 128:(kvh * 16 + c + 1) * 128],
                                extra=[scs[g][0] for g in range(4)] if c == 0 else ())
                        st[("phat_used", kvh, qt)] = t_last
                    if is_a and "av_s" not in DBG_NO:
                        c0 = max(0, qt - 1)
                        for jj, c in enumerate(range(c0, qt + 1)):
                            t_last = av_group(
                                [PHS[:, g * 256 + jj * 128:g * 256 + (jj + 1) * 128]
                                 for g in range(4)],
                                VKM[:, (kvh * 16 + c) * 128:(kvh * 16 + c + 1) * 128],
                                extra=[scs[g][2] for g in range(4)] if jj == 0 else ())
                        st[("phs_used", kvh, qt)] = t_last

                for g in range(4):
                    emit_scores(g)
                emit_avs_all()

                t_all_av = ("pe", P.count["pe"])
                abuf = qt % 2
                t_ev = P.v(lambda e, abuf=abuf: e.tensor_copy(
                    out=AOST[abuf], in_=PB[:, 0:512]),
                    deps=[t_all_av, st.get(("aodma", kvh, qt - 2)) or
                          st.get(("aodma", kvh - 1, NQT - 2 + qt))])
                t_st = P.dma("gpsimd",
                             ao_scr[qt, :, kvh * 4:(kvh + 1) * 4, :],
                             sb3(AOST[abuf], 4), deps=[t_ev])
                st[("aodma", kvh, qt)] = t_st
                st[("aoev", kvh, qt)] = t_ev
            st[("qfree", kvh)] = ("pe", P.count["pe"])
        return P.bar()

    def phase_oproj(w_kind, x_in, x_out):
        bb = P.bar()
        t_w = None
        AT = [STG, STG2]
        XT = [FWORK[:, 0:1024], FWORK[:, 1024:2048]]
        XO = [FWORK[:, 2048:3072], FWORK[:, 3072:4096]]
        t_at = [None, None]
        t_xt = [None, None]
        t_st = [None, None]
        t_evx = [None, None]
        for os_ in range(4):
            if w_kind == "woT":
                t_w = P.dma("sync", sb3(WSL, 32), pmaj(woT[os_]),
                            deps=bb + [t_w] + t_evx)
            else:
                t_w = P.dma("gpsimd", sb3(WSL, 32),
                            wo_b.rearrange("(hd p) o -> p hd o", p=128)[
                                :, :, os_ * 1024:(os_ + 1) * 1024],
                            deps=bb + [t_w] + t_evx)
            for qt in range(NQT):
                buf = qt % 2
                t_at[buf] = P.dma("sync", AT[buf][:, :],
                                  ao_scr[qt].rearrange("p hd t -> p (hd t)"),
                                  deps=bb + [t_at[buf], t_st[buf]])
                t_xt[buf] = P.dma("sync", XT[buf],
                                  x_in[qt * 128:(qt + 1) * 128,
                                       os_ * 1024:(os_ + 1) * 1024],
                                  deps=[t_xt[buf], t_evx[buf]])
                psl = PA[:, buf * 1024:buf * 1024 + 1024]
                t_pe = None
                for hd in range(32):
                    for j in range(2):
                        t_pe = P.mm(psl[:, j * 512:(j + 1) * 512],
                                    AT[buf][:, hd * 128:(hd + 1) * 128],
                                    WSL[:, hd * 1024 + j * 512:hd * 1024 + (j + 1) * 512],
                                    start=(hd == 0), stop=(hd == 31),
                                    deps=([t_at[buf], t_w, t_evx[buf]]
                                          if hd == 0 and j == 0 else []),
                                    inc=(1 if hd == 31 and j == 1 else 0))
                t_evx[buf] = P.v(lambda e, psl=psl, buf=buf: e.tensor_tensor(
                    out=XO[buf], in0=psl, in1=XT[buf], op=OP.add),
                    deps=[t_pe, t_xt[buf], t_st[buf]])
                t_st[buf] = P.dma("gpsimd",
                                  x_out[qt * 128:(qt + 1) * 128,
                                        os_ * 1024:(os_ + 1) * 1024],
                                  XO[buf], deps=[t_evx[buf]])
        return P.bar()

    def phase_ffn(x_in, x_out):
        bb = P.bar()
        t_w = None
        HT = [STG, STG2]
        XT = [FWORK[:, 0:1024], FWORK[:, 1024:2048]]
        XO = [FWORK[:, 2048:3072], FWORK[:, 3072:4096]]
        t_ht = [None, None]
        t_xt = [None, None]
        t_st = [None, None]
        t_evx = [None, None]
        for os_ in range(4):
            t_w = P.dma("sync", sb3(WSL, 32), pmaj(bwT[os_]),
                        deps=bb + [t_w] + t_evx)
            for qt in range(NQT):
                buf = qt % 2
                t_ht[buf] = P.dma("sync", HT[buf][:, :], h_scr[qt],
                                  deps=bb + [t_ht[buf], t_evx[buf]])
                t_xt[buf] = P.dma("sync", XT[buf],
                                  x_in[qt * 128:(qt + 1) * 128,
                                       os_ * 1024:(os_ + 1) * 1024],
                                  deps=[t_xt[buf], t_evx[buf]])
                psl = PA[:, buf * 1024:buf * 1024 + 1024]
                t_pe = None
                for ic in range(NIC):
                    for j in range(2):
                        t_pe = P.mm(psl[:, j * 512:(j + 1) * 512],
                                    HT[buf][:, ic * 128:(ic + 1) * 128],
                                    WSL[:, ic * 1024 + j * 512:ic * 1024 + (j + 1) * 512],
                                    start=(ic == 0), stop=(ic == NIC - 1),
                                    deps=([t_ht[buf], t_w, t_evx[buf]]
                                          if ic == 0 and j == 0 else []),
                                    inc=(1 if ic == NIC - 1 and j == 1 else 0))
                t_evx[buf] = P.v(lambda e, psl=psl, buf=buf: e.tensor_tensor(
                    out=XO[buf], in0=psl, in1=XT[buf], op=OP.add),
                    deps=[t_pe, t_xt[buf], t_st[buf]])
                t_st[buf] = P.dma("gpsimd",
                                  x_out[qt * 128:(qt + 1) * 128,
                                        os_ * 1024:(os_ + 1) * 1024],
                                  XO[buf], deps=[t_evx[buf]])
        return P.bar()

    with es:
        block = es.enter_context(nc.Block())
        for engname in ENGINES:
            def emit(engobj, engname=engname):
                P.reset_for(engname, engobj)
                program()
            getattr(block, engname)(emit)

    return nc, tap_outs


def make_in_maps(inputs):
    CONST = host_constants()
    x_full = np.ascontiguousarray(np.asarray(inputs["L0_attn_a_hidden_states"],
                                             dtype=np.float32))
    base = {
        "w_qkv": inputs["L0_attn_a_w_qkv"], "w_o": inputs["L0_attn_a_w_o"],
        "k_pos": inputs["L0_attn_a_k_pos"], "v_pos": inputs["L0_attn_a_v_pos"],
        "k_w1": inputs["L0_attn_a_k_w1"], "k_b1": inputs["L0_attn_a_k_b1"],
        "k_w2": inputs["L0_attn_a_k_w2"], "k_b2": inputs["L0_attn_a_k_b2"],
        "v_w1": inputs["L0_attn_a_v_w1"], "v_b1": inputs["L0_attn_a_v_b1"],
        "v_w2": inputs["L0_attn_a_v_w2"], "v_b2": inputs["L0_attn_a_v_b2"],
        "mem_kv": inputs["L0_attn_a_mem_kv"], "w_gate": inputs["L0_attn_a_w_gate"],
        "b_gate": inputs["L0_attn_a_b_gate"],
        "wq_b": inputs["L0_attn_b_wq"], "wk_b": inputs["L0_attn_b_wk"],
        "wv_b": inputs["L0_attn_b_wv"], "wo_b": inputs["L0_attn_b_wo"],
        "w_ffn": inputs["L0_ffn_weight"],
    }
    base = {k: np.ascontiguousarray(np.asarray(v, dtype=np.float32))
            for k, v in base.items()}
    for k, v in CONST.items():
        base["c_" + k] = np.ascontiguousarray(v)
    return [dict(base, x0=np.ascontiguousarray(x_full[i])) for i in range(B)]


def kernel(**inputs):
    nc, _ = build_program()
    in_maps = make_in_maps(inputs)
    res = run_bass_kernel_spmd(nc, in_maps, core_ids=list(range(B)))
    return np.stack([np.asarray(r["y"], dtype=np.float32)
                     for r in res.results], axis=0)
